# version 03
# baseline (speedup 1.0000x reference)
import math
import numpy as np
import contextlib

import concourse.bass as bass
import concourse.mybir as mybir
from concourse.bass_utils import run_bass_kernel_spmd

F32 = mybir.dt.float32
BF16 = mybir.dt.bfloat16
AF = mybir.ActivationFunctionType
ALU = mybir.AluOpType
AX = mybir.AxisListType

D = 1024
NCHUNK = 8
H = 16
E = 64
CS_DH = 256
GLA_C = 64
MOE0_H = 2816
MOE0_E = 8
SSM_N = 16
MOE1_H = 1024
MOE1_E = 8
SHARED_H = 2048
TBLK = 512


class Q:

    def __init__(self, name):
        self.name = name
        self.ops = []

    def wait(self, sem, val):
        if val > 0:
            self.ops.append(("wait", sem, val))

    def op(self, meth, *args, incs=(), **kw):
        self.ops.append(("op", meth, args, kw, tuple(incs)))

    def fold_nops(self):
        out = []
        for o in self.ops:
            if o[0] == "op" and o[1] == "nop" and o[4]:
                for k in range(len(out) - 1, -1, -1):
                    if out[k][0] == "op" and out[k][1] != "dma_start":
                        merged = {}
                        for s_, n_ in out[k][4] + o[4]:
                            merged[s_] = merged.get(s_, 0) + n_
                        assert len(merged) <= 1, f"multi-sem inc on PE op: {merged}"
                        out[k] = (out[k][0], out[k][1], out[k][2], out[k][3],
                                  tuple(merged.items()))
                        break
                else:
                    out.append(o)
                continue
            out.append(o)
        self.ops = out

    def replay(self, eng, sems):
        for o in self.ops:
            if o[0] == "wait":
                eng.wait_ge(sems[o[1]], o[2])
            else:
                _, meth, args, kw, incs = o
                r = getattr(eng, meth)(*args, **kw)
                for s, n in incs:
                    r = r.then_inc(sems[s], n)


class Ctx:

    def __init__(self, nc, T):
        self.nc = nc
        self.T = T
        self.t = Q("tensor")
        self.v = Q("vector")
        self.s = Q("scalar")
        self.g = Q("gpsimd")
        self.y = Q("sync")
        self.sem = {}
        self.semset = []
        self.dma_sems = set()

    def new_sem(self, name):
        assert name not in self.sem
        self.sem[name] = 0
        self.semset.append(name)
        return name

    def inc(self, name, n=1):
        self.sem[name] += n
        if n >= 16:
            self.dma_sems.add(name)
        return (name, n)

    def val(self, name):
        return self.sem[name]

    def queues(self):
        return [("tensor", self.t), ("vector", self.v), ("scalar", self.s),
                ("gpsimd", self.g), ("sync", self.y)]


class Arena:

    def __init__(self, tensor, n_elems, esize):
        self.t = tensor
        self.n = n_elems
        self.esize = esize
        self.off = 0
        self.marks = []

    def alloc(self, n_elems, tag=""):
        assert self.off + n_elems <= self.n, (
            f"arena overflow: {tag} wants {n_elems} at {self.off}/{self.n}")
        o = self.off
        self.off += n_elems
        return self.t[:, o:o + n_elems]

    def push(self):
        self.marks.append(self.off)

    def pop(self):
        self.off = self.marks.pop()


def fm_ap(dram3d, t0, tl, c0=0, cn=NCHUNK):
    return dram3d[c0:c0 + cn, :, t0:t0 + tl].rearrange("c p t -> p c t")


def silu_np(x):
    return x / (1 + np.exp(-x))


def host_constants():
    cst = {}
    cst["c_id32"] = np.eye(128, dtype=np.float32)
    cst["c_id16"] = np.eye(128, dtype=np.float32)
    cst["c_tri"] = np.triu(np.ones((GLA_C, GLA_C), np.float32))
    cst["c_triu16"] = np.triu(np.ones((GLA_C, GLA_C), np.float32))
    sel = np.zeros((H, NCHUNK * 128), np.float32)
    for j in range(NCHUNK):
        sel[2 * j, j * 128:j * 128 + 64] = 1.0
        sel[2 * j + 1, j * 128 + 64:j * 128 + 128] = 1.0
    cst["c_sel"] = sel
    cst["c_ones"] = np.ones((1, 128), np.float32)
    return cst


def build_program(T, stages, core_dbg=False):
    nc = bass.Bass("TRN2", target_bir_lowering=False)
    NT = T // 128
    NB = T // TBLK

    io = {}

    def inp(name, shape, dtype=F32):
        io[name] = nc.dram_tensor(name, list(shape), dtype, kind="ExternalInput").ap()
        return io[name]

    x_in = inp("x", (T, D))
    w = {}
    for n, shape in [
            ("q_w", (D, D)), ("k_w", (D, D)), ("v_w", (D, D)), ("o_w", (D, D)),
            ("cs_w1", (CS_DH, D)), ("cs_w2", (H, CS_DH)),
            ("moe0_w1", (MOE0_E, MOE0_H, D)), ("moe0_w2", (MOE0_E, D, MOE0_H)),
            ("moe0_w3", (MOE0_E, MOE0_H, D)), ("moe0_wg", (MOE0_E, D)),
            ("win", (D, D)), ("wg", (D, D)), ("wdt", (D, D)), ("wout", (D, D)),
            ("wb", (D, SSM_N)), ("wc", (D, SSM_N)),
            ("moe1_w1", (MOE1_E, MOE1_H, D)), ("moe1_w2", (MOE1_E, D, MOE1_H)),
            ("moe1_w3", (MOE1_E, MOE1_H, D)), ("moe1_wg", (MOE1_E, D)),
            ("sw1", (SHARED_H, D)), ("sw3", (SHARED_H, D)), ("sw2", (D, SHARED_H)),
    ]:
        w[n] = inp(n, shape)
    vecs = {}
    for n, shape in [
            ("cs_b1", (CS_DH,)), ("cs_b2", (H,)), ("decay", (H, E)),
            ("b_in", (D,)), ("bg", (D,)), ("bout", (D,)), ("bdt", (D,)),
            ("convw", (D, 3)), ("convb", (D,)), ("A", (D, SSM_N)),
            ("bb", (SSM_N,)), ("bc", (SSM_N,)), ("gate_bias", (MOE1_E,)),
    ]:
        vecs[n] = inp(n, shape)
    for n, shape in [("c_id32", (128, 128)), ("c_id16", (128, 128)),
                     ("c_tri", (GLA_C, GLA_C)), ("c_triu16", (GLA_C, GLA_C)),
                     ("c_sel", (H, NCHUNK * 128)), ("c_ones", (1, 128))]:
        inp(n, shape)

    y_out = nc.dram_tensor("y", [T, D], F32, kind="ExternalOutput").ap()

    def scr(name, shape, dtype):
        return nc.dram_tensor(name, list(shape), dtype, kind="Internal").ap()

    R0 = scr("R0", (NCHUNK, 128, T), F32)
    R1 = scr("R1", (NCHUNK, 128, T), F32)
    XLN = scr("XLN", (NCHUNK, 128, T), BF16)
    ZSC = scr("ZSC", (NCHUNK, 128, T), BF16)
    XSS = scr("XSS", (NCHUNK, 128, T), F32)
    XIN = scr("XIN", (NCHUNK, 128, T), F32)
    GSC = scr("GSC", (MOE0_E, T), F32)
    GSC1 = scr("GSC1", (MOE1_E, T), F32)
    BMR = scr("BMR", (2, SSM_N, T), F32)
    wscr = {}
    for n in ["q_w", "k_w", "v_w", "o_w", "cs_w1", "cs_w2", "moe0_w1",
              "moe0_w2", "moe0_w3", "moe1_w1", "moe1_w2", "moe1_w3",
              "sw1", "sw3", "sw2", "moe0_wg", "moe1_wg"]:
        wscr[n] = scr("S_" + n, io[n].shape, BF16)

    es = contextlib.ExitStack()
    prog = {}
    with es:
        WB_N = 70656
        AF_N = 13056
        WB = Arena(es.enter_context(nc.sbuf_tensor("WB", [128, WB_N], BF16)), WB_N, 2)
        AFA = Arena(es.enter_context(nc.sbuf_tensor("AFA", [128, AF_N], F32)), AF_N, 4)
        CONST = es.enter_context(nc.sbuf_tensor("CONST", [128, 2048], F32))
        CONST16 = es.enter_context(nc.sbuf_tensor("CONST16", [128, 512], BF16))
        ps = [es.enter_context(nc.psum_tensor(f"ps{i}", [128, 512], F32))
              for i in range(8)]

        ctx = Ctx(nc, T)
        for s in (["gld", "hld", "sst", "cst", "tt", "vv", "ss", "pp", "fin"]
                  + [f"sl{i}" for i in range(24)]):
            ctx.new_sem(s)
        sems = {}
        for s in ctx.semset:
            sems[s] = es.enter_context(nc.semaphore(s))

        st = {"io": io, "w": w, "vecs": vecs, "wscr": wscr, "R0": R0, "R1": R1,
              "XLN": XLN, "ZSC": ZSC, "XSS": XSS, "XIN": XIN, "BMR": BMR,
              "GSC": GSC, "GSC1": GSC1,
              "WB": WB, "AF": AFA, "CONST": CONST, "CONST16": CONST16,
              "ps": ps, "y": y_out, "x": x_in,
              "T": T, "NT": NT, "NB": NB}

        for stage_fn in stages:
            stage_fn(ctx, st)

        for name in ctx.semset:
            if name in ctx.dma_sems and ctx.val(name) > 0:
                ctx.y.wait(name, ctx.val(name))

        ctx.t.fold_nops()
        ctx.v.fold_nops()
        ctx.s.fold_nops()
        ctx.g.fold_nops()
        with nc.Block() as block:
            @block.tensor
            def _(eng):
                ctx.t.replay(eng, sems)

            @block.vector
            def _(eng):
                ctx.v.replay(eng, sems)

            @block.scalar
            def _(eng):
                ctx.s.replay(eng, sems)

            @block.gpsimd
            def _(eng):
                ctx.g.replay(eng, sems)

            @block.sync
            def _(eng):
                ctx.y.replay(eng, sems)

    return nc


def stage_consts(ctx, st):
    io, vecs = st["io"], st["vecs"]
    nc = ctx.nc
    g, v, s, t, y = ctx.g, ctx.v, ctx.s, ctx.t, ctx.y
    CONST, CONST16 = st["CONST"], st["CONST16"]

    st["cpos"] = dict(id32=0, logbase=128, decay_stage=136, cs_b1=144, b_in=152,
                      bg=160, bout=168, bdt=176, convb=184, convw=192, A=216,
                      cs_b2=344, bb=345, bc=346, gate_bias=347, tri=348,
                      sel=412, ones=1440, ones_col=1568, gbrow=1570, gbias_b=1580)
    cp = st["cpos"]

    g.op("dma_start", out=CONST[:, 0:128], in_=io["c_id32"], incs=[ctx.inc("gld", 16)])
    g.op("dma_start", out=CONST16[:, 0:128], in_=io["c_id16"], incs=[ctx.inc("gld", 16)])
    g.op("dma_start", out=CONST16[0:64, 128:192], in_=io["c_triu16"], incs=[ctx.inc("gld", 16)])
    g.op("memset", CONST16[:, 192:320], 0.0)
    g.op("dma_start", out=CONST[0:64, cp["tri"]:cp["tri"] + 64], in_=io["c_tri"],
         incs=[ctx.inc("gld", 16)])
    g.op("dma_start", out=CONST[0:16, cp["sel"]:cp["sel"] + NCHUNK * 128].rearrange("p (c n) -> p c n", c=NCHUNK),
         in_=io["c_sel"].rearrange("p (c n) -> p c n", c=NCHUNK), incs=[ctx.inc("gld", 16)])
    g.op("dma_start", out=CONST[0:1, cp["ones"]:cp["ones"] + 128], in_=io["c_ones"],
         incs=[ctx.inc("gld", 16)])
    g.op("memset", CONST[:, cp["ones_col"]:cp["ones_col"] + 1], 1.0)
    for name, pos in [("b_in", "b_in"), ("bg", "bg"), ("bout", "bout"),
                      ("bdt", "bdt"), ("convb", "convb")]:
        g.op("dma_start", out=CONST[:, cp[pos]:cp[pos] + 8],
             in_=vecs[name].rearrange("(c p) -> p c", p=128),
             allow_slow_non_contiguous=True, incs=[ctx.inc("gld", 16)])
    g.op("dma_start", out=CONST[:, cp["cs_b1"]:cp["cs_b1"] + 2],
         in_=vecs["cs_b1"].rearrange("(c p) -> p c", p=128),
         allow_slow_non_contiguous=True, incs=[ctx.inc("gld", 16)])
    g.op("dma_start", out=CONST[:, cp["convw"]:cp["convw"] + 24].rearrange("p (c k) -> p c k", c=NCHUNK),
         in_=vecs["convw"].rearrange("(c p) k -> p c k", p=128), incs=[ctx.inc("gld", 16)])
    g.op("dma_start", out=CONST[:, cp["A"]:cp["A"] + 128].rearrange("p (c k) -> p c k", c=NCHUNK),
         in_=vecs["A"].rearrange("(c p) k -> p c k", p=128), incs=[ctx.inc("gld", 16)])
    g.op("dma_start", out=CONST[0:16, cp["cs_b2"]:cp["cs_b2"] + 1], in_=vecs["cs_b2"][:, None],
         incs=[ctx.inc("gld", 16)])
    g.op("dma_start", out=CONST[0:16, cp["bb"]:cp["bb"] + 1], in_=vecs["bb"][:, None],
         incs=[ctx.inc("gld", 16)])
    g.op("dma_start", out=CONST[0:16, cp["bc"]:cp["bc"] + 1], in_=vecs["bc"][:, None],
         incs=[ctx.inc("gld", 16)])
    g.op("dma_start", out=CONST[0:8, cp["gate_bias"]:cp["gate_bias"] + 1],
         in_=vecs["gate_bias"][:, None], incs=[ctx.inc("gld", 16)])
    g.op("dma_start", out=CONST[0:1, cp["gbrow"]:cp["gbrow"] + 8],
         in_=vecs["gate_bias"][None, :], incs=[ctx.inc("gld", 16)])
    g.op("dma_start", out=CONST[:, cp["decay_stage"]:cp["decay_stage"] + 8],
         in_=vecs["decay"].rearrange("h e -> (h e)").rearrange("(c p) -> p c", p=128),
         allow_slow_non_contiguous=True, incs=[ctx.inc("gld", 16)])
    gld_all = ctx.val("gld")

    v.wait("gld", gld_all)
    dstage = CONST[:, cp["decay_stage"]:cp["decay_stage"] + 8]
    lb = CONST[:, cp["logbase"]:cp["logbase"] + 8]
    s.wait("gld", gld_all)
    s.op("activation", lb, dstage, AF.Abs)
    s.op("activation", lb, lb, AF.Exp, scale=-1.0)
    s.op("activation", lb, lb, AF.Ln, bias=1.0, incs=[ctx.inc("ss")])
    v.wait("ss", ctx.val("ss"))
    v.op("tensor_scalar", dstage, dstage, -1.0, 0.0, ALU.mult, ALU.max)
    v.op("drain")
    v.op("tensor_tensor", lb, lb, dstage, ALU.add)
    v.op("drain")
    v.op("tensor_scalar", lb, lb, -1.0, None, ALU.mult, incs=[ctx.inc("vv")])
    t.wait("gld", gld_all)
    t.op("matmul", st["ps"][0][:, 0:8], CONST[0:1, cp["ones"]:cp["ones"] + 128],
         CONST[0:1, cp["gbrow"]:cp["gbrow"] + 8], start=True, stop=True,
         incs=[ctx.inc("tt")])
    v.wait("tt", ctx.val("tt"))
    v.op("tensor_copy", CONST[:, cp["gbias_b"]:cp["gbias_b"] + 8], st["ps"][0][:, 0:8],
         incs=[ctx.inc("vv")])
    st["_gbias_slice"] = CONST[:, cp["gbias_b"]:cp["gbias_b"] + 8]
    st["consts_done"] = (ctx.val("vv"), gld_all)


def _repack_one(ctx, st, src_ap, dst_ap, R, C, stage_tiles):
    g, y = ctx.g, ctx.y
    a = max(1, 2048 // C)
    while R % (128 * a) != 0:
        a //= 2
    assert a >= 1 and R % (128 * a) == 0
    nslab = R // (128 * a)
    for i in range(nslab):
        k = st["_rp_i"] % 2
        st["_rp_i"] += 1
        lds, sts = f"sl{2 * k}", f"sl{2 * k + 1}"
        g.wait(sts, ctx.val(sts))
        g.op("dma_start",
             out=stage_tiles[k][:, 0:a * C],
             in_=src_ap[i * 128 * a:(i + 1) * 128 * a, :].rearrange("(p a) c -> p (a c)", a=a),
             max_dma_last_dim=4096, incs=[ctx.inc(lds, 16)])
        y.wait(lds, ctx.val(lds))
        y.op("dma_start",
             out=dst_ap[i * 128 * a:(i + 1) * 128 * a, :].rearrange("(p a) c -> p (a c)", a=a),
             in_=stage_tiles[k][:, 0:a * C], incs=[ctx.inc(sts, 16)])


def stage_repack(ctx, st):
    WB = st["WB"]
    WB.push()
    stage_tiles = [WB.alloc(4096, "rp0"), WB.alloc(4096, "rp1")]
    st["_rp_i"] = 0
    io, wscr = st["io"], st["wscr"]
    jobs = []
    for n in ["q_w", "k_w", "v_w", "o_w", "cs_w1", "sw1", "sw3", "sw2"]:
        shp = io[n].shape
        jobs.append((io[n], wscr[n], shp[0], shp[1]))
    for n in ["moe0_w1", "moe0_w2", "moe0_w3", "moe1_w1", "moe1_w2", "moe1_w3"]:
        shp = io[n].shape
        for e in range(shp[0]):
            jobs.append((io[n][e], wscr[n][e], shp[1], shp[2]))
    for s_, d_, R, C in jobs:
        _repack_one(ctx, st, s_, d_, R, C, stage_tiles)
    g, y = ctx.g, ctx.y
    for n in ["cs_w2", "moe0_wg", "moe1_wg"]:
        shp = io[n].shape
        k = st["_rp_i"] % 2
        st["_rp_i"] += 1
        lds, sts = f"sl{2 * k}", f"sl{2 * k + 1}"
        g.wait(sts, ctx.val(sts))
        g.op("dma_start", out=stage_tiles[k][0:shp[0], 0:shp[1]], in_=io[n],
             max_dma_last_dim=4096, incs=[ctx.inc(lds, 16)])
        y.wait(lds, ctx.val(lds))
        y.op("dma_start", out=wscr[n], in_=stage_tiles[k][0:shp[0], 0:shp[1]],
             incs=[ctx.inc(sts, 16)])
    st["repack_done"] = (ctx.val("sl1"), ctx.val("sl3"))
    WB.pop()


def stage_xt(ctx, st):
    nc, T, NT = ctx.nc, st["T"], st["NT"]
    g, t, v, y = ctx.g, ctx.t, ctx.v, ctx.y
    AFA, CONST, ps = st["AF"], st["CONST"], st["ps"]
    cp = st["cpos"]
    id32 = CONST[:, 0:128]
    AFA.push()
    xin = [AFA.alloc(1024, "xt_in0"), AFA.alloc(1024, "xt_in1")]
    xfm = [AFA.alloc(1024, "xt_fm0"), AFA.alloc(1024, "xt_fm1")]
    ev_t = {0: 0, 1: 0}
    stg_t = {0: 0, 1: 0}
    cw = st["consts_done"]
    t.wait("gld", cw[1])
    for i in range(NT):
        b = i % 2
        lds, sts = f"sl{4 + b}", f"sl{6 + b}"
        if i >= 2:
            g.wait("tt", stg_t[b])
        g.op("dma_start", out=xin[b], in_=st["x"][i * 128:(i + 1) * 128, :],
             incs=[ctx.inc(lds, 16)])
        t.wait(lds, ctx.val(lds))
        if i >= 2:
            t.wait("vv", ev_t[b])
        for c in range(8):
            bank = ps[c // 4 + 2 * b]
            t.op("matmul", bank[:, (c % 4) * 128:(c % 4) * 128 + 128],
                 xin[b].rearrange("p (c n) -> p c n", c=8)[:, c, :], id32,
                 start=True, stop=True)
        t.op("nop", incs=[ctx.inc("tt")])
        stg_t[b] = ctx.val("tt")
        v.wait("tt", stg_t[b])
        if i >= 2:
            v.wait(sts, ctx.val(sts))
        v.op("tensor_copy", xfm[b][:, 0:512], ps[2 * b][:, :])
        v.op("tensor_copy", xfm[b][:, 512:1024], ps[2 * b + 1][:, :], incs=[ctx.inc("vv")])
        ev_t[b] = ctx.val("vv")
        y.wait("vv", ev_t[b])
        y.op("dma_start", out=fm_ap(st["R0"], i * 128, 128),
             in_=xfm[b].rearrange("p (c t) -> p c t", c=8), incs=[ctx.inc(sts, 16)])
    AFA.pop()
    st["xt_done"] = (ctx.val("sl6"), ctx.val("sl7"))


def emit_ln(ctx, st, xf, xln16, bS, bQ, bR, bM, sq):
    t, v, s = ctx.t, ctx.v, ctx.s
    CONST, cp = st["CONST"], st["cpos"]
    ps = st["ps"]
    ones_col = CONST[:, cp["ones_col"]:cp["ones_col"] + 1]
    ones_row = CONST[0:1, cp["ones"]:cp["ones"] + 128]
    m_, q_ = st["_ln_rows"][0], st["_ln_rows"][1]
    r_ = sq[0][0:1, 0:512]
    for c in range(8):
        t.op("matmul", ps[bS][0:1, :], ones_col, xf[c],
             start=(c == 0), stop=(c == 7), incs=([ctx.inc("tt")] if c == 7 else []))
    sqmm = {}
    for c in range(8):
        if c >= 2:
            s.wait("tt", sqmm[c - 2])
        s.op("activation", sq[c % 2], xf[c], AF.Square, incs=[ctx.inc("ss")])
        t.wait("ss", ctx.val("ss"))
        t.op("matmul", ps[bQ][0:1, :], ones_col, sq[c % 2],
             start=(c == 0), stop=(c == 7), incs=[ctx.inc("tt")])
        sqmm[c] = ctx.val("tt")
    tstat = ctx.val("tt")
    v.wait("tt", tstat)
    v.op("tensor_scalar", m_, ps[bS][0:1, :], 1.0 / D, None, ALU.mult)
    v.op("tensor_scalar", q_, ps[bQ][0:1, :], 1.0 / D, None, ALU.mult)
    v.op("drain")
    v.op("tensor_tensor", r_, m_, m_, ALU.mult)
    v.op("drain")
    v.op("tensor_tensor", q_, q_, r_, ALU.subtract)
    v.op("drain")
    v.op("tensor_scalar", q_, q_, 1e-5, None, ALU.add, incs=[ctx.inc("vv")])
    s.wait("vv", ctx.val("vv"))
    s.op("activation", q_, q_, AF.Ln)
    s.op("drain")
    s.op("activation", q_, q_, AF.Exp, scale=-0.5, incs=[ctx.inc("ss")])
    v.wait("ss", ctx.val("ss"))
    v.op("tensor_tensor", m_, m_, q_, ALU.mult, incs=[ctx.inc("vv")])
    t.wait("vv", ctx.val("vv"))
    t.op("matmul", ps[bR][:, :], ones_row, q_, start=True, stop=True)
    t.op("matmul", ps[bM][:, :], ones_row, m_, start=True, stop=True,
         incs=[ctx.inc("tt")])
    v.wait("tt", ctx.val("tt"))
    for c in range(8):
        v.op("tensor_tensor", sq[c % 2], xf[c], ps[bR][:, :], ALU.mult)
        v.op("drain")
        v.op("tensor_tensor", xln16[c], sq[c % 2], ps[bM][:, :], ALU.subtract,
             incs=([ctx.inc("vv")] if c == 7 else []))
    return ctx.val("vv")


L0_PHASES = 99
L0_SUB = 99
L0_KHT = 2


def stage_l0_attn(ctx, st):
    T, NB = st["T"], st["NB"]
    t, v, s, g, y = ctx.t, ctx.v, ctx.s, ctx.g, ctx.y
    WB, AFA, CONST, CONST16, ps = st["WB"], st["AF"], st["CONST"], st["CONST16"], st["ps"]
    cp = st["cpos"]
    wscr = st["wscr"]
    id32 = CONST[:, 0:128]
    id16 = CONST16[:, 0:128]
    triu16 = CONST16[0:64, 128:192]

    WB.push(); AFA.push()
    qkvoT = {n: WB.alloc(8192, n + "T") for n in ["q_w", "k_w", "v_w", "o_w"]}
    csw1T = WB.alloc(2048, "cs_w1T")
    csw2T = WB.alloc(32, "cs_w2T")
    y.wait("sl1", st["repack_done"][0])
    y.wait("sl3", st["repack_done"][1])
    for n, dst in qkvoT.items():
        for c in range(8):
            y.op("dma_start", out=dst[:, c * 1024:(c + 1) * 1024],
                 in_=wscr[n][:, c * 128:(c + 1) * 128], transpose=True,
                 incs=[ctx.inc("hld", 16)])
    for c in range(8):
        y.op("dma_start", out=csw1T[:, c * 256:(c + 1) * 256],
             in_=wscr["cs_w1"][:, c * 128:(c + 1) * 128], transpose=True,
             incs=[ctx.inc("hld", 16)])
    for c in range(2):
        y.op("dma_start", out=csw2T[:, c * 16:(c + 1) * 16],
             in_=wscr["cs_w2"][:, c * 128:(c + 1) * 128], transpose=True,
             incs=[ctx.inc("hld", 16)])
    wready = ctx.val("hld")

    xfb = AFA.alloc(4096, "l0x")
    xf = [xfb[:, c * 512:(c + 1) * 512] for c in range(8)]
    sq = [AFA.alloc(512, "l0sq0"), AFA.alloc(512, "l0sq1")]
    r3 = AFA.alloc(1024, "l0r3")
    st["_ln_rows"] = [r3[0:1, 0:512], r3[0:1, 512:1024]]
    lam = AFA.alloc(4096, "l0lam")
    ltm = AFA.alloc(1024, "l0ltm")
    negd = AFA.alloc(8, "l0negd")
    posd = AFA.alloc(8, "l0posd")
    sd = AFA.alloc(64, "l0sd")
    lend = AFA.alloc(64, "l0lend")
    Sst = AFA.alloc(512, "l0S")
    kvt = AFA.alloc(1024, "l0kvt")
    lcb = sq[1]
    xln16 = [WB.alloc(512, f"l0xln{c}") for c in range(8)]
    qs = [WB.alloc(512, f"l0q{c}") for c in range(8)]
    ks = [WB.alloc(512, f"l0k{c}") for c in range(8)]
    kh = [WB.alloc(512, f"l0kh{c}") for c in range(8)]
    lam16 = [WB.alloc(512, f"l0g{c}") for c in range(8)]
    vtb = [WB.alloc(1024, "l0vt0"), WB.alloc(1024, "l0vt1")]
    aofmb = WB.alloc(4096, "l0ao")
    aofm = [aofmb[:, c * 512:(c + 1) * 512] for c in range(8)]
    cs116 = aofmb[:, 0:1024]
    S16 = WB.alloc(512, "l0S16")
    aoc = WB.alloc(512, "l0aoc")
    aoc2 = WB.alloc(512, "l0aoc2")
    khT = WB.alloc(1024, "l0khT")
    pb = WB.alloc(1024, "l0pb")
    ao = WB.alloc(1024, "l0aoh")

    g.op("memset", Sst[:, :], 0.0, incs=[ctx.inc("pp")])
    v.wait("pp", ctx.val("pp"))
    t.wait("hld", wready)
    t.wait("vv", st["consts_done"][0])
    t.wait("gld", st["consts_done"][1])
    s.wait("gld", st["consts_done"][1])
    xt0, xt1 = st["xt_done"]
    g.wait("sl6", xt0)
    g.wait("sl7", xt1)

    st["_l0aev"] = {0: 0, 1: 0}
    st["_l0vfree"] = {0: 0, 1: 0}
    st["_l0khv"] = 0
    xfree = 0
    for b in range(NB):
        t0 = b * TBLK
        lds = f"sl{14 + (b % 2)}"
        if b >= 1:
            g.wait("vv", xfree)
        g.op("dma_start", out=xfb.rearrange("p (c t) -> p c t", c=8),
             in_=fm_ap(st["R0"], t0, TBLK), incs=[ctx.inc(lds, 16)])
        for q_ in (t, v, s):
            q_.wait(lds, ctx.val(lds))
        for q_ in (v, s):
            q_.wait("sl16", ctx.val("sl16"))
            q_.wait("sl17", ctx.val("sl17"))
        lnv = emit_ln(ctx, st, xf, xln16, 0, 1, 2, 3, sq)
        t.wait("vv", lnv)
        for j in range(2):
            for c in range(8):
                t.op("matmul", ps[4][:, :],
                     csw1T[:, c * 256 + j * 128:c * 256 + j * 128 + 128],
                     xln16[c], start=(c == 0), stop=(c == 7),
                     incs=([ctx.inc("tt")] if c == 7 else []))
            s.wait("tt", ctx.val("tt"))
            s.op("activation", cs116[:, j * 512:(j + 1) * 512], ps[4][:, :], AF.Tanh,
                 bias=CONST[:, cp["cs_b1"] + j:cp["cs_b1"] + j + 1], incs=[ctx.inc("ss")])
            t.wait("ss", ctx.val("ss"))
        for j in range(2):
            t.op("matmul", ps[5][0:16, :], csw2T[:, j * 16:(j + 1) * 16],
                 cs116[:, j * 512:(j + 1) * 512], start=(j == 0), stop=(j == 1),
                 incs=([ctx.inc("tt")] if j == 1 else []))
        s.wait("tt", ctx.val("tt"))
        s.op("activation", lcb[0:16, :], ps[5][0:16, :], AF.Sigmoid,
             bias=CONST[0:16, cp["cs_b2"]:cp["cs_b2"] + 1])
        s.op("drain")
        s.op("activation", lcb[0:16, :], lcb[0:16, :], AF.Ln, scale=0.2, bias=1.0,
             incs=[ctx.inc("ss")])
        t.wait("ss", ctx.val("ss"))
        lamv = {}
        for j in range(8):
            bk = 4 + (j % 2)
            if j >= 2:
                t.wait("vv", lamv[j % 2])
            t.op("matmul", ps[bk][:, :],
                 CONST[0:16, cp["sel"] + j * 128:cp["sel"] + (j + 1) * 128],
                 lcb[0:16, :], start=True, stop=True, incs=[ctx.inc("tt")])
            v.wait("tt", ctx.val("tt"))
            v.op("tensor_scalar", lam[:, j * 512:(j + 1) * 512], ps[bk][:, :],
                 CONST[:, cp["logbase"] + j:cp["logbase"] + j + 1],
                 math.log(0.9995), ALU.add, ALU.min, incs=[ctx.inc("vv")])
            lamv[j % 2] = ctx.val("vv")
        t.wait("vv", ctx.val("vv"))
        if L0_PHASES < 1:
            continue
        for gch in range(8):
            for j in range(8):
                t.op("matmul", ps[4 + j // 4][0:64, (j % 4) * 128:(j % 4) * 128 + 128],
                     lam[:, j * 512 + gch * 64:j * 512 + gch * 64 + 64], id32,
                     start=True, stop=True)
            t.op("nop", incs=[ctx.inc("tt")])
            v.wait("tt", ctx.val("tt"))
            v.op("tensor_copy", ltm[0:64, 0:512], ps[4][0:64, :])
            v.op("tensor_copy", ltm[0:64, 512:1024], ps[5][0:64, :], incs=[ctx.inc("vv")])
            t.wait("vv", ctx.val("vv"))
            for half in range(2):
                t.op("matmul", ps[6 + half][0:64, 0:512],
                     CONST[0:64, cp["tri"]:cp["tri"] + 64],
                     ltm[0:64, half * 512:(half + 1) * 512], start=True, stop=True,
                     incs=([ctx.inc("tt")] if half == 1 else []))
            v.wait("tt", ctx.val("tt"))
            v.op("tensor_copy", ltm[0:64, 0:512], ps[6][0:64, 0:512])
            v.op("tensor_copy", ltm[0:64, 512:1024], ps[7][0:64, 0:512],
                 incs=[ctx.inc("vv")])
            t.wait("vv", ctx.val("vv"))
            for j in range(8):
                t.op("matmul", ps[4 + j // 4][:, (j % 4) * 64:(j % 4) * 64 + 64],
                     ltm[0:64, j * 128:(j + 1) * 128], id32[0:64, 0:64],
                     start=True, stop=True)
            t.op("nop", incs=[ctx.inc("tt")])
            s.wait("tt", ctx.val("tt"))
            for j in range(8):
                Lp = ps[4 + j // 4][:, (j % 4) * 64:(j % 4) * 64 + 64]
                Lcol = ps[4 + j // 4][:, (j % 4) * 64 + 63:(j % 4) * 64 + 64]
                dcol = gch * 8 + j
                s.op("activation", negd[:, j:j + 1], Lcol, AF.Copy, scale=-0.5, bias=0.0)
                s.op("activation", posd[:, j:j + 1], Lcol, AF.Copy, scale=0.5, bias=0.0)
                s.op("activation", sd[:, dcol:dcol + 1], Lcol, AF.Exp, scale=0.5)
                s.op("activation", lend[:, dcol:dcol + 1], Lcol, AF.Exp)
                s.op("drain")
                s.op("activation", lam16[j][:, gch * 64:gch * 64 + 64], Lp, AF.Exp,
                     bias=negd[:, j:j + 1])
                s.op("activation", kh[j][:, gch * 64:gch * 64 + 64], Lp, AF.Exp,
                     scale=-1.0, bias=posd[:, j:j + 1])
            s.op("nop", incs=[ctx.inc("ss")])
            t.wait("ss", ctx.val("ss"))
        if L0_PHASES < 2:
            continue
        for j in range(8):
            for c in range(8):
                t.op("matmul", ps[4][:, :],
                     qkvoT["q_w"][:, c * 1024 + j * 128:c * 1024 + j * 128 + 128],
                     xln16[c], start=(c == 0), stop=(c == 7),
                     incs=([ctx.inc("tt")] if c == 7 else []))
            v.wait("tt", ctx.val("tt"))
            v.op("tensor_tensor", qs[j], ps[4][:, :], lam16[j], ALU.mult,
                 incs=[ctx.inc("vv")])
            qdone = ctx.val("vv")
            for c in range(8):
                t.op("matmul", ps[5][:, :],
                     qkvoT["k_w"][:, c * 1024 + j * 128:c * 1024 + j * 128 + 128],
                     xln16[c], start=(c == 0), stop=(c == 7),
                     incs=([ctx.inc("tt")] if c == 7 else []))
            t.wait("vv", qdone)
            v.wait("tt", ctx.val("tt"))
            v.op("tensor_tensor", ks[j], ps[5][:, :], kh[j], ALU.mult)
            v.op("drain")
            for gch in range(8):
                dcol = gch * 8 + j
                v.op("tensor_scalar", kh[j][:, gch * 64:gch * 64 + 64],
                     ks[j][:, gch * 64:gch * 64 + 64], sd[:, dcol:dcol + 1],
                     None, ALU.mult)
            v.op("nop", incs=[ctx.inc("vv")])
            t.wait("vv", ctx.val("vv"))
        projv = ctx.val("vv")
        if L0_PHASES < 3:
            continue
        t.wait("vv", projv)
        for gch in range(8):
            cc = slice(gch * 64, gch * 64 + 64)
            vt = vtb[gch % 2]
            t.wait("vv", st["_l0khv"])
            for oh in range(2):
                for c in range(8):
                    t.op("matmul", ps[6 + oh][0:64, :],
                         xln16[c][:, cc],
                         qkvoT["v_w"][:, c * 1024 + oh * 512:c * 1024 + oh * 512 + 512],
                         start=(c == 0), stop=(c == 7),
                         incs=([ctx.inc("tt")] if c == 7 else []))
            if gch >= 2:
                v.wait("tt", st["_l0vfree"][gch % 2])
            v.wait("tt", ctx.val("tt"))
            v.op("tensor_copy", vt[0:64, 0:512], ps[6][0:64, :])
            v.op("tensor_copy", vt[0:64, 512:1024], ps[7][0:64, :], incs=[ctx.inc("vv")])
            t.wait("vv", ctx.val("vv"))
            if L0_SUB < 1:
                continue
            for j in range(8):
                dcol = gch * 8 + j
                v.op("tensor_scalar", S16[:, j * 64:(j + 1) * 64],
                     Sst[:, j * 64:(j + 1) * 64], sd[:, dcol:dcol + 1], None, ALU.mult,
                     incs=([ctx.inc("vv")] if j == 7 else []))
            s16v = ctx.val("vv")
            t.wait("vv", s16v)
            if L0_KHT >= 1:
                for j in range(8 if L0_KHT != 3 else 1):
                    t.op("matmul", ps[6 + j // 4][0:64, (j % 4) * 128:(j % 4) * 128 + 128],
                         kh[j][:, cc], id16, start=True, stop=True)
                t.op("nop", incs=[ctx.inc("tt")])
            if L0_KHT >= 2:
                v.wait("tt", ctx.val("tt"))
                v.op("tensor_copy", khT[0:64, 0:512], ps[6][0:64, :])
                v.op("tensor_copy", khT[0:64, 512:1024], ps[7][0:64, :],
                     incs=[ctx.inc("vv")])
            khv = ctx.val("vv")
            st["_l0khv"] = khv
            if L0_SUB < 2:
                continue
            horder = [h for h in range(16) if h % 2 == 0] + \
                     [h for h in range(16) if h % 2 == 1]
            for i_, h in enumerate(horder):
                j, hh = h // 2, h % 2
                rr = slice(hh * 64, hh * 64 + 64)
                t.op("matmul", ps[h // 8][0:64, (h % 8) * 64:(h % 8) * 64 + 64],
                     ks[j][rr, cc], qs[j][rr, cc], start=True, stop=True,
                     incs=([ctx.inc("tt")] if i_ == 15 else []))
            v.wait("tt", ctx.val("tt"))
            for h in range(16):
                v.op("tensor_tensor", pb[0:64, h * 64:(h + 1) * 64],
                     ps[h // 8][0:64, (h % 8) * 64:(h % 8) * 64 + 64], triu16,
                     ALU.mult, incs=([ctx.inc("vv")] if h == 15 else []))
            pbv = ctx.val("vv")
            if L0_SUB < 3:
                continue
            t.wait("vv", pbv)
            for h in range(16):
                t.op("matmul", ps[2 + h // 8][0:64, (h % 8) * 64:(h % 8) * 64 + 64],
                     pb[0:64, h * 64:(h + 1) * 64],
                     vt[0:64, h * 64:h * 64 + 64], start=True, stop=True)
            for i_, h in enumerate(horder):
                j, hh = h // 2, h % 2
                rr = slice(hh * 64, hh * 64 + 64)
                t.op("matmul", ps[6 + h // 8][0:64, (h % 8) * 64:(h % 8) * 64 + 64],
                     qs[j][rr, cc], S16[rr, j * 64:(j + 1) * 64], start=True, stop=True,
                     incs=([ctx.inc("tt")] if i_ == 15 else []))
            s.wait("tt", ctx.val("tt"))
            s.op("activation", aoc[0:64, :], ps[2][0:64, :], AF.Copy, bias=0.0)
            s.op("activation", aoc[0:64, 512:1024] if False else aoc2[0:64, :],
                 ps[3][0:64, :], AF.Copy, bias=0.0, incs=[ctx.inc("ss")])
            v.wait("ss", ctx.val("ss"))
            v.op("tensor_tensor", ao[0:64, 0:512], aoc[0:64, :], ps[6][0:64, :], ALU.add)
            v.op("tensor_tensor", ao[0:64, 512:1024], aoc2[0:64, :], ps[7][0:64, :],
                 ALU.add, incs=[ctx.inc("vv")])
            st["_l0aev"][0] = ctx.val("vv")
            if L0_SUB < 4:
                continue
            t.wait("vv", khv)
            for h in range(16):
                j, hh = h // 2, h % 2
                t.op("matmul", ps[4 + h // 8][0:64, (h % 8) * 64:(h % 8) * 64 + 64],
                     vt[0:64, h * 64:h * 64 + 64],
                     khT[0:64, j * 128 + hh * 64:j * 128 + hh * 64 + 64],
                     start=True, stop=True,
                     incs=([ctx.inc("tt")] if h == 15 else []))
            v.wait("tt", ctx.val("tt"))
            v.op("tensor_copy", kvt[0:64, 0:512], ps[4][0:64, 0:512])
            v.op("tensor_copy", kvt[0:64, 512:1024], ps[5][0:64, 0:512],
                 incs=[ctx.inc("vv")])
            t.wait("vv", ctx.val("vv"))
            if L0_SUB < 5:
                continue
            for j in range(8):
                t.op("matmul", ps[6][:, j * 64:(j + 1) * 64],
                     kvt[0:64, j * 128:(j + 1) * 128], id32[0:64, 0:64],
                     start=True, stop=True)
                t.op("matmul", ps[7][:, j * 64:(j + 1) * 64],
                     ao[0:64, j * 128:(j + 1) * 128], id16[0:64, 0:64],
                     start=True, stop=True)
            t.op("nop", incs=[ctx.inc("tt")])
            v.wait("tt", ctx.val("tt"))
            for j in range(8):
                dcol = gch * 8 + j
                v.op("scalar_tensor_tensor", Sst[:, j * 64:(j + 1) * 64],
                     Sst[:, j * 64:(j + 1) * 64], lend[:, dcol:dcol + 1],
                     ps[6][:, j * 64:(j + 1) * 64], ALU.mult, ALU.add)
                v.op("tensor_copy", aofm[j][:, cc], ps[7][:, j * 64:(j + 1) * 64])
            v.op("nop", incs=[ctx.inc("vv")])
            st["_l0vfree"][gch % 2] = ctx.val("tt")
            t.wait("vv", ctx.val("vv"))
        if L0_PHASES < 4:
            continue
        for oo in range(8):
            bk = oo % 2
            for c in range(8):
                t.op("matmul", ps[bk][:, :],
                     qkvoT["o_w"][:, c * 1024 + oo * 128:c * 1024 + oo * 128 + 128],
                     aofm[c], start=(c == 0), stop=(c == 7),
                     incs=([ctx.inc("tt")] if c == 7 else []))
            v.wait("tt", ctx.val("tt"))
            if oo >= 2:
                v.wait(f"sl{16 + (oo % 2)}", ctx.val(f"sl{16 + (oo % 2)}"))
            v.op("tensor_tensor", sq[oo % 2], xf[oo], ps[bk][:, :], ALU.add,
                 incs=[ctx.inc("vv")])
            evv = ctx.val("vv")
            t.wait("vv", evv)
            y.wait("vv", evv)
            y.op("dma_start", out=fm_ap(st["R1"], t0, TBLK, oo, 1),
                 in_=sq[oo % 2].rearrange("p (c t) -> p c t", c=1),
                 incs=[ctx.inc(f"sl{16 + (oo % 2)}", 16)])
        v.op("nop", incs=[ctx.inc("vv")])
        xfree = ctx.val("vv")
    st["l0_done"] = (ctx.val("sl16"), ctx.val("sl17"))
    WB.pop(); AFA.pop()


def emit_gates(ctx, st, xln16, gscr, wgT, t0, renorm, use_bias=False):
    t, v, s, g, y = ctx.t, ctx.v, ctx.s, ctx.g, ctx.y
    ps = st["ps"]
    gt = st["_gt"]
    grow = st["_grow"]
    gs512 = st["_gs512"]
    CONST, cp = st["CONST"], st["cpos"]
    id32 = CONST[:, 0:128]
    for c in range(8):
        t.op("matmul", ps[4][0:8, :], wgT[:, c * 8:(c + 1) * 8], xln16[c],
             start=(c == 0), stop=(c == 7), incs=([ctx.inc("tt")] if c == 7 else []))
    logv = ctx.val("tt")
    v.wait("tt", logv)
    v.op("tensor_copy", gs512[0:8, :], ps[4][0:8, :], incs=[ctx.inc("vv")])
    t.wait("vv", ctx.val("vv"))
    for ti in range(4):
        t.op("matmul", ps[5][:, ti * 8:(ti + 1) * 8],
             gs512[0:8, ti * 128:(ti + 1) * 128], id32[0:8, 0:8],
             start=True, stop=True)
    t.op("nop", incs=[ctx.inc("tt")])
    v.wait("tt", ctx.val("tt"))
    lg = gt[:, 0:32]
    v.op("tensor_copy", lg, ps[5][:, 0:32], incs=[ctx.inc("vv")])
    s.wait("vv", ctx.val("vv"))
    m1 = grow[:, 0:1]; sm = grow[:, 1:2]; m2 = grow[:, 2:3]; ws = grow[:, 3:4]
    sc = gt[:, 32:64]
    for ti in range(4):
        sl = lg[:, ti * 8:(ti + 1) * 8]
        so = sc[:, ti * 8:(ti + 1) * 8]
        v.op("tensor_reduce", m1, sl, AX.X, ALU.max)
        v.op("drain")
        v.op("tensor_scalar", so, sl, m1, None, ALU.subtract, incs=[ctx.inc("vv")])
        s.wait("vv", ctx.val("vv"))
        s.op("activation", so, so, AF.Exp, incs=[ctx.inc("ss")])
        v.wait("ss", ctx.val("ss"))
        v.op("tensor_reduce", sm, so, AX.X, ALU.add)
        v.op("drain")
        v.op("reciprocal", sm, sm)
        v.op("drain")
        v.op("tensor_scalar", so, so, sm, None, ALU.mult)
        v.op("drain")
        if use_bias:
            sb = lg[:, ti * 8:(ti + 1) * 8]
            v.op("tensor_tensor", sb, so, st["_gbias_slice"], ALU.add)
            v.op("drain")
            base = sb
        else:
            base = so
        v.op("tensor_reduce", m1, base, AX.X, ALU.max)
        v.op("drain")
        v.op("tensor_scalar", grow[:, 4:12], base, m1, None, ALU.is_lt)
        v.op("drain")
        v.op("tensor_tensor", grow[:, 4:12], base, grow[:, 4:12], ALU.mult)
        v.op("drain")
        v.op("tensor_reduce", m2, grow[:, 4:12], AX.X, ALU.max)
        v.op("drain")
        v.op("tensor_scalar", grow[:, 4:12], base, m2, None, ALU.is_ge)
        v.op("drain")
        v.op("tensor_tensor", so, so, grow[:, 4:12], ALU.mult)
        v.op("drain")
        if renorm:
            v.op("tensor_reduce", ws, so, AX.X, ALU.add)
            v.op("drain")
            v.op("tensor_scalar", ws, ws, 1e-20, None, ALU.add)
            v.op("drain")
            v.op("reciprocal", ws, ws)
            v.op("drain")
            v.op("tensor_scalar", so, so, ws, None, ALU.mult)
            v.op("drain")
    v.op("nop", incs=[ctx.inc("vv")])
    t.wait("vv", ctx.val("vv"))
    for ti in range(4):
        t.op("matmul", ps[6][0:8, ti * 128:(ti + 1) * 128],
             sc[:, ti * 8:(ti + 1) * 8], id32, start=True, stop=True)
    t.op("nop", incs=[ctx.inc("tt")])
    v.wait("tt", ctx.val("tt"))
    v.op("tensor_copy", gs512[0:8, :], ps[6][0:8, :], incs=[ctx.inc("vv")])
    y.wait("vv", ctx.val("vv"))
    y.op("dma_start", out=gscr[:, t0:t0 + 512], in_=gs512[0:8, :],
         incs=[ctx.inc("cst", 16)])


def stage_l0_moe(ctx, st):
    T, NB = st["T"], st["NB"]
    t, v, s, g, y = ctx.t, ctx.v, ctx.s, ctx.g, ctx.y
    WB, AFA, CONST, CONST16, ps = st["WB"], st["AF"], st["CONST"], st["CONST16"], st["ps"]
    cp = st["cpos"]
    wscr = st["wscr"]
    id16 = CONST16[:, 0:128]

    WB.push(); AFA.push()
    xf = [AFA.alloc(512, f"m0x{c}") for c in range(8)]
    sq = [AFA.alloc(512, "m0sq0"), AFA.alloc(512, "m0sq1")]
    r3 = AFA.alloc(1024, "m0r3")
    st["_ln_rows"] = [r3[0:1, 0:512], r3[0:1, 512:1024]]
    gt = AFA.alloc(64, "m0gt")
    grow = AFA.alloc(16, "m0grow")
    st["_gt"], st["_grow"] = gt, grow
    cwb = AFA.alloc(512, "m0cwb")
    st["_gs512"] = cwb
    contrib = [AFA.alloc(512, "m0ct0"), AFA.alloc(512, "m0ct1")]
    cwrow = AFA.alloc(512, "m0cwrow")

    xln16 = [WB.alloc(512, f"m0xln{c}") for c in range(8)]
    wgT = WB.alloc(64, "m0wgT")
    wgn = WB.alloc(1024, "m0wgn")
    w1T = WB.alloc(11264, "m0w1T")
    w3T = WB.alloc(11264, "m0w3T")
    w2T = WB.alloc(11264, "m0w2T")
    hact = WB.alloc(5632, "m0hact")
    hs16 = [WB.alloc(512, "m0hs0"), WB.alloc(512, "m0hs1")]

    HHALF = MOE0_H // 2
    NHC = 11
    st["_m0free"] = {0: 0, 1: 0}
    st["_m0wfree"] = {0: 0, 1: 0}
    st["_m0hv"] = {0: 0, 1: 0}
    st["_m0xcons"] = {0: 0, 1: 0}
    st["_m0ov"] = {0: 0, 1: 0}
    st["_m0adma"] = {0: 0, 1: 0}

    l0a, l0b = st["l0_done"]
    g.wait("sl16", l0a)
    g.wait("sl17", l0b)
    g.op("dma_start", out=wgn[0:8, :], in_=wscr["moe0_wg"], incs=[ctx.inc("gld", 16)])
    t.wait("gld", ctx.val("gld"))
    for c in range(8):
        t.op("matmul", ps[7][:, c * 8:(c + 1) * 8],
             wgn[0:8, c * 128:(c + 1) * 128], id16[0:8, 0:8], start=True, stop=True)
    t.op("nop", incs=[ctx.inc("tt")])
    v.wait("tt", ctx.val("tt"))
    v.op("tensor_copy", wgT[:, :], ps[7][:, 0:64], incs=[ctx.inc("vv")])
    t.wait("vv", ctx.val("vv"))

    g.wait("vv", ctx.val("vv"))
    m0free = 0
    for b in range(NB):
        t0 = b * TBLK
        lds = f"sl{14 + (b % 2)}"
        if b >= 1:
            g.wait("vv", m0free)
        g.op("dma_start", out=_blk8(xf), in_=fm_ap(st["R1"], t0, TBLK),
             incs=[ctx.inc(lds, 16)])
        for q_ in (t, v, s):
            q_.wait(lds, ctx.val(lds))
        lnv = emit_ln(ctx, st, xf, xln16, 0, 1, 2, 3, sq)
        y.wait("vv", lnv)
        y.op("dma_start", out=fm_ap(st["XLN"], t0, TBLK), in_=_blk8x(xln16),
             incs=[ctx.inc("sst", 16)])
        t.wait("vv", lnv)
        emit_gates(ctx, st, xln16, st["GSC"], wgT, t0, renorm=True)
        v.op("nop", incs=[ctx.inc("vv")])
        m0free = ctx.val("vv")
    xlnv = ctx.val("sst")
    gatev = ctx.val("cst")

    g.wait("sst", xlnv)
    g.wait("cst", gatev)
    t.wait("sst", xlnv)
    acc_done = {tb: (0, 0) for tb in range(NB)}
    ACC = ("sl20", "sl21")
    m0wfree = 0
    m0xcons = 0
    for e in range(MOE0_E):
        for half in range(2):
            h0 = half * HHALF
            wld = f"sl{18 + ((e * 2 + half) % 2)}"
            y.wait("tt", m0wfree)
            for c in range(8):
                y.op("dma_start", out=w1T[:, c * HHALF:(c + 1) * HHALF],
                     in_=wscr["moe0_w1"][e][h0:h0 + HHALF, c * 128:(c + 1) * 128],
                     transpose=True, incs=[ctx.inc(wld, 16)])
                y.op("dma_start", out=w3T[:, c * HHALF:(c + 1) * HHALF],
                     in_=wscr["moe0_w3"][e][h0:h0 + HHALF, c * 128:(c + 1) * 128],
                     transpose=True, incs=[ctx.inc(wld, 16)])
            for hc in range(NHC):
                y.op("dma_start", out=w2T[:, hc * 1024:(hc + 1) * 1024],
                     in_=wscr["moe0_w2"][e][:, h0 + hc * 128:h0 + hc * 128 + 128],
                     transpose=True, incs=[ctx.inc(wld, 16)])
            t.wait(wld, ctx.val(wld))
            for tb in range(NB):
                t0 = tb * TBLK
                lds = f"sl{14 + (tb % 2)}"
                g.wait("tt", m0xcons)
                g.op("dma_start", out=_blk8x(xln16), in_=fm_ap(st["XLN"], t0, TBLK),
                     incs=[ctx.inc(lds, 16)])
                g.op("dma_start", out=cwrow[0:1, :], in_=st["GSC"][e:e + 1, t0:t0 + 512],
                     incs=[ctx.inc(lds, 16)])
                for q_ in (t, v):
                    q_.wait(lds, ctx.val(lds))
                for hc in range(NHC):
                    pa, pb_ = ps[2 * (hc % 2)], ps[2 * (hc % 2) + 1]
                    if hc >= 2:
                        t.wait("vv", st["_m0hv"][hc % 2])
                    for c in range(8):
                        t.op("matmul", pa[:, :],
                             w1T[:, c * HHALF + hc * 128:c * HHALF + hc * 128 + 128],
                             xln16[c], start=(c == 0), stop=(c == 7),
                             incs=([ctx.inc("tt")] if c == 7 else []))
                    s.wait("tt", ctx.val("tt"))
                    for c in range(8):
                        t.op("matmul", pb_[:, :],
                             w3T[:, c * HHALF + hc * 128:c * HHALF + hc * 128 + 128],
                             xln16[c], start=(c == 0), stop=(c == 7),
                             incs=([ctx.inc("tt")] if c == 7 else []))
                    if hc >= 2:
                        s.wait("vv", st["_m0hv"][hc % 2])
                    s.op("activation", hs16[hc % 2][:, :], pa[:, :], AF.Silu,
                         incs=[ctx.inc("ss")])
                    v.wait("tt", ctx.val("tt"))
                    v.wait("ss", ctx.val("ss"))
                    v.op("tensor_tensor", hact[:, hc * 512:(hc + 1) * 512],
                         hs16[hc % 2][:, :], pb_[:, :], ALU.mult, incs=[ctx.inc("vv")])
                    st["_m0hv"][hc % 2] = ctx.val("vv")
                t.wait("vv", ctx.val("vv"))
                t.op("matmul", ps[6][:, :], CONST[0:1, cp["ones"]:cp["ones"] + 128],
                     cwrow[0:1, :], start=True, stop=True, incs=[ctx.inc("tt")])
                m0xcons = ctx.val("tt")
                v.wait("tt", ctx.val("tt"))
                v.op("tensor_copy", cwb[:, :], ps[6][:, :], incs=[ctx.inc("vv")])
                cwv = ctx.val("vv")
                t.wait("vv", cwv)
                if e > 0 or half > 0:
                    g.wait(ACC[0], acc_done[tb][0])
                    g.wait(ACC[1], acc_done[tb][1])
                for dc in range(8):
                    ob = 4 + (dc % 2)
                    if dc >= 2:
                        t.wait("vv", st["_m0ov"][dc % 2])
                    for hc in range(NHC):
                        t.op("matmul", ps[ob][:, :],
                             w2T[:, hc * 1024 + dc * 128:hc * 1024 + dc * 128 + 128],
                             hact[:, hc * 512:(hc + 1) * 512],
                             start=(hc == 0), stop=(hc == NHC - 1),
                             incs=([ctx.inc("tt")] if hc == NHC - 1 else []))
                    v.wait("tt", ctx.val("tt"))
                    if dc >= 2:
                        v.wait(ACC[dc % 2], st["_m0adma"][dc % 2])
                    v.op("tensor_tensor", contrib[dc % 2], cwb[:, :], ps[ob][:, :],
                         ALU.mult, incs=[ctx.inc("vv")])
                    st["_m0ov"][dc % 2] = ctx.val("vv")
                    g.wait("vv", ctx.val("vv"))
                    g.op("dma_start", out=fm_ap(st["R1"], t0, TBLK, dc, 1),
                         in_=contrib[dc % 2].rearrange("p (c t) -> p c t", c=1),
                         accum_op=ALU.add, incs=[ctx.inc(ACC[dc % 2], 16)])
                    st["_m0adma"][dc % 2] = ctx.val(ACC[dc % 2])
                acc_done[tb] = (ctx.val(ACC[0]), ctx.val(ACC[1]))
            t.op("nop", incs=[ctx.inc("tt")])
            m0wfree = ctx.val("tt")
    st["moe0_done"] = (ctx.val(ACC[0]), ctx.val(ACC[1]))
    WB.pop(); AFA.pop()


def _blk8(tiles):
    t0 = tiles[0]
    n = t0.shape[1]
    par = t0.tensor
    off = t0.offset
    return bass.AP(par, off, [list(t0.ap[0]), [n, 8], [1, n]])


def _blk8x(tiles):
    return _blk8(tiles)


def stage_l1_attn(ctx, st):
    T, NB = st["T"], st["NB"]
    t, v, s, g, y = ctx.t, ctx.v, ctx.s, ctx.g, ctx.y
    WB, AFA, CONST, CONST16, ps = st["WB"], st["AF"], st["CONST"], st["CONST16"], st["ps"]
    cp = st["cpos"]
    io = st["io"]

    WB.push()
    winn = WB.alloc(8192, "winn")
    wgn = WB.alloc(8192, "wgn1")
    wdtn = WB.alloc(8192, "wdtn")
    woutn = WB.alloc(8192, "woutn")
    wbn = WB.alloc(128, "wbn")
    wcn = WB.alloc(128, "wcn")
    m0a, m0b = st["moe0_done"]
    g.wait("sl20", m0a)
    g.wait("sl21", m0b)
    for nme, dst in [("win", winn), ("wg", wgn), ("wdt", wdtn), ("wout", woutn)]:
        g.op("dma_start", out=dst.rearrange("p (c o) -> p c o", c=8),
             in_=io[nme].rearrange("(c p) o -> p c o", p=128), incs=[ctx.inc("gld", 16)])
    for nme, dst in [("wb", wbn), ("wc", wcn)]:
        g.op("dma_start", out=dst.rearrange("p (c o) -> p c o", c=8),
             in_=io[nme].rearrange("(c p) o -> p c o", p=128), incs=[ctx.inc("gld", 16)])
    wready = ctx.val("gld")
    t.wait("gld", wready)

    AFA.push(); WB.push()
    xf = [AFA.alloc(512, f"a1x{c}") for c in range(8)]
    sq = [AFA.alloc(512, "a1sq0"), AFA.alloc(512, "a1sq1")]
    r3 = AFA.alloc(1024, "a1r3")
    st["_ln_rows"] = [r3[0:1, 0:512], r3[0:1, 512:1024]]
    xinf = [AFA.alloc(512, "a1xi0"), AFA.alloc(512, "a1xi1")]
    xln16 = [WB.alloc(512, f"a1xln{c}") for c in range(8)]
    z16 = [WB.alloc(512, f"a1z{c}") for c in range(8)]
    st["_a1"] = {0: 0, 1: 0}
    st["_a1z"] = {0: 0, 1: 0}
    lnfree = 0
    zstore = 0
    for b in range(NB):
        t0 = b * TBLK
        lds = f"sl{14 + (b % 2)}"
        if b >= 1:
            g.wait("ss", lnfree)
        g.op("dma_start", out=_blk8(xf), in_=fm_ap(st["R1"], t0, TBLK),
             incs=[ctx.inc(lds, 16)])
        for q_ in (t, v, s):
            q_.wait(lds, ctx.val(lds))
        s.wait("sst", zstore)
        lnv = emit_ln(ctx, st, xf, xln16, 0, 1, 2, 3, sq)
        t.wait("vv", lnv)
        for oo in range(8):
            bk = 4 + (oo % 2)
            if oo >= 2:
                t.wait("ss", st["_a1"][oo % 2])
            for c in range(8):
                t.op("matmul", ps[bk][:, :], winn[:, c * 1024 + oo * 128:c * 1024 + oo * 128 + 128],
                     xln16[c], start=(c == 0), stop=(c == 7),
                     incs=([ctx.inc("tt")] if c == 7 else []))
            s.wait("tt", ctx.val("tt"))
            if oo >= 2:
                s.wait(f"sl{18 + (oo % 2)}", ctx.val(f"sl{18 + (oo % 2)}"))
            s.op("activation", xinf[oo % 2], ps[bk][:, :], AF.Identity,
                 bias=CONST[:, cp["b_in"] + oo:cp["b_in"] + oo + 1], incs=[ctx.inc("ss")])
            st["_a1"][oo % 2] = ctx.val("ss")
            y.wait("ss", ctx.val("ss"))
            y.op("dma_start", out=fm_ap(st["XIN"], t0, TBLK, oo, 1),
                 in_=xinf[oo % 2].rearrange("p (c t) -> p c t", c=1),
                 incs=[ctx.inc(f"sl{18 + (oo % 2)}", 16)])
        for oo in range(8):
            bk = 6 + (oo % 2)
            if oo >= 2:
                t.wait("ss", st["_a1z"][oo % 2])
            for c in range(8):
                t.op("matmul", ps[bk][:, :], wgn[:, c * 1024 + oo * 128:c * 1024 + oo * 128 + 128],
                     xln16[c], start=(c == 0), stop=(c == 7),
                     incs=([ctx.inc("tt")] if c == 7 else []))
            s.wait("tt", ctx.val("tt"))
            s.op("activation", z16[oo], ps[bk][:, :], AF.Silu,
                 bias=CONST[:, cp["bg"] + oo:cp["bg"] + oo + 1], incs=[ctx.inc("ss")])
            st["_a1z"][oo % 2] = ctx.val("ss")
        zv = ctx.val("ss")
        y.wait("ss", zv)
        y.op("dma_start", out=fm_ap(st["ZSC"], t0, TBLK), in_=_blk8(z16),
             incs=[ctx.inc("sst", 16)])
        zstore = ctx.val("sst")
        lnfree = ctx.val("ss")
    passa_st = (ctx.val("sl18"), ctx.val("sl19"), ctx.val("sst"))
    AFA.pop(); WB.pop()

    AFA.push(); WB.push()
    xinb = AFA.alloc(4112, "b1xin")
    xsf = [AFA.alloc(512, f"b1xs{c}") for c in range(8)]
    dtf = [xinb[:, c * 514:c * 514 + 512] for c in range(8)]
    halo = AFA.alloc(16, "b1halo")
    yc = AFA.alloc(512, "b1y")
    dA = AFA.alloc(512, "b1dA")
    ub = AFA.alloc(512, "b1u")
    hb = [AFA.alloc(512, "b1h0"), AFA.alloc(512, "b1h1")]
    ytmp = AFA.alloc(512, "b1yt")
    hlast = AFA.alloc(128, "b1hl")
    bmS = AFA.alloc(512, "b1bm")
    sfp = AFA.alloc(512, "b1sfp")
    xs16 = [WB.alloc(512, f"b1xs16{c}") for c in range(8)]
    dtx16 = [WB.alloc(512, f"b1dtx{c}") for c in range(8)]
    bmb16 = WB.alloc(8192, "b1bmb")
    cmb16 = WB.alloc(8192, "b1cmb")

    g.op("memset", hlast[:, :], 0.0)
    g.wait("sl18", passa_st[0])
    g.wait("sl19", passa_st[1])
    g.wait("sst", passa_st[2])
    g.op("nop" if False else "nop", incs=[ctx.inc("pp")])
    for q_ in (t, v, s):
        q_.wait("pp", ctx.val("pp"))
    st["_b1"] = {0: 0, 1: 0}
    st["_b1x"] = 0
    st["_b1bk"] = {0: 0, 1: 0}
    st["_b1row"] = {0: 0, 1: 0}
    for b in range(NB):
        t0 = b * TBLK
        lds = f"sl{14 + (b % 2)}"
        if b >= 1:
            g.wait("vv", st["_b1x"])
        g.op("dma_start", out=xinb.rearrange("p (c t) -> p c t", c=8)[:, :, 1:513],
             in_=fm_ap(st["XIN"], t0, TBLK), incs=[ctx.inc(lds, 16)])
        if b > 0:
            g.op("dma_start", out=halo[:, 0:8].rearrange("p (c t) -> p c t", c=8),
                 in_=fm_ap(st["XIN"], t0 - 1, 1), allow_slow_non_contiguous=True,
                 incs=[ctx.inc(lds, 16)])
        else:
            g.op("memset", halo[:, 0:8], 0.0, incs=[ctx.inc("pp")])
        if b < NB - 1:
            g.op("dma_start", out=halo[:, 8:16].rearrange("p (c t) -> p c t", c=8),
                 in_=fm_ap(st["XIN"], t0 + TBLK, 1), allow_slow_non_contiguous=True,
                 incs=[ctx.inc(lds, 16)])
        else:
            g.op("memset", halo[:, 8:16], 0.0, incs=[ctx.inc("pp")])
        for q_ in (t, v, s):
            q_.wait(lds, ctx.val(lds))
            q_.wait("pp", ctx.val("pp"))
        for c in range(8):
            v.op("tensor_copy", xinb[:, c * 514:c * 514 + 1], halo[:, c:c + 1])
            v.op("tensor_copy", xinb[:, c * 514 + 513:c * 514 + 514], halo[:, 8 + c:9 + c])
        v.op("drain")
        for c in range(8):
            o = c * 514
            v.op("tensor_scalar", xsf[c], xinb[:, o + 1:o + 513],
                 CONST[:, cp["convw"] + c * 3 + 1:cp["convw"] + c * 3 + 2], None, ALU.mult)
            v.op("drain")
            v.op("scalar_tensor_tensor", xsf[c], xinb[:, o:o + 512],
                 CONST[:, cp["convw"] + c * 3 + 0:cp["convw"] + c * 3 + 1], xsf[c],
                 ALU.mult, ALU.add)
            v.op("drain")
            v.op("scalar_tensor_tensor", xsf[c], xinb[:, o + 2:o + 514],
                 CONST[:, cp["convw"] + c * 3 + 2:cp["convw"] + c * 3 + 3], xsf[c],
                 ALU.mult, ALU.add, incs=[ctx.inc("vv")])
            s.wait("vv", ctx.val("vv"))
            s.op("activation", xs16[c], xsf[c], AF.Silu,
                 bias=CONST[:, cp["convb"] + c:cp["convb"] + c + 1])
            s.op("drain")
            s.op("activation", xsf[c], xsf[c], AF.Silu,
                 bias=CONST[:, cp["convb"] + c:cp["convb"] + c + 1], incs=[ctx.inc("ss")])
        convv = ctx.val("ss")
        t.wait("ss", convv)
        for oo in range(8):
            bk = 4 + (oo % 2)
            for c in range(8):
                t.op("matmul", ps[bk][:, :], wdtn[:, c * 1024 + oo * 128:c * 1024 + oo * 128 + 128],
                     xs16[c], start=(c == 0), stop=(c == 7),
                     incs=([ctx.inc("tt")] if c == 7 else []))
            s.wait("tt", ctx.val("tt"))
            s.op("activation", sfp[:, :], ps[bk][:, :], AF.Identity,
                 bias=CONST[:, cp["bdt"] + oo:cp["bdt"] + oo + 1])
            s.op("drain")
            s.op("activation", dtf[oo], sfp[:, :], AF.Abs)
            s.op("drain")
            s.op("activation", dtf[oo], dtf[oo], AF.Exp, scale=-1.0)
            s.op("drain")
            s.op("activation", dtf[oo], dtf[oo], AF.Ln, bias=1.0, incs=[ctx.inc("ss")])
            v.wait("ss", ctx.val("ss"))
            v.op("tensor_scalar", sfp[:, :], sfp[:, :], 0.0, None, ALU.max)
            v.op("drain")
            v.op("tensor_tensor", dtf[oo], dtf[oo], sfp[:, :], ALU.add)
            v.op("drain")
            v.op("tensor_tensor", dtx16[oo], dtf[oo], xsf[oo], ALU.mult,
                 incs=[ctx.inc("vv")])
            t.wait("vv", ctx.val("vv"))
            s.wait("vv", ctx.val("vv"))
        for im, (wn, bcol) in enumerate([(wbn, "bb"), (wcn, "bc")]):
            for c in range(8):
                t.op("matmul", ps[6][0:16, :], wn[:, c * 16:(c + 1) * 16], xs16[c],
                     start=(c == 0), stop=(c == 7),
                     incs=([ctx.inc("tt")] if c == 7 else []))
            s.wait("tt", ctx.val("tt"))
            s.op("activation", bmS[0:16, :], ps[6][0:16, :], AF.Identity,
                 bias=CONST[0:16, cp[bcol]:cp[bcol] + 1], incs=[ctx.inc("ss")])
            y.wait("ss", ctx.val("ss"))
            y.op("dma_start", out=st["BMR"][im, :, t0:t0 + 512], in_=bmS[0:16, :],
                 incs=[ctx.inc("cst", 16)])
            t.wait("cst", ctx.val("cst"))
        bmv = ctx.val("cst")
        g.wait("cst", bmv)
        ones_row = CONST[0:1, cp["ones"]:cp["ones"] + 128]
        for im in range(2):
            dstb = bmb16 if im == 0 else cmb16
            for n in range(SSM_N):
                rs = f"sl{22 + (n % 2)}"
                rowslot = (yc if n % 2 == 0 else dA)[0:1, 0:512]
                g.wait("vv", st["_b1row"][n % 2])
                g.op("dma_start", out=rowslot,
                     in_=st["BMR"][im, n:n + 1, t0:t0 + 512], incs=[ctx.inc(rs, 16)])
                t.wait(rs, ctx.val(rs))
                bk = 6 + (n % 2)
                if n >= 2 or im > 0:
                    t.wait("vv", st["_b1bk"][n % 2])
                t.op("matmul", ps[bk][:, :], ones_row, rowslot, start=True, stop=True,
                     incs=[ctx.inc("tt")])
                v.wait("tt", ctx.val("tt"))
                v.op("tensor_copy", dstb[:, n * 512:(n + 1) * 512], ps[bk][:, :],
                     incs=[ctx.inc("vv")])
                st["_b1bk"][n % 2] = ctx.val("vv")
                st["_b1row"][n % 2] = ctx.val("vv")
        v.op("nop", incs=[ctx.inc("vv")])
        bcv = ctx.val("vv")
        for q_ in (s, g):
            q_.wait("vv", bcv)
        for c in range(8):
            for n in range(SSM_N):
                hcol = c * 16 + n
                hbuf = hb[n % 2]
                s.op("activation", dA[:, :], dtf[c], AF.Exp,
                     scale=CONST[:, cp["A"] + c * 16 + n:cp["A"] + c * 16 + n + 1],
                     incs=[ctx.inc("ss")])
                g.op("tensor_tensor", ub[:, :], dtx16[c], bmb16[:, n * 512:(n + 1) * 512],
                     ALU.mult, incs=[ctx.inc("pp")])
                v.wait("ss", ctx.val("ss"))
                v.wait("pp", ctx.val("pp"))
                v.op("tensor_tensor_scan", hbuf[:, :], dA[:, :], ub[:, :],
                     hlast[:, hcol:hcol + 1], ALU.mult, ALU.add)
                v.op("drain")
                v.op("tensor_copy", hlast[:, hcol:hcol + 1], hbuf[:, 511:512])
                if n == 0:
                    v.op("tensor_tensor", yc[:, :], hbuf[:, :],
                         cmb16[:, n * 512:(n + 1) * 512], ALU.mult)
                else:
                    v.op("tensor_tensor", ytmp[:, :], hbuf[:, :],
                         cmb16[:, n * 512:(n + 1) * 512], ALU.mult)
                    v.op("drain")
                    v.op("tensor_tensor", yc[:, :], yc[:, :], ytmp[:, :], ALU.add)
                v.op("drain")
                v.op("nop", incs=[ctx.inc("vv")])
                s.wait("vv", ctx.val("vv") - 1 if False else ctx.val("vv"))
                g.wait("vv", ctx.val("vv"))
            v.op("nop", incs=[ctx.inc("vv")])
            y.wait("vv", ctx.val("vv"))
            y.op("dma_start", out=fm_ap(st["XSS"], t0, TBLK, c, 1),
                 in_=yc.rearrange("p (c t) -> p c t", c=1), incs=[ctx.inc("sst", 16)])
            v.wait("sst", ctx.val("sst"))
        v.op("nop", incs=[ctx.inc("vv")])
        st["_b1x"] = ctx.val("vv")
    passb_st = ctx.val("sst")
    AFA.pop(); WB.pop()

    AFA.push(); WB.push()
    xssf = [AFA.alloc(512, f"c1s{c}") for c in range(8)]
    r1f = [AFA.alloc(512, f"c1r{c}") for c in range(8)]
    of = [AFA.alloc(512, "c1o0"), AFA.alloc(512, "c1o1")]
    zm16 = [WB.alloc(512, f"c1z{c}") for c in range(8)]
    zl16 = [WB.alloc(512, f"c1zl{c}") for c in range(8)]
    c1f = 0
    g.wait("sst", passb_st)
    for b in range(NB):
        t0 = b * TBLK
        lds = f"sl{14 + (b % 2)}"
        if b >= 1:
            g.wait("vv", c1f)
        g.op("dma_start", out=_blk8(xssf), in_=fm_ap(st["XSS"], t0, TBLK),
             incs=[ctx.inc(lds, 16)])
        g.op("dma_start", out=_blk8(r1f), in_=fm_ap(st["R1"], t0, TBLK),
             incs=[ctx.inc(lds, 16)])
        g.op("dma_start", out=_blk8(zl16), in_=fm_ap(st["ZSC"], t0, TBLK),
             incs=[ctx.inc(lds, 16)])
        for q_ in (t, v):
            q_.wait(lds, ctx.val(lds))
        for c in range(8):
            v.op("tensor_tensor", zm16[c], xssf[c], zl16[c], ALU.mult,
                 incs=([ctx.inc("vv")] if c == 7 else []))
        zmv = ctx.val("vv")
        t.wait("vv", zmv)
        for oo in range(8):
            bk = 4 + (oo % 2)
            for c in range(8):
                t.op("matmul", ps[bk][:, :], woutn[:, c * 1024 + oo * 128:c * 1024 + oo * 128 + 128],
                     zm16[c], start=(c == 0), stop=(c == 7),
                     incs=([ctx.inc("tt")] if c == 7 else []))
            s.wait("tt", ctx.val("tt"))
            if oo >= 2:
                s.wait(f"sl{18 + (oo % 2)}", ctx.val(f"sl{18 + (oo % 2)}"))
            s.op("activation", of[oo % 2], ps[bk][:, :], AF.Identity,
                 bias=CONST[:, cp["bout"] + oo:cp["bout"] + oo + 1], incs=[ctx.inc("ss")])
            v.wait("ss", ctx.val("ss"))
            v.op("tensor_tensor", of[oo % 2], of[oo % 2], r1f[oo], ALU.add,
                 incs=[ctx.inc("vv")])
            t.wait("vv", ctx.val("vv"))
            y.wait("vv", ctx.val("vv"))
            y.op("dma_start", out=fm_ap(st["R0"], t0, TBLK, oo, 1),
                 in_=of[oo % 2].rearrange("p (c t) -> p c t", c=1),
                 incs=[ctx.inc(f"sl{18 + (oo % 2)}", 16)])
        v.op("nop", incs=[ctx.inc("vv")])
        c1f = ctx.val("vv")
    st["l1a_done"] = (ctx.val("sl18"), ctx.val("sl19"))
    AFA.pop(); WB.pop()
    WB.pop()


def stage_l1_ffn(ctx, st):
    T, NB = st["T"], st["NB"]
    t, v, s, g, y = ctx.t, ctx.v, ctx.s, ctx.g, ctx.y
    WB, AFA, CONST, CONST16, ps = st["WB"], st["AF"], st["CONST"], st["CONST16"], st["ps"]
    cp = st["cpos"]
    wscr = st["wscr"]
    id16 = CONST16[:, 0:128]

    WB.push(); AFA.push()
    xf = [AFA.alloc(512, f"f1x{c}") for c in range(8)]
    sq = [AFA.alloc(512, "f1sq0"), AFA.alloc(512, "f1sq1")]
    r3 = AFA.alloc(1024, "f1r3")
    st["_ln_rows"] = [r3[0:1, 0:512], r3[0:1, 512:1024]]
    gt = AFA.alloc(64, "f1gt")
    grow = AFA.alloc(16, "f1grow")
    st["_gt"], st["_grow"] = gt, grow
    cwb = AFA.alloc(512, "f1cwb")
    st["_gs512"] = cwb
    contrib = [AFA.alloc(512, "f1ct0"), AFA.alloc(512, "f1ct1")]
    cwrow = AFA.alloc(512, "f1cwrow")

    xln16 = [WB.alloc(512, f"f1xln{c}") for c in range(8)]
    wgT = WB.alloc(64, "f1wgT")
    wgn = WB.alloc(1024, "f1wgn")
    w1T = WB.alloc(8192, "f1w1T")
    w3T = WB.alloc(8192, "f1w3T")
    w2T = WB.alloc(8192, "f1w2T")
    hact = WB.alloc(4096, "f1hact")
    hs16 = [WB.alloc(512, "f1hs0"), WB.alloc(512, "f1hs1")]

    st["_f1free"] = {0: 0, 1: 0}
    st["_f1wfree"] = {0: 0, 1: 0}
    st["_f1hv"] = {0: 0, 1: 0}
    st["_f1xcons"] = {0: 0, 1: 0}
    st["_f1ov"] = {0: 0, 1: 0}
    st["_f1adma"] = {0: 0, 1: 0}

    a1, a2 = st["l1a_done"]
    g.wait("sl18", a1)
    g.wait("sl19", a2)
    g.wait("vv", ctx.val("vv"))
    g.op("dma_start", out=wgn[0:8, :], in_=wscr["moe1_wg"], incs=[ctx.inc("gld", 16)])
    t.wait("gld", ctx.val("gld"))
    for c in range(8):
        t.op("matmul", ps[7][:, c * 8:(c + 1) * 8],
             wgn[0:8, c * 128:(c + 1) * 128], id16[0:8, 0:8], start=True, stop=True)
    t.op("nop", incs=[ctx.inc("tt")])
    v.wait("tt", ctx.val("tt"))
    v.op("tensor_copy", wgT[:, :], ps[7][:, 0:64], incs=[ctx.inc("vv")])
    t.wait("vv", ctx.val("vv"))

    f1free = 0
    for b in range(NB):
        t0 = b * TBLK
        lds = f"sl{14 + (b % 2)}"
        if b >= 1:
            g.wait("vv", f1free)
        g.op("dma_start", out=_blk8(xf), in_=fm_ap(st["R0"], t0, TBLK),
             incs=[ctx.inc(lds, 16)])
        for q_ in (t, v, s):
            q_.wait(lds, ctx.val(lds))
        lnv = emit_ln(ctx, st, xf, xln16, 0, 1, 2, 3, sq)
        y.wait("vv", lnv)
        y.op("dma_start", out=fm_ap(st["XLN"], t0, TBLK), in_=_blk8(xln16),
             incs=[ctx.inc("sst", 16)])
        t.wait("vv", lnv)
        emit_gates(ctx, st, xln16, st["GSC1"], wgT, t0, renorm=False, use_bias=True)
        v.op("nop", incs=[ctx.inc("vv")])
        f1free = ctx.val("vv")
    xlnv = ctx.val("sst")
    gatev = ctx.val("cst")

    g.wait("sst", xlnv)
    g.wait("cst", gatev)
    t.wait("sst", xlnv)
    acc_done = {tb: (0, 0) for tb in range(NB)}
    ACC = ("sl20", "sl21")
    vexp = [("sh", 0), ("sh", 1)] + [("e", e) for e in range(MOE1_E)]
    f1wfree = 0
    f1xcons = 0
    for vi, (kind, e) in enumerate(vexp):
        wld = f"sl{16 + (vi % 2)}"
        y.wait("tt", f1wfree)
        if kind == "sh":
            h0 = e * 1024
            for c in range(8):
                y.op("dma_start", out=w1T[:, c * 1024:(c + 1) * 1024],
                     in_=wscr["sw1"][h0:h0 + 1024, c * 128:(c + 1) * 128],
                     transpose=True, incs=[ctx.inc(wld, 16)])
                y.op("dma_start", out=w3T[:, c * 1024:(c + 1) * 1024],
                     in_=wscr["sw3"][h0:h0 + 1024, c * 128:(c + 1) * 128],
                     transpose=True, incs=[ctx.inc(wld, 16)])
                y.op("dma_start", out=w2T[:, c * 1024:(c + 1) * 1024],
                     in_=wscr["sw2"][:, h0 + c * 128:h0 + c * 128 + 128],
                     transpose=True, incs=[ctx.inc(wld, 16)])
        else:
            for c in range(8):
                y.op("dma_start", out=w1T[:, c * 1024:(c + 1) * 1024],
                     in_=wscr["moe1_w1"][e][:, c * 128:(c + 1) * 128],
                     transpose=True, incs=[ctx.inc(wld, 16)])
                y.op("dma_start", out=w3T[:, c * 1024:(c + 1) * 1024],
                     in_=wscr["moe1_w3"][e][:, c * 128:(c + 1) * 128],
                     transpose=True, incs=[ctx.inc(wld, 16)])
                y.op("dma_start", out=w2T[:, c * 1024:(c + 1) * 1024],
                     in_=wscr["moe1_w2"][e][:, c * 128:(c + 1) * 128],
                     transpose=True, incs=[ctx.inc(wld, 16)])
        t.wait(wld, ctx.val(wld))
        for tb in range(NB):
            t0 = tb * TBLK
            lds = f"sl{14 + (tb % 2)}"
            g.wait("tt", f1xcons)
            g.op("dma_start", out=_blk8(xln16), in_=fm_ap(st["XLN"], t0, TBLK),
                 incs=[ctx.inc(lds, 16)])
            if kind == "e":
                g.op("dma_start", out=cwrow[0:1, :], in_=st["GSC1"][e:e + 1, t0:t0 + 512],
                     incs=[ctx.inc(lds, 16)])
            for q_ in (t, v):
                q_.wait(lds, ctx.val(lds))
            for hc in range(8):
                pa, pb_ = ps[2 * (hc % 2)], ps[2 * (hc % 2) + 1]
                if hc >= 2:
                    t.wait("vv", st["_f1hv"][hc % 2])
                for c in range(8):
                    t.op("matmul", pa[:, :], w1T[:, c * 1024 + hc * 128:c * 1024 + hc * 128 + 128],
                         xln16[c], start=(c == 0), stop=(c == 7),
                         incs=([ctx.inc("tt")] if c == 7 else []))
                s.wait("tt", ctx.val("tt"))
                for c in range(8):
                    t.op("matmul", pb_[:, :], w3T[:, c * 1024 + hc * 128:c * 1024 + hc * 128 + 128],
                         xln16[c], start=(c == 0), stop=(c == 7),
                         incs=([ctx.inc("tt")] if c == 7 else []))
                if hc >= 2:
                    s.wait("vv", st["_f1hv"][hc % 2])
                s.op("activation", hs16[hc % 2][:, :], pa[:, :], AF.Silu,
                     incs=[ctx.inc("ss")])
                v.wait("tt", ctx.val("tt"))
                v.wait("ss", ctx.val("ss"))
                v.op("tensor_tensor", hact[:, hc * 512:(hc + 1) * 512],
                     hs16[hc % 2][:, :], pb_[:, :], ALU.mult, incs=[ctx.inc("vv")])
                st["_f1hv"][hc % 2] = ctx.val("vv")
            t.wait("vv", ctx.val("vv"))
            if kind == "e":
                t.op("matmul", ps[6][:, :], CONST[0:1, cp["ones"]:cp["ones"] + 128],
                     cwrow[0:1, :], start=True, stop=True, incs=[ctx.inc("tt")])
                v.wait("tt", ctx.val("tt"))
                v.op("tensor_copy", cwb[:, :], ps[6][:, :], incs=[ctx.inc("vv")])
                cwv = ctx.val("vv")
                t.wait("vv", cwv)
            f1xcons = ctx.val("tt")
            if vi > 0:
                g.wait(ACC[0], acc_done[tb][0])
                g.wait(ACC[1], acc_done[tb][1])
            for dc in range(8):
                ob = 4 + (dc % 2)
                if dc >= 2:
                    t.wait("vv", st["_f1ov"][dc % 2])
                for hc in range(8):
                    t.op("matmul", ps[ob][:, :],
                         w2T[:, hc * 1024 + dc * 128:hc * 1024 + dc * 128 + 128],
                         hact[:, hc * 512:(hc + 1) * 512],
                         start=(hc == 0), stop=(hc == 7),
                         incs=([ctx.inc("tt")] if hc == 7 else []))
                v.wait("tt", ctx.val("tt"))
                if dc >= 2:
                    v.wait(ACC[dc % 2], st["_f1adma"][dc % 2])
                if kind == "e":
                    v.op("tensor_tensor", contrib[dc % 2], cwb[:, :], ps[ob][:, :],
                         ALU.mult, incs=[ctx.inc("vv")])
                else:
                    v.op("tensor_copy", contrib[dc % 2], ps[ob][:, :],
                         incs=[ctx.inc("vv")])
                st["_f1ov"][dc % 2] = ctx.val("vv")
                g.wait("vv", ctx.val("vv"))
                g.op("dma_start", out=fm_ap(st["R0"], t0, TBLK, dc, 1),
                     in_=contrib[dc % 2].rearrange("p (c t) -> p c t", c=1),
                     accum_op=ALU.add, incs=[ctx.inc(ACC[dc % 2], 16)])
                st["_f1adma"][dc % 2] = ctx.val(ACC[dc % 2])
            acc_done[tb] = (ctx.val(ACC[0]), ctx.val(ACC[1]))
        t.op("nop", incs=[ctx.inc("tt")])
        f1wfree = ctx.val("tt")
    st["moe1_done"] = (ctx.val(ACC[0]), ctx.val(ACC[1]))
    WB.pop(); AFA.pop()


def stage_final_out(ctx, st):
    T, NT = st["T"], st["NT"]
    t, v, g, y = ctx.t, ctx.v, ctx.g, ctx.y
    AFA, CONST, ps = st["AF"], st["CONST"], st["ps"]
    id32 = CONST[:, 0:128]
    AFA.push()
    rin = [AFA.alloc(1024, "fo_in0"), AFA.alloc(1024, "fo_in1")]
    rtm = [AFA.alloc(1024, "fo_tm0"), AFA.alloc(1024, "fo_tm1")]
    ev_t = {0: 0, 1: 0}
    stg_t = {0: 0, 1: 0}
    m1a, m1b = st["moe1_done"]
    g.wait("sl20", m1a)
    g.wait("sl21", m1b)
    for i in range(NT):
        b = i % 2
        lds, sts = f"sl{8 + b}", f"sl{10 + b}"
        if i >= 2:
            g.wait("tt", stg_t[b])
        g.op("dma_start", out=rin[b].rearrange("p (c t) -> p c t", c=8),
             in_=fm_ap(st["R0"], i * 128, 128), incs=[ctx.inc(lds, 16)])
        t.wait(lds, ctx.val(lds))
        if i >= 2:
            t.wait("vv", ev_t[b])
        for c in range(8):
            bank = ps[c // 4 + 2 * b + 4]
            t.op("matmul", bank[:, (c % 4) * 128:(c % 4) * 128 + 128],
                 rin[b].rearrange("p (c n) -> p c n", c=8)[:, c, :], id32,
                 start=True, stop=True)
        t.op("nop", incs=[ctx.inc("tt")])
        stg_t[b] = ctx.val("tt")
        v.wait("tt", stg_t[b])
        if i >= 2:
            v.wait(sts, ctx.val(sts))
        v.op("tensor_copy", rtm[b][:, 0:512], ps[4 + 2 * b][:, :])
        v.op("tensor_copy", rtm[b][:, 512:1024], ps[5 + 2 * b][:, :], incs=[ctx.inc("vv")])
        ev_t[b] = ctx.val("vv")
        y.wait("vv", ev_t[b])
        y.op("dma_start", out=st["y"][i * 128:(i + 1) * 128, :], in_=rtm[b],
             incs=[ctx.inc(sts, 16)])
    AFA.pop()


def _mk_nc_full(T):
    return build_program(T, [stage_consts, stage_repack, stage_xt, stage_l0_attn,
                             stage_l0_moe, stage_l1_attn, stage_l1_ffn,
                             stage_final_out])


def stage_out_copy(ctx, st):
    nc, T, NT = ctx.nc, st["T"], st["NT"]
    g, t, v, y = ctx.g, ctx.t, ctx.v, ctx.y
    AFA, CONST, ps = st["AF"], st["CONST"], st["ps"]
    id32 = CONST[:, 0:128]
    AFA.push()
    rin = [AFA.alloc(1024, "oc_in0"), AFA.alloc(1024, "oc_in1")]
    rtm = [AFA.alloc(1024, "oc_tm0"), AFA.alloc(1024, "oc_tm1")]
    ev_t = {0: 0, 1: 0}
    stg_t = {0: 0, 1: 0}
    g.wait(f"sl6", st["xt_done"][0])
    g.wait(f"sl7", st["xt_done"][1])
    for i in range(NT):
        b = i % 2
        lds, sts = f"sl{8 + b}", f"sl{10 + b}"
        if i >= 2:
            g.wait("tt", stg_t[b])
        g.op("dma_start", out=rin[b].rearrange("p (c t) -> p c t", c=8),
             in_=fm_ap(st["R0"], i * 128, 128), incs=[ctx.inc(lds, 16)])
        t.wait(lds, ctx.val(lds))
        if i >= 2:
            t.wait("vv", ev_t[b])
        for c in range(8):
            bank = ps[c // 4 + 2 * b + 4]
            t.op("matmul", bank[:, (c % 4) * 128:(c % 4) * 128 + 128],
                 rin[b].rearrange("p (c n) -> p c n", c=8)[:, c, :], id32,
                 start=True, stop=True)
        t.op("nop", incs=[ctx.inc("tt")])
        stg_t[b] = ctx.val("tt")
        v.wait("tt", stg_t[b])
        if i >= 2:
            v.wait(sts, ctx.val(sts))
        v.op("tensor_copy", rtm[b][:, 0:512], ps[4 + 2 * b][:, :])
        v.op("tensor_copy", rtm[b][:, 512:1024], ps[5 + 2 * b][:, :], incs=[ctx.inc("vv")])
        ev_t[b] = ctx.val("vv")
        y.wait("vv", ev_t[b])
        y.op("dma_start", out=st["y"][i * 128:(i + 1) * 128, :], in_=rtm[b],
             incs=[ctx.inc(sts, 16)])
    AFA.pop()


def kernel(**inputs):
    T = 8192
    cst = host_constants()
    key = {
        "x": None,
        "q_w": "L0_attn_q_w", "k_w": "L0_attn_k_w", "v_w": "L0_attn_v_w",
        "o_w": "L0_attn_o_w", "cs_w1": "L0_attn_cs_w1", "cs_w2": "L0_attn_cs_w2",
        "moe0_w1": "L0_ffn_W1", "moe0_w2": "L0_ffn_W2", "moe0_w3": "L0_ffn_W3",
        "moe0_wg": "L0_ffn_Wg",
        "win": "L1_attn_Win", "wg": "L1_attn_Wg", "wdt": "L1_attn_Wdt",
        "wout": "L1_attn_Wout", "wb": "L1_attn_Wb", "wc": "L1_attn_Wc",
        "moe1_w1": "L1_ffn_w1", "moe1_w2": "L1_ffn_w2", "moe1_w3": "L1_ffn_w3",
        "moe1_wg": "L1_ffn_gate_w",
        "sw1": "L1_ffn_sw1", "sw3": "L1_ffn_sw3", "sw2": "L1_ffn_sw2",
        "cs_b1": "L0_attn_cs_b1", "cs_b2": "L0_attn_cs_b2",
        "decay": "L0_attn_decay_params",
        "b_in": "L1_attn_b_in", "bg": "L1_attn_bg", "bout": "L1_attn_bout",
        "bdt": "L1_attn_bdt", "convw": "L1_attn_convw", "convb": "L1_attn_convb",
        "A": "L1_attn_A", "bb": "L1_attn_bb", "bc": "L1_attn_bc",
        "gate_bias": "L1_ffn_gate_bias",
    }
    x = np.ascontiguousarray(inputs["L0_attn_x"])
    base = {}
    for k, src in key.items():
        if src is not None:
            base[k] = np.ascontiguousarray(inputs[src], dtype=np.float32)
    base.update(cst)
    nc = _mk_nc_full(T)
    in_maps = []
    for b in range(8):
        m = dict(base)
        m["x"] = np.ascontiguousarray(x[b])
        in_maps.append(m)
    res = run_bass_kernel_spmd(nc, in_maps, core_ids=list(range(8)))
    out = np.stack([r["y"] for r in res.results], axis=0)
    return out


# version 04
# speedup vs baseline: 1.0922x; 1.0922x over previous
import math
import numpy as np
import contextlib

import concourse.bass as bass
import concourse.mybir as mybir
from concourse.bass_utils import run_bass_kernel_spmd

F32 = mybir.dt.float32
BF16 = mybir.dt.bfloat16
AF = mybir.ActivationFunctionType
ALU = mybir.AluOpType
AX = mybir.AxisListType

D = 1024
NCHUNK = 8
H = 16
E = 64
CS_DH = 256
GLA_C = 64
MOE0_H = 2816
MOE0_E = 8
SSM_N = 16
MOE1_H = 1024
MOE1_E = 8
SHARED_H = 2048
TBLK = 512


class Q:

    def __init__(self, name):
        self.name = name
        self.ops = []

    def wait(self, sem, val):
        if val > 0:
            self.ops.append(("wait", sem, val))

    def op(self, meth, *args, incs=(), **kw):
        self.ops.append(("op", meth, args, kw, tuple(incs)))

    def fold_nops(self):
        out = []
        for o in self.ops:
            if o[0] == "op" and o[1] == "nop" and o[4]:
                for k in range(len(out) - 1, -1, -1):
                    if out[k][0] == "op" and out[k][1] != "dma_start":
                        merged = {}
                        for s_, n_ in out[k][4] + o[4]:
                            merged[s_] = merged.get(s_, 0) + n_
                        assert len(merged) <= 1, f"multi-sem inc on PE op: {merged}"
                        out[k] = (out[k][0], out[k][1], out[k][2], out[k][3],
                                  tuple(merged.items()))
                        break
                else:
                    out.append(o)
                continue
            out.append(o)
        self.ops = out

    def replay(self, eng, sems):
        for o in self.ops:
            if o[0] == "wait":
                eng.wait_ge(sems[o[1]], o[2])
            else:
                _, meth, args, kw, incs = o
                r = getattr(eng, meth)(*args, **kw)
                for s, n in incs:
                    r = r.then_inc(sems[s], n)


class Ctx:

    def __init__(self, nc, T):
        self.nc = nc
        self.T = T
        self.t = Q("tensor")
        self.v = Q("vector")
        self.s = Q("scalar")
        self.g = Q("gpsimd")
        self.y = Q("sync")
        self.sem = {}
        self.semset = []
        self.dma_sems = set()

    def new_sem(self, name):
        assert name not in self.sem
        self.sem[name] = 0
        self.semset.append(name)
        return name

    def inc(self, name, n=1):
        self.sem[name] += n
        if n >= 16:
            self.dma_sems.add(name)
        return (name, n)

    def val(self, name):
        return self.sem[name]

    def queues(self):
        return [("tensor", self.t), ("vector", self.v), ("scalar", self.s),
                ("gpsimd", self.g), ("sync", self.y)]


class Arena:

    def __init__(self, tensor, n_elems, esize):
        self.t = tensor
        self.n = n_elems
        self.esize = esize
        self.off = 0
        self.marks = []

    def alloc(self, n_elems, tag=""):
        assert self.off + n_elems <= self.n, (
            f"arena overflow: {tag} wants {n_elems} at {self.off}/{self.n}")
        o = self.off
        self.off += n_elems
        return self.t[:, o:o + n_elems]

    def push(self):
        self.marks.append(self.off)

    def pop(self):
        self.off = self.marks.pop()


def fm_ap(dram3d, t0, tl, c0=0, cn=NCHUNK):
    return dram3d[c0:c0 + cn, :, t0:t0 + tl].rearrange("c p t -> p c t")


def silu_np(x):
    return x / (1 + np.exp(-x))


def host_constants():
    cst = {}
    cst["c_id32"] = np.eye(128, dtype=np.float32)
    cst["c_id16"] = np.eye(128, dtype=np.float32)
    cst["c_tri"] = np.triu(np.ones((GLA_C, GLA_C), np.float32))
    cst["c_triu16"] = np.triu(np.ones((GLA_C, GLA_C), np.float32))
    sel = np.zeros((H, NCHUNK * 128), np.float32)
    for j in range(NCHUNK):
        sel[2 * j, j * 128:j * 128 + 64] = 1.0
        sel[2 * j + 1, j * 128 + 64:j * 128 + 128] = 1.0
    cst["c_sel"] = sel
    cst["c_ones"] = np.ones((1, 128), np.float32)
    return cst


def build_program(T, stages, core_dbg=False):
    nc = bass.Bass("TRN2", target_bir_lowering=False)
    NT = T // 128
    NB = T // TBLK

    io = {}

    def inp(name, shape, dtype=F32):
        io[name] = nc.dram_tensor(name, list(shape), dtype, kind="ExternalInput").ap()
        return io[name]

    x_in = inp("x", (T, D))
    w = {}
    for n, shape in [
            ("q_w", (D, D)), ("k_w", (D, D)), ("v_w", (D, D)), ("o_w", (D, D)),
            ("cs_w1", (CS_DH, D)), ("cs_w2", (H, CS_DH)),
            ("moe0_w1", (MOE0_E, MOE0_H, D)), ("moe0_w2", (MOE0_E, D, MOE0_H)),
            ("moe0_w3", (MOE0_E, MOE0_H, D)), ("moe0_wg", (MOE0_E, D)),
            ("win", (D, D)), ("wg", (D, D)), ("wdt", (D, D)), ("wout", (D, D)),
            ("wb", (D, SSM_N)), ("wc", (D, SSM_N)),
            ("moe1_w1", (MOE1_E, MOE1_H, D)), ("moe1_w2", (MOE1_E, D, MOE1_H)),
            ("moe1_w3", (MOE1_E, MOE1_H, D)), ("moe1_wg", (MOE1_E, D)),
            ("sw1", (SHARED_H, D)), ("sw3", (SHARED_H, D)), ("sw2", (D, SHARED_H)),
    ]:
        w[n] = inp(n, shape)
    vecs = {}
    for n, shape in [
            ("cs_b1", (CS_DH,)), ("cs_b2", (H,)), ("decay", (H, E)),
            ("b_in", (D,)), ("bg", (D,)), ("bout", (D,)), ("bdt", (D,)),
            ("convw", (D, 3)), ("convb", (D,)), ("A", (D, SSM_N)),
            ("bb", (SSM_N,)), ("bc", (SSM_N,)), ("gate_bias", (MOE1_E,)),
    ]:
        vecs[n] = inp(n, shape)
    for n, shape in [("c_id32", (128, 128)), ("c_id16", (128, 128)),
                     ("c_tri", (GLA_C, GLA_C)), ("c_triu16", (GLA_C, GLA_C)),
                     ("c_sel", (H, NCHUNK * 128)), ("c_ones", (1, 128))]:
        inp(n, shape)

    y_out = nc.dram_tensor("y", [T, D], F32, kind="ExternalOutput").ap()

    def scr(name, shape, dtype):
        return nc.dram_tensor(name, list(shape), dtype, kind="Internal").ap()

    R0 = scr("R0", (NCHUNK, 128, T), F32)
    R1 = scr("R1", (NCHUNK, 128, T), F32)
    XLN = scr("XLN", (NCHUNK, 128, T), BF16)
    ZSC = scr("ZSC", (NCHUNK, 128, T), BF16)
    XSS = scr("XSS", (NCHUNK, 128, T), F32)
    XIN = scr("XIN", (NCHUNK, 128, T), F32)
    GSC = scr("GSC", (MOE0_E, T), F32)
    GSC1 = scr("GSC1", (MOE1_E, T), F32)
    BMR = scr("BMR", (2, SSM_N, T), F32)
    wscr = {}
    for n in ["q_w", "k_w", "v_w", "o_w", "cs_w1", "cs_w2", "moe0_w1",
              "moe0_w2", "moe0_w3", "moe1_w1", "moe1_w2", "moe1_w3",
              "sw1", "sw3", "sw2", "moe0_wg", "moe1_wg"]:
        wscr[n] = scr("S_" + n, io[n].shape, BF16)

    es = contextlib.ExitStack()
    prog = {}
    with es:
        WB_N = 70656
        AF_N = 13056
        WB = Arena(es.enter_context(nc.sbuf_tensor("WB", [128, WB_N], BF16)), WB_N, 2)
        AFA = Arena(es.enter_context(nc.sbuf_tensor("AFA", [128, AF_N], F32)), AF_N, 4)
        CONST = es.enter_context(nc.sbuf_tensor("CONST", [128, 2048], F32))
        CONST16 = es.enter_context(nc.sbuf_tensor("CONST16", [128, 512], BF16))
        ps = [es.enter_context(nc.psum_tensor(f"ps{i}", [128, 512], F32))
              for i in range(8)]

        ctx = Ctx(nc, T)
        for s in (["gld", "hld", "sst", "cst", "tt", "vv", "ss", "pp", "fin"]
                  + [f"sl{i}" for i in range(24)]):
            ctx.new_sem(s)
        sems = {}
        for s in ctx.semset:
            sems[s] = es.enter_context(nc.semaphore(s))

        st = {"io": io, "w": w, "vecs": vecs, "wscr": wscr, "R0": R0, "R1": R1,
              "XLN": XLN, "ZSC": ZSC, "XSS": XSS, "XIN": XIN, "BMR": BMR,
              "GSC": GSC, "GSC1": GSC1,
              "WB": WB, "AF": AFA, "CONST": CONST, "CONST16": CONST16,
              "ps": ps, "y": y_out, "x": x_in,
              "T": T, "NT": NT, "NB": NB}

        for stage_fn in stages:
            stage_fn(ctx, st)

        for name in ctx.semset:
            if name in ctx.dma_sems and ctx.val(name) > 0:
                ctx.y.wait(name, ctx.val(name))

        ctx.t.fold_nops()
        ctx.v.fold_nops()
        ctx.s.fold_nops()
        ctx.g.fold_nops()
        with nc.Block() as block:
            @block.tensor
            def _(eng):
                ctx.t.replay(eng, sems)

            @block.vector
            def _(eng):
                ctx.v.replay(eng, sems)

            @block.scalar
            def _(eng):
                ctx.s.replay(eng, sems)

            @block.gpsimd
            def _(eng):
                ctx.g.replay(eng, sems)

            @block.sync
            def _(eng):
                ctx.y.replay(eng, sems)

    return nc


def stage_consts(ctx, st):
    io, vecs = st["io"], st["vecs"]
    nc = ctx.nc
    g, v, s, t, y = ctx.g, ctx.v, ctx.s, ctx.t, ctx.y
    CONST, CONST16 = st["CONST"], st["CONST16"]

    st["cpos"] = dict(id32=0, logbase=128, decay_stage=136, cs_b1=144, b_in=152,
                      bg=160, bout=168, bdt=176, convb=184, convw=192, A=216,
                      cs_b2=344, bb=345, bc=346, gate_bias=347, tri=348,
                      sel=412, ones=1440, ones_col=1568, gbrow=1570, gbias_b=1580)
    cp = st["cpos"]

    g.op("dma_start", out=CONST[:, 0:128], in_=io["c_id32"], incs=[ctx.inc("gld", 16)])
    g.op("dma_start", out=CONST16[:, 0:128], in_=io["c_id16"], incs=[ctx.inc("gld", 16)])
    g.op("dma_start", out=CONST16[0:64, 128:192], in_=io["c_triu16"], incs=[ctx.inc("gld", 16)])
    g.op("memset", CONST16[:, 192:320], 0.0)
    g.op("dma_start", out=CONST[0:64, cp["tri"]:cp["tri"] + 64], in_=io["c_tri"],
         incs=[ctx.inc("gld", 16)])
    g.op("dma_start", out=CONST[0:16, cp["sel"]:cp["sel"] + NCHUNK * 128].rearrange("p (c n) -> p c n", c=NCHUNK),
         in_=io["c_sel"].rearrange("p (c n) -> p c n", c=NCHUNK), incs=[ctx.inc("gld", 16)])
    g.op("dma_start", out=CONST[0:1, cp["ones"]:cp["ones"] + 128], in_=io["c_ones"],
         incs=[ctx.inc("gld", 16)])
    g.op("memset", CONST[:, cp["ones_col"]:cp["ones_col"] + 1], 1.0)
    for name, pos in [("b_in", "b_in"), ("bg", "bg"), ("bout", "bout"),
                      ("bdt", "bdt"), ("convb", "convb")]:
        g.op("dma_start", out=CONST[:, cp[pos]:cp[pos] + 8],
             in_=vecs[name].rearrange("(c p) -> p c", p=128),
             allow_slow_non_contiguous=True, incs=[ctx.inc("gld", 16)])
    g.op("dma_start", out=CONST[:, cp["cs_b1"]:cp["cs_b1"] + 2],
         in_=vecs["cs_b1"].rearrange("(c p) -> p c", p=128),
         allow_slow_non_contiguous=True, incs=[ctx.inc("gld", 16)])
    g.op("dma_start", out=CONST[:, cp["convw"]:cp["convw"] + 24].rearrange("p (c k) -> p c k", c=NCHUNK),
         in_=vecs["convw"].rearrange("(c p) k -> p c k", p=128), incs=[ctx.inc("gld", 16)])
    g.op("dma_start", out=CONST[:, cp["A"]:cp["A"] + 128].rearrange("p (c k) -> p c k", c=NCHUNK),
         in_=vecs["A"].rearrange("(c p) k -> p c k", p=128), incs=[ctx.inc("gld", 16)])
    g.op("dma_start", out=CONST[0:16, cp["cs_b2"]:cp["cs_b2"] + 1], in_=vecs["cs_b2"][:, None],
         incs=[ctx.inc("gld", 16)])
    g.op("dma_start", out=CONST[0:16, cp["bb"]:cp["bb"] + 1], in_=vecs["bb"][:, None],
         incs=[ctx.inc("gld", 16)])
    g.op("dma_start", out=CONST[0:16, cp["bc"]:cp["bc"] + 1], in_=vecs["bc"][:, None],
         incs=[ctx.inc("gld", 16)])
    g.op("dma_start", out=CONST[0:8, cp["gate_bias"]:cp["gate_bias"] + 1],
         in_=vecs["gate_bias"][:, None], incs=[ctx.inc("gld", 16)])
    g.op("dma_start", out=CONST[0:1, cp["gbrow"]:cp["gbrow"] + 8],
         in_=vecs["gate_bias"][None, :], incs=[ctx.inc("gld", 16)])
    g.op("dma_start", out=CONST[:, cp["decay_stage"]:cp["decay_stage"] + 8],
         in_=vecs["decay"].rearrange("h e -> (h e)").rearrange("(c p) -> p c", p=128),
         allow_slow_non_contiguous=True, incs=[ctx.inc("gld", 16)])
    gld_all = ctx.val("gld")

    v.wait("gld", gld_all)
    dstage = CONST[:, cp["decay_stage"]:cp["decay_stage"] + 8]
    lb = CONST[:, cp["logbase"]:cp["logbase"] + 8]
    s.wait("gld", gld_all)
    s.op("activation", lb, dstage, AF.Abs)
    s.op("activation", lb, lb, AF.Exp, scale=-1.0)
    s.op("activation", lb, lb, AF.Ln, bias=1.0, incs=[ctx.inc("ss")])
    v.wait("ss", ctx.val("ss"))
    v.op("tensor_scalar", dstage, dstage, -1.0, 0.0, ALU.mult, ALU.max)
    v.op("drain")
    v.op("tensor_tensor", lb, lb, dstage, ALU.add)
    v.op("drain")
    v.op("tensor_scalar", lb, lb, -1.0, None, ALU.mult, incs=[ctx.inc("vv")])
    t.wait("gld", gld_all)
    t.op("matmul", st["ps"][0][:, 0:8], CONST[0:1, cp["ones"]:cp["ones"] + 128],
         CONST[0:1, cp["gbrow"]:cp["gbrow"] + 8], start=True, stop=True,
         incs=[ctx.inc("tt")])
    v.wait("tt", ctx.val("tt"))
    v.op("tensor_copy", CONST[:, cp["gbias_b"]:cp["gbias_b"] + 8], st["ps"][0][:, 0:8],
         incs=[ctx.inc("vv")])
    st["_gbias_slice"] = CONST[:, cp["gbias_b"]:cp["gbias_b"] + 8]
    st["consts_done"] = (ctx.val("vv"), gld_all)


def _repack_one(ctx, st, src_ap, dst_ap, R, C, stage_tiles):
    g, y = ctx.g, ctx.y
    a = max(1, 2048 // C)
    while R % (128 * a) != 0:
        a //= 2
    assert a >= 1 and R % (128 * a) == 0
    nslab = R // (128 * a)
    for i in range(nslab):
        k = st["_rp_i"] % 2
        st["_rp_i"] += 1
        lds, sts = f"sl{2 * k}", f"sl{2 * k + 1}"
        g.wait(sts, ctx.val(sts))
        g.op("dma_start",
             out=stage_tiles[k][:, 0:a * C],
             in_=src_ap[i * 128 * a:(i + 1) * 128 * a, :].rearrange("(p a) c -> p (a c)", a=a),
             max_dma_last_dim=4096, incs=[ctx.inc(lds, 16)])
        y.wait(lds, ctx.val(lds))
        y.op("dma_start",
             out=dst_ap[i * 128 * a:(i + 1) * 128 * a, :].rearrange("(p a) c -> p (a c)", a=a),
             in_=stage_tiles[k][:, 0:a * C], incs=[ctx.inc(sts, 16)])


def stage_repack(ctx, st):
    WB = st["WB"]
    WB.push()
    stage_tiles = [WB.alloc(4096, "rp0"), WB.alloc(4096, "rp1")]
    st["_rp_i"] = 0
    io, wscr = st["io"], st["wscr"]
    jobs = []
    for n in ["q_w", "k_w", "v_w", "o_w", "cs_w1", "sw1", "sw3", "sw2"]:
        shp = io[n].shape
        jobs.append((io[n], wscr[n], shp[0], shp[1]))
    for n in ["moe0_w1", "moe0_w2", "moe0_w3", "moe1_w1", "moe1_w2", "moe1_w3"]:
        shp = io[n].shape
        for e in range(shp[0]):
            jobs.append((io[n][e], wscr[n][e], shp[1], shp[2]))
    for s_, d_, R, C in jobs:
        _repack_one(ctx, st, s_, d_, R, C, stage_tiles)
    g, y = ctx.g, ctx.y
    for n in ["cs_w2", "moe0_wg", "moe1_wg"]:
        shp = io[n].shape
        k = st["_rp_i"] % 2
        st["_rp_i"] += 1
        lds, sts = f"sl{2 * k}", f"sl{2 * k + 1}"
        g.wait(sts, ctx.val(sts))
        g.op("dma_start", out=stage_tiles[k][0:shp[0], 0:shp[1]], in_=io[n],
             max_dma_last_dim=4096, incs=[ctx.inc(lds, 16)])
        y.wait(lds, ctx.val(lds))
        y.op("dma_start", out=wscr[n], in_=stage_tiles[k][0:shp[0], 0:shp[1]],
             incs=[ctx.inc(sts, 16)])
    st["repack_done"] = (ctx.val("sl1"), ctx.val("sl3"))
    WB.pop()


def stage_xt(ctx, st):
    nc, T, NT = ctx.nc, st["T"], st["NT"]
    g, t, v, y = ctx.g, ctx.t, ctx.v, ctx.y
    AFA, CONST, ps = st["AF"], st["CONST"], st["ps"]
    cp = st["cpos"]
    id32 = CONST[:, 0:128]
    AFA.push()
    xin = [AFA.alloc(1024, "xt_in0"), AFA.alloc(1024, "xt_in1")]
    xfm = [AFA.alloc(1024, "xt_fm0"), AFA.alloc(1024, "xt_fm1")]
    ev_t = {0: 0, 1: 0}
    stg_t = {0: 0, 1: 0}
    cw = st["consts_done"]
    t.wait("gld", cw[1])
    for i in range(NT):
        b = i % 2
        lds, sts = f"sl{4 + b}", f"sl{6 + b}"
        if i >= 2:
            g.wait("tt", stg_t[b])
        g.op("dma_start", out=xin[b], in_=st["x"][i * 128:(i + 1) * 128, :],
             incs=[ctx.inc(lds, 16)])
        t.wait(lds, ctx.val(lds))
        if i >= 2:
            t.wait("vv", ev_t[b])
        for c in range(8):
            bank = ps[c // 4 + 2 * b]
            t.op("matmul", bank[:, (c % 4) * 128:(c % 4) * 128 + 128],
                 xin[b].rearrange("p (c n) -> p c n", c=8)[:, c, :], id32,
                 start=True, stop=True)
        t.op("nop", incs=[ctx.inc("tt")])
        stg_t[b] = ctx.val("tt")
        v.wait("tt", stg_t[b])
        if i >= 2:
            v.wait(sts, ctx.val(sts))
        v.op("tensor_copy", xfm[b][:, 0:512], ps[2 * b][:, :])
        v.op("tensor_copy", xfm[b][:, 512:1024], ps[2 * b + 1][:, :], incs=[ctx.inc("vv")])
        ev_t[b] = ctx.val("vv")
        y.wait("vv", ev_t[b])
        y.op("dma_start", out=fm_ap(st["R0"], i * 128, 128),
             in_=xfm[b].rearrange("p (c t) -> p c t", c=8), incs=[ctx.inc(sts, 16)])
    AFA.pop()
    st["xt_done"] = (ctx.val("sl6"), ctx.val("sl7"))


def emit_ln(ctx, st, xf, xln16, bS, bQ, bR, bM, sq):
    t, v, s = ctx.t, ctx.v, ctx.s
    CONST, cp = st["CONST"], st["cpos"]
    ps = st["ps"]
    ones_col = CONST[:, cp["ones_col"]:cp["ones_col"] + 1]
    ones_row = CONST[0:1, cp["ones"]:cp["ones"] + 128]
    m_, q_ = st["_ln_rows"][0], st["_ln_rows"][1]
    r_ = sq[0][0:1, 0:512]
    for c in range(8):
        t.op("matmul", ps[bS][0:1, :], ones_col, xf[c],
             start=(c == 0), stop=(c == 7), incs=([ctx.inc("tt")] if c == 7 else []))
    sqmm = {}
    for c in range(8):
        if c >= 2:
            s.wait("tt", sqmm[c - 2])
        s.op("activation", sq[c % 2], xf[c], AF.Square, incs=[ctx.inc("ss")])
        t.wait("ss", ctx.val("ss"))
        t.op("matmul", ps[bQ][0:1, :], ones_col, sq[c % 2],
             start=(c == 0), stop=(c == 7), incs=[ctx.inc("tt")])
        sqmm[c] = ctx.val("tt")
    tstat = ctx.val("tt")
    v.wait("tt", tstat)
    v.op("tensor_scalar", m_, ps[bS][0:1, :], 1.0 / D, None, ALU.mult)
    v.op("tensor_scalar", q_, ps[bQ][0:1, :], 1.0 / D, None, ALU.mult)
    v.op("drain")
    v.op("tensor_tensor", r_, m_, m_, ALU.mult)
    v.op("drain")
    v.op("tensor_tensor", q_, q_, r_, ALU.subtract)
    v.op("drain")
    v.op("tensor_scalar", q_, q_, 1e-5, None, ALU.add, incs=[ctx.inc("vv")])
    s.wait("vv", ctx.val("vv"))
    s.op("activation", q_, q_, AF.Ln)
    s.op("drain")
    s.op("activation", q_, q_, AF.Exp, scale=-0.5, incs=[ctx.inc("ss")])
    v.wait("ss", ctx.val("ss"))
    v.op("tensor_tensor", m_, m_, q_, ALU.mult, incs=[ctx.inc("vv")])
    t.wait("vv", ctx.val("vv"))
    t.op("matmul", ps[bR][:, :], ones_row, q_, start=True, stop=True)
    t.op("matmul", ps[bM][:, :], ones_row, m_, start=True, stop=True,
         incs=[ctx.inc("tt")])
    v.wait("tt", ctx.val("tt"))
    for c in range(8):
        v.op("tensor_tensor", sq[c % 2], xf[c], ps[bR][:, :], ALU.mult)
        v.op("drain")
        v.op("tensor_tensor", xln16[c], sq[c % 2], ps[bM][:, :], ALU.subtract,
             incs=([ctx.inc("vv")] if c == 7 else []))
    return ctx.val("vv")


L0_PHASES = 99
L0_SUB = 99
L0_KHT = 2


def stage_l0_attn(ctx, st):
    T, NB = st["T"], st["NB"]
    t, v, s, g, y = ctx.t, ctx.v, ctx.s, ctx.g, ctx.y
    WB, AFA, CONST, CONST16, ps = st["WB"], st["AF"], st["CONST"], st["CONST16"], st["ps"]
    cp = st["cpos"]
    wscr = st["wscr"]
    id32 = CONST[:, 0:128]
    id16 = CONST16[:, 0:128]
    triu16 = CONST16[0:64, 128:192]

    WB.push(); AFA.push()
    qkvoT = {n: WB.alloc(8192, n + "T") for n in ["q_w", "k_w", "v_w", "o_w"]}
    csw1T = WB.alloc(2048, "cs_w1T")
    csw2T = WB.alloc(32, "cs_w2T")
    y.wait("sl1", st["repack_done"][0])
    y.wait("sl3", st["repack_done"][1])
    for n, dst in qkvoT.items():
        for c in range(8):
            y.op("dma_start", out=dst[:, c * 1024:(c + 1) * 1024],
                 in_=wscr[n][:, c * 128:(c + 1) * 128], transpose=True,
                 incs=[ctx.inc("hld", 16)])
    for c in range(8):
        y.op("dma_start", out=csw1T[:, c * 256:(c + 1) * 256],
             in_=wscr["cs_w1"][:, c * 128:(c + 1) * 128], transpose=True,
             incs=[ctx.inc("hld", 16)])
    for c in range(2):
        y.op("dma_start", out=csw2T[:, c * 16:(c + 1) * 16],
             in_=wscr["cs_w2"][:, c * 128:(c + 1) * 128], transpose=True,
             incs=[ctx.inc("hld", 16)])
    wready = ctx.val("hld")

    xfb = AFA.alloc(4096, "l0x")
    xf = [xfb[:, c * 512:(c + 1) * 512] for c in range(8)]
    sq = [AFA.alloc(512, "l0sq0"), AFA.alloc(512, "l0sq1")]
    r3 = AFA.alloc(1024, "l0r3")
    st["_ln_rows"] = [r3[0:1, 0:512], r3[0:1, 512:1024]]
    lam = AFA.alloc(4096, "l0lam")
    ltm = AFA.alloc(1024, "l0ltm")
    negd = AFA.alloc(8, "l0negd")
    posd = AFA.alloc(8, "l0posd")
    sd = AFA.alloc(64, "l0sd")
    lend = AFA.alloc(64, "l0lend")
    Sst = AFA.alloc(512, "l0S")
    kvt = AFA.alloc(1024, "l0kvt")
    lcb = sq[1]
    xln16 = [WB.alloc(512, f"l0xln{c}") for c in range(8)]
    qs = [WB.alloc(512, f"l0q{c}") for c in range(8)]
    ks = [WB.alloc(512, f"l0k{c}") for c in range(8)]
    kh = [WB.alloc(512, f"l0kh{c}") for c in range(8)]
    lam16 = [WB.alloc(512, f"l0g{c}") for c in range(8)]
    vtb = [WB.alloc(1024, "l0vt0"), WB.alloc(1024, "l0vt1")]
    aofmb = WB.alloc(4096, "l0ao")
    aofm = [aofmb[:, c * 512:(c + 1) * 512] for c in range(8)]
    cs116 = aofmb[:, 0:1024]
    S16 = WB.alloc(512, "l0S16")
    aoc = WB.alloc(512, "l0aoc")
    aoc2 = WB.alloc(512, "l0aoc2")
    khT = WB.alloc(1024, "l0khT")
    pb = WB.alloc(1024, "l0pb")
    ao = WB.alloc(1024, "l0aoh")

    g.op("memset", Sst[:, :], 0.0, incs=[ctx.inc("pp")])
    v.wait("pp", ctx.val("pp"))
    t.wait("hld", wready)
    t.wait("vv", st["consts_done"][0])
    t.wait("gld", st["consts_done"][1])
    s.wait("gld", st["consts_done"][1])
    xt0, xt1 = st["xt_done"]
    g.wait("sl6", xt0)
    g.wait("sl7", xt1)

    st["_l0aev"] = {0: 0, 1: 0}
    st["_l0vfree"] = {0: 0, 1: 0}
    st["_l0khv"] = 0
    xfree = 0
    for b in range(NB):
        t0 = b * TBLK
        lds = f"sl{14 + (b % 2)}"
        if b >= 1:
            g.wait("vv", xfree)
        g.op("dma_start", out=xfb.rearrange("p (c t) -> p c t", c=8),
             in_=fm_ap(st["R0"], t0, TBLK), incs=[ctx.inc(lds, 16)])
        for q_ in (t, v, s):
            q_.wait(lds, ctx.val(lds))
        for q_ in (v, s):
            q_.wait("sl16", ctx.val("sl16"))
            q_.wait("sl17", ctx.val("sl17"))
        lnv = emit_ln(ctx, st, xf, xln16, 0, 1, 2, 3, sq)
        t.wait("vv", lnv)
        for j in range(2):
            for c in range(8):
                t.op("matmul", ps[4][:, :],
                     csw1T[:, c * 256 + j * 128:c * 256 + j * 128 + 128],
                     xln16[c], start=(c == 0), stop=(c == 7),
                     incs=([ctx.inc("tt")] if c == 7 else []))
            s.wait("tt", ctx.val("tt"))
            s.op("activation", cs116[:, j * 512:(j + 1) * 512], ps[4][:, :], AF.Tanh,
                 bias=CONST[:, cp["cs_b1"] + j:cp["cs_b1"] + j + 1], incs=[ctx.inc("ss")])
            t.wait("ss", ctx.val("ss"))
        for j in range(2):
            t.op("matmul", ps[5][0:16, :], csw2T[:, j * 16:(j + 1) * 16],
                 cs116[:, j * 512:(j + 1) * 512], start=(j == 0), stop=(j == 1),
                 incs=([ctx.inc("tt")] if j == 1 else []))
        s.wait("tt", ctx.val("tt"))
        s.op("activation", lcb[0:16, :], ps[5][0:16, :], AF.Sigmoid,
             bias=CONST[0:16, cp["cs_b2"]:cp["cs_b2"] + 1])
        s.op("drain")
        s.op("activation", lcb[0:16, :], lcb[0:16, :], AF.Ln, scale=0.2, bias=1.0,
             incs=[ctx.inc("ss")])
        t.wait("ss", ctx.val("ss"))
        lamv = {}
        for j in range(8):
            bk = 4 + (j % 2)
            if j >= 2:
                t.wait("vv", lamv[j % 2])
            t.op("matmul", ps[bk][:, :],
                 CONST[0:16, cp["sel"] + j * 128:cp["sel"] + (j + 1) * 128],
                 lcb[0:16, :], start=True, stop=True, incs=[ctx.inc("tt")])
            v.wait("tt", ctx.val("tt"))
            v.op("tensor_scalar", lam[:, j * 512:(j + 1) * 512], ps[bk][:, :],
                 CONST[:, cp["logbase"] + j:cp["logbase"] + j + 1],
                 math.log(0.9995), ALU.add, ALU.min, incs=[ctx.inc("vv")])
            lamv[j % 2] = ctx.val("vv")
        t.wait("vv", ctx.val("vv"))
        if L0_PHASES < 1:
            continue
        for gch in range(8):
            for j in range(8):
                t.op("matmul", ps[4 + j // 4][0:64, (j % 4) * 128:(j % 4) * 128 + 128],
                     lam[:, j * 512 + gch * 64:j * 512 + gch * 64 + 64], id32,
                     start=True, stop=True)
            t.op("nop", incs=[ctx.inc("tt")])
            v.wait("tt", ctx.val("tt"))
            v.op("tensor_copy", ltm[0:64, 0:512], ps[4][0:64, :])
            v.op("tensor_copy", ltm[0:64, 512:1024], ps[5][0:64, :], incs=[ctx.inc("vv")])
            t.wait("vv", ctx.val("vv"))
            for half in range(2):
                t.op("matmul", ps[6 + half][0:64, 0:512],
                     CONST[0:64, cp["tri"]:cp["tri"] + 64],
                     ltm[0:64, half * 512:(half + 1) * 512], start=True, stop=True,
                     incs=([ctx.inc("tt")] if half == 1 else []))
            v.wait("tt", ctx.val("tt"))
            v.op("tensor_copy", ltm[0:64, 0:512], ps[6][0:64, 0:512])
            v.op("tensor_copy", ltm[0:64, 512:1024], ps[7][0:64, 0:512],
                 incs=[ctx.inc("vv")])
            t.wait("vv", ctx.val("vv"))
            for j in range(8):
                t.op("matmul", ps[4 + j // 4][:, (j % 4) * 64:(j % 4) * 64 + 64],
                     ltm[0:64, j * 128:(j + 1) * 128], id32[0:64, 0:64],
                     start=True, stop=True)
            t.op("nop", incs=[ctx.inc("tt")])
            s.wait("tt", ctx.val("tt"))
            for j in range(8):
                Lp = ps[4 + j // 4][:, (j % 4) * 64:(j % 4) * 64 + 64]
                Lcol = ps[4 + j // 4][:, (j % 4) * 64 + 63:(j % 4) * 64 + 64]
                dcol = gch * 8 + j
                s.op("activation", negd[:, j:j + 1], Lcol, AF.Copy, scale=-0.5, bias=0.0)
                s.op("activation", posd[:, j:j + 1], Lcol, AF.Copy, scale=0.5, bias=0.0)
                s.op("activation", sd[:, dcol:dcol + 1], Lcol, AF.Exp, scale=0.5)
                s.op("activation", lend[:, dcol:dcol + 1], Lcol, AF.Exp)
                s.op("drain")
                s.op("activation", lam16[j][:, gch * 64:gch * 64 + 64], Lp, AF.Exp,
                     bias=negd[:, j:j + 1])
                s.op("activation", kh[j][:, gch * 64:gch * 64 + 64], Lp, AF.Exp,
                     scale=-1.0, bias=posd[:, j:j + 1])
            s.op("nop", incs=[ctx.inc("ss")])
            t.wait("ss", ctx.val("ss"))
        if L0_PHASES < 2:
            continue
        for j in range(8):
            for c in range(8):
                t.op("matmul", ps[4][:, :],
                     qkvoT["q_w"][:, c * 1024 + j * 128:c * 1024 + j * 128 + 128],
                     xln16[c], start=(c == 0), stop=(c == 7),
                     incs=([ctx.inc("tt")] if c == 7 else []))
            v.wait("tt", ctx.val("tt"))
            v.op("tensor_tensor", qs[j], ps[4][:, :], lam16[j], ALU.mult,
                 incs=[ctx.inc("vv")])
            qdone = ctx.val("vv")
            for c in range(8):
                t.op("matmul", ps[5][:, :],
                     qkvoT["k_w"][:, c * 1024 + j * 128:c * 1024 + j * 128 + 128],
                     xln16[c], start=(c == 0), stop=(c == 7),
                     incs=([ctx.inc("tt")] if c == 7 else []))
            t.wait("vv", qdone)
            v.wait("tt", ctx.val("tt"))
            v.op("tensor_tensor", ks[j], ps[5][:, :], kh[j], ALU.mult)
            v.op("drain")
            for gch in range(8):
                dcol = gch * 8 + j
                v.op("tensor_scalar", kh[j][:, gch * 64:gch * 64 + 64],
                     ks[j][:, gch * 64:gch * 64 + 64], sd[:, dcol:dcol + 1],
                     None, ALU.mult)
            v.op("nop", incs=[ctx.inc("vv")])
            t.wait("vv", ctx.val("vv"))
        projv = ctx.val("vv")
        if L0_PHASES < 3:
            continue
        t.wait("vv", projv)
        for gch in range(8):
            cc = slice(gch * 64, gch * 64 + 64)
            vt = vtb[gch % 2]
            t.wait("vv", st["_l0khv"])
            for oh in range(2):
                for c in range(8):
                    t.op("matmul", ps[6 + oh][0:64, :],
                         xln16[c][:, cc],
                         qkvoT["v_w"][:, c * 1024 + oh * 512:c * 1024 + oh * 512 + 512],
                         start=(c == 0), stop=(c == 7),
                         incs=([ctx.inc("tt")] if c == 7 else []))
            if gch >= 2:
                v.wait("tt", st["_l0vfree"][gch % 2])
            v.wait("tt", ctx.val("tt"))
            v.op("tensor_copy", vt[0:64, 0:512], ps[6][0:64, :])
            v.op("tensor_copy", vt[0:64, 512:1024], ps[7][0:64, :], incs=[ctx.inc("vv")])
            t.wait("vv", ctx.val("vv"))
            if L0_SUB < 1:
                continue
            for j in range(8):
                dcol = gch * 8 + j
                v.op("tensor_scalar", S16[:, j * 64:(j + 1) * 64],
                     Sst[:, j * 64:(j + 1) * 64], sd[:, dcol:dcol + 1], None, ALU.mult,
                     incs=([ctx.inc("vv")] if j == 7 else []))
            s16v = ctx.val("vv")
            t.wait("vv", s16v)
            if L0_KHT >= 1:
                for j in range(8 if L0_KHT != 3 else 1):
                    t.op("matmul", ps[6 + j // 4][0:64, (j % 4) * 128:(j % 4) * 128 + 128],
                         kh[j][:, cc], id16, start=True, stop=True)
                t.op("nop", incs=[ctx.inc("tt")])
            if L0_KHT >= 2:
                v.wait("tt", ctx.val("tt"))
                v.op("tensor_copy", khT[0:64, 0:512], ps[6][0:64, :])
                v.op("tensor_copy", khT[0:64, 512:1024], ps[7][0:64, :],
                     incs=[ctx.inc("vv")])
            khv = ctx.val("vv")
            st["_l0khv"] = khv
            if L0_SUB < 2:
                continue
            horder = [h for h in range(16) if h % 2 == 0] + \
                     [h for h in range(16) if h % 2 == 1]
            for i_, h in enumerate(horder):
                j, hh = h // 2, h % 2
                rr = slice(hh * 64, hh * 64 + 64)
                t.op("matmul", ps[h // 8][0:64, (h % 8) * 64:(h % 8) * 64 + 64],
                     ks[j][rr, cc], qs[j][rr, cc], start=True, stop=True,
                     incs=([ctx.inc("tt")] if i_ == 15 else []))
            v.wait("tt", ctx.val("tt"))
            for h in range(16):
                v.op("tensor_tensor", pb[0:64, h * 64:(h + 1) * 64],
                     ps[h // 8][0:64, (h % 8) * 64:(h % 8) * 64 + 64], triu16,
                     ALU.mult, incs=([ctx.inc("vv")] if h == 15 else []))
            pbv = ctx.val("vv")
            if L0_SUB < 3:
                continue
            t.wait("vv", pbv)
            for h in range(16):
                t.op("matmul", ps[2 + h // 8][0:64, (h % 8) * 64:(h % 8) * 64 + 64],
                     pb[0:64, h * 64:(h + 1) * 64],
                     vt[0:64, h * 64:h * 64 + 64], start=True, stop=True)
            for i_, h in enumerate(horder):
                j, hh = h // 2, h % 2
                rr = slice(hh * 64, hh * 64 + 64)
                t.op("matmul", ps[6 + h // 8][0:64, (h % 8) * 64:(h % 8) * 64 + 64],
                     qs[j][rr, cc], S16[rr, j * 64:(j + 1) * 64], start=True, stop=True,
                     incs=([ctx.inc("tt")] if i_ == 15 else []))
            s.wait("tt", ctx.val("tt"))
            s.op("activation", aoc[0:64, :], ps[2][0:64, :], AF.Copy, bias=0.0)
            s.op("activation", aoc[0:64, 512:1024] if False else aoc2[0:64, :],
                 ps[3][0:64, :], AF.Copy, bias=0.0, incs=[ctx.inc("ss")])
            v.wait("ss", ctx.val("ss"))
            v.op("tensor_tensor", ao[0:64, 0:512], aoc[0:64, :], ps[6][0:64, :], ALU.add)
            v.op("tensor_tensor", ao[0:64, 512:1024], aoc2[0:64, :], ps[7][0:64, :],
                 ALU.add, incs=[ctx.inc("vv")])
            st["_l0aev"][0] = ctx.val("vv")
            if L0_SUB < 4:
                continue
            t.wait("vv", khv)
            for h in range(16):
                j, hh = h // 2, h % 2
                t.op("matmul", ps[4 + h // 8][0:64, (h % 8) * 64:(h % 8) * 64 + 64],
                     vt[0:64, h * 64:h * 64 + 64],
                     khT[0:64, j * 128 + hh * 64:j * 128 + hh * 64 + 64],
                     start=True, stop=True,
                     incs=([ctx.inc("tt")] if h == 15 else []))
            v.wait("tt", ctx.val("tt"))
            v.op("tensor_copy", kvt[0:64, 0:512], ps[4][0:64, 0:512])
            v.op("tensor_copy", kvt[0:64, 512:1024], ps[5][0:64, 0:512],
                 incs=[ctx.inc("vv")])
            t.wait("vv", ctx.val("vv"))
            if L0_SUB < 5:
                continue
            for j in range(8):
                t.op("matmul", ps[6][:, j * 64:(j + 1) * 64],
                     kvt[0:64, j * 128:(j + 1) * 128], id32[0:64, 0:64],
                     start=True, stop=True)
                t.op("matmul", ps[7][:, j * 64:(j + 1) * 64],
                     ao[0:64, j * 128:(j + 1) * 128], id16[0:64, 0:64],
                     start=True, stop=True)
            t.op("nop", incs=[ctx.inc("tt")])
            v.wait("tt", ctx.val("tt"))
            for j in range(8):
                dcol = gch * 8 + j
                v.op("scalar_tensor_tensor", Sst[:, j * 64:(j + 1) * 64],
                     Sst[:, j * 64:(j + 1) * 64], lend[:, dcol:dcol + 1],
                     ps[6][:, j * 64:(j + 1) * 64], ALU.mult, ALU.add)
                v.op("tensor_copy", aofm[j][:, cc], ps[7][:, j * 64:(j + 1) * 64])
            v.op("nop", incs=[ctx.inc("vv")])
            st["_l0vfree"][gch % 2] = ctx.val("tt")
            t.wait("vv", ctx.val("vv"))
        if L0_PHASES < 4:
            continue
        for oo in range(8):
            bk = oo % 2
            for c in range(8):
                t.op("matmul", ps[bk][:, :],
                     qkvoT["o_w"][:, c * 1024 + oo * 128:c * 1024 + oo * 128 + 128],
                     aofm[c], start=(c == 0), stop=(c == 7),
                     incs=([ctx.inc("tt")] if c == 7 else []))
            v.wait("tt", ctx.val("tt"))
            if oo >= 2:
                v.wait(f"sl{16 + (oo % 2)}", ctx.val(f"sl{16 + (oo % 2)}"))
            v.op("tensor_tensor", sq[oo % 2], xf[oo], ps[bk][:, :], ALU.add,
                 incs=[ctx.inc("vv")])
            evv = ctx.val("vv")
            t.wait("vv", evv)
            y.wait("vv", evv)
            y.op("dma_start", out=fm_ap(st["R1"], t0, TBLK, oo, 1),
                 in_=sq[oo % 2].rearrange("p (c t) -> p c t", c=1),
                 incs=[ctx.inc(f"sl{16 + (oo % 2)}", 16)])
        v.op("nop", incs=[ctx.inc("vv")])
        xfree = ctx.val("vv")
    st["l0_done"] = (ctx.val("sl16"), ctx.val("sl17"))
    WB.pop(); AFA.pop()


def emit_gates(ctx, st, xln16, gscr, wgT, t0, renorm, use_bias=False):
    t, v, s, g, y = ctx.t, ctx.v, ctx.s, ctx.g, ctx.y
    ps = st["ps"]
    gt = st["_gt"]
    grow = st["_grow"]
    gs512 = st["_gs512"]
    CONST, cp = st["CONST"], st["cpos"]
    id32 = CONST[:, 0:128]
    for c in range(8):
        t.op("matmul", ps[4][0:8, :], wgT[:, c * 8:(c + 1) * 8], xln16[c],
             start=(c == 0), stop=(c == 7), incs=([ctx.inc("tt")] if c == 7 else []))
    logv = ctx.val("tt")
    v.wait("tt", logv)
    v.op("tensor_copy", gs512[0:8, :], ps[4][0:8, :], incs=[ctx.inc("vv")])
    t.wait("vv", ctx.val("vv"))
    for ti in range(4):
        t.op("matmul", ps[5][:, ti * 8:(ti + 1) * 8],
             gs512[0:8, ti * 128:(ti + 1) * 128], id32[0:8, 0:8],
             start=True, stop=True)
    t.op("nop", incs=[ctx.inc("tt")])
    v.wait("tt", ctx.val("tt"))
    lg = gt[:, 0:32]
    v.op("tensor_copy", lg, ps[5][:, 0:32], incs=[ctx.inc("vv")])
    s.wait("vv", ctx.val("vv"))
    m1 = grow[:, 0:1]; sm = grow[:, 1:2]; m2 = grow[:, 2:3]; ws = grow[:, 3:4]
    sc = gt[:, 32:64]
    for ti in range(4):
        sl = lg[:, ti * 8:(ti + 1) * 8]
        so = sc[:, ti * 8:(ti + 1) * 8]
        v.op("tensor_reduce", m1, sl, AX.X, ALU.max)
        v.op("drain")
        v.op("tensor_scalar", so, sl, m1, None, ALU.subtract, incs=[ctx.inc("vv")])
        s.wait("vv", ctx.val("vv"))
        s.op("activation", so, so, AF.Exp, incs=[ctx.inc("ss")])
        v.wait("ss", ctx.val("ss"))
        v.op("tensor_reduce", sm, so, AX.X, ALU.add)
        v.op("drain")
        v.op("reciprocal", sm, sm)
        v.op("drain")
        v.op("tensor_scalar", so, so, sm, None, ALU.mult)
        v.op("drain")
        if use_bias:
            sb = lg[:, ti * 8:(ti + 1) * 8]
            v.op("tensor_tensor", sb, so, st["_gbias_slice"], ALU.add)
            v.op("drain")
            base = sb
        else:
            base = so
        v.op("tensor_reduce", m1, base, AX.X, ALU.max)
        v.op("drain")
        v.op("tensor_scalar", grow[:, 4:12], base, m1, None, ALU.is_lt)
        v.op("drain")
        v.op("tensor_tensor", grow[:, 4:12], base, grow[:, 4:12], ALU.mult)
        v.op("drain")
        v.op("tensor_reduce", m2, grow[:, 4:12], AX.X, ALU.max)
        v.op("drain")
        v.op("tensor_scalar", grow[:, 4:12], base, m2, None, ALU.is_ge)
        v.op("drain")
        v.op("tensor_tensor", so, so, grow[:, 4:12], ALU.mult)
        v.op("drain")
        if renorm:
            v.op("tensor_reduce", ws, so, AX.X, ALU.add)
            v.op("drain")
            v.op("tensor_scalar", ws, ws, 1e-20, None, ALU.add)
            v.op("drain")
            v.op("reciprocal", ws, ws)
            v.op("drain")
            v.op("tensor_scalar", so, so, ws, None, ALU.mult)
            v.op("drain")
    v.op("nop", incs=[ctx.inc("vv")])
    t.wait("vv", ctx.val("vv"))
    for ti in range(4):
        t.op("matmul", ps[6][0:8, ti * 128:(ti + 1) * 128],
             sc[:, ti * 8:(ti + 1) * 8], id32, start=True, stop=True)
    t.op("nop", incs=[ctx.inc("tt")])
    v.wait("tt", ctx.val("tt"))
    v.op("tensor_copy", gs512[0:8, :], ps[6][0:8, :], incs=[ctx.inc("vv")])
    y.wait("vv", ctx.val("vv"))
    y.op("dma_start", out=gscr[:, t0:t0 + 512], in_=gs512[0:8, :],
         incs=[ctx.inc("cst", 16)])


def stage_l0_moe(ctx, st):
    T, NB = st["T"], st["NB"]
    t, v, s, g, y = ctx.t, ctx.v, ctx.s, ctx.g, ctx.y
    WB, AFA, CONST, CONST16, ps = st["WB"], st["AF"], st["CONST"], st["CONST16"], st["ps"]
    cp = st["cpos"]
    wscr = st["wscr"]
    id16 = CONST16[:, 0:128]

    WB.push(); AFA.push()
    xf = [AFA.alloc(512, f"m0x{c}") for c in range(8)]
    sq = [AFA.alloc(512, "m0sq0"), AFA.alloc(512, "m0sq1")]
    r3 = AFA.alloc(1024, "m0r3")
    st["_ln_rows"] = [r3[0:1, 0:512], r3[0:1, 512:1024]]
    gt = AFA.alloc(64, "m0gt")
    grow = AFA.alloc(16, "m0grow")
    st["_gt"], st["_grow"] = gt, grow
    cwb = AFA.alloc(512, "m0cwb")
    st["_gs512"] = cwb
    contrib = [AFA.alloc(512, "m0ct0"), AFA.alloc(512, "m0ct1")]
    cwrow2 = AFA.alloc(1024, "m0cwrow")

    xln16 = [WB.alloc(512, f"m0xln{c}") for c in range(8)]
    xln16b = [WB.alloc(512, f"m0xlnb{c}") for c in range(8)]
    xln2 = [xln16, xln16b]
    wgT = WB.alloc(64, "m0wgT")
    wgn = WB.alloc(1024, "m0wgn")
    w1T = WB.alloc(11264, "m0w1T")
    w3T = WB.alloc(11264, "m0w3T")
    w2T = WB.alloc(11264, "m0w2T")
    hact = WB.alloc(5632, "m0hact")
    hs16 = [WB.alloc(512, "m0hs0"), WB.alloc(512, "m0hs1")]

    HHALF = MOE0_H // 2
    NHC = 11
    st["_m0free"] = {0: 0, 1: 0}
    st["_m0wfree"] = {0: 0, 1: 0}
    st["_m0hv"] = {0: 0, 1: 0}
    st["_m0xcons"] = {0: 0, 1: 0}
    st["_m0ov"] = {0: 0, 1: 0}
    st["_m0adma"] = {0: 0, 1: 0}

    l0a, l0b = st["l0_done"]
    g.wait("sl16", l0a)
    g.wait("sl17", l0b)
    g.op("dma_start", out=wgn[0:8, :], in_=wscr["moe0_wg"], incs=[ctx.inc("gld", 16)])
    t.wait("gld", ctx.val("gld"))
    for c in range(8):
        t.op("matmul", ps[7][:, c * 8:(c + 1) * 8],
             wgn[0:8, c * 128:(c + 1) * 128], id16[0:8, 0:8], start=True, stop=True)
    t.op("nop", incs=[ctx.inc("tt")])
    v.wait("tt", ctx.val("tt"))
    v.op("tensor_copy", wgT[:, :], ps[7][:, 0:64], incs=[ctx.inc("vv")])
    t.wait("vv", ctx.val("vv"))

    g.wait("vv", ctx.val("vv"))
    m0free = 0
    for b in range(NB):
        t0 = b * TBLK
        lds = f"sl{14 + (b % 2)}"
        if b >= 1:
            g.wait("vv", m0free)
        g.op("dma_start", out=_blk8(xf), in_=fm_ap(st["R1"], t0, TBLK),
             incs=[ctx.inc(lds, 16)])
        for q_ in (t, v, s):
            q_.wait(lds, ctx.val(lds))
        lnv = emit_ln(ctx, st, xf, xln16, 0, 1, 2, 3, sq)
        y.wait("vv", lnv)
        y.op("dma_start", out=fm_ap(st["XLN"], t0, TBLK), in_=_blk8x(xln16),
             incs=[ctx.inc("sst", 16)])
        t.wait("vv", lnv)
        emit_gates(ctx, st, xln16, st["GSC"], wgT, t0, renorm=True)
        v.op("nop", incs=[ctx.inc("vv")])
        m0free = ctx.val("vv")
    xlnv = ctx.val("sst")
    gatev = ctx.val("cst")

    g.wait("sst", xlnv)
    g.wait("cst", gatev)
    t.wait("sst", xlnv)
    acc_done = {tb: (0, 0) for tb in range(NB)}
    ACC = ("sl20", "sl21")
    m0wfree = 0
    m0xcons = {0: 0, 1: 0}
    nld = [0]
    ldtgt = {}
    NIT = MOE0_E * 2 * NB

    def _m0_prefetch():
        it = nld[0]
        par = it % 2
        t0p = (it % NB) * TBLK
        ep = it // (2 * NB)
        lds = f"sl{14 + par}"
        g.wait("tt", m0xcons[par])
        g.op("dma_start", out=_blk8(xln2[par]), in_=fm_ap(st["XLN"], t0p, TBLK),
             incs=[ctx.inc(lds, 16)])
        g.op("dma_start", out=cwrow2[0:1, par * 512:par * 512 + 512],
             in_=st["GSC"][ep:ep + 1, t0p:t0p + 512], incs=[ctx.inc(lds, 16)])
        ldtgt[it] = ctx.val(lds)
        nld[0] += 1
    for e in range(MOE0_E):
        for half in range(2):
            h0 = half * HHALF
            wld = f"sl{18 + ((e * 2 + half) % 2)}"
            y.wait("tt", m0wfree)
            for c in range(8):
                y.op("dma_start", out=w1T[:, c * HHALF:(c + 1) * HHALF],
                     in_=wscr["moe0_w1"][e][h0:h0 + HHALF, c * 128:(c + 1) * 128],
                     transpose=True, incs=[ctx.inc(wld, 16)])
                y.op("dma_start", out=w3T[:, c * HHALF:(c + 1) * HHALF],
                     in_=wscr["moe0_w3"][e][h0:h0 + HHALF, c * 128:(c + 1) * 128],
                     transpose=True, incs=[ctx.inc(wld, 16)])
            for hc in range(NHC):
                y.op("dma_start", out=w2T[:, hc * 1024:(hc + 1) * 1024],
                     in_=wscr["moe0_w2"][e][:, h0 + hc * 128:h0 + hc * 128 + 128],
                     transpose=True, incs=[ctx.inc(wld, 16)])
            t.wait(wld, ctx.val(wld))
            for tb in range(NB):
                t0 = tb * TBLK
                it = (e * 2 + half) * NB + tb
                par = it % 2
                xln16c = xln2[par]
                cwrow = cwrow2[0:1, par * 512:par * 512 + 512]
                while nld[0] <= it + 1 and nld[0] < NIT:
                    _m0_prefetch()
                lds = f"sl{14 + par}"
                t.wait(lds, ldtgt[it])
                v.wait(lds, ldtgt[it])
                for hc in range(NHC):
                    pa, pb_ = ps[2 * (hc % 2)], ps[2 * (hc % 2) + 1]
                    if hc >= 2:
                        t.wait("vv", st["_m0hv"][hc % 2])
                    for c in range(8):
                        t.op("matmul", pa[:, :],
                             w1T[:, c * HHALF + hc * 128:c * HHALF + hc * 128 + 128],
                             xln16c[c], start=(c == 0), stop=(c == 7),
                             incs=([ctx.inc("tt")] if c == 7 else []))
                    s.wait("tt", ctx.val("tt"))
                    for c in range(8):
                        t.op("matmul", pb_[:, :],
                             w3T[:, c * HHALF + hc * 128:c * HHALF + hc * 128 + 128],
                             xln16c[c], start=(c == 0), stop=(c == 7),
                             incs=([ctx.inc("tt")] if c == 7 else []))
                    if hc >= 2:
                        s.wait("vv", st["_m0hv"][hc % 2])
                    s.op("activation", hs16[hc % 2][:, :], pa[:, :], AF.Silu,
                         incs=[ctx.inc("ss")])
                    v.wait("tt", ctx.val("tt"))
                    v.wait("ss", ctx.val("ss"))
                    v.op("tensor_tensor", hact[:, hc * 512:(hc + 1) * 512],
                         hs16[hc % 2][:, :], pb_[:, :], ALU.mult, incs=[ctx.inc("vv")])
                    st["_m0hv"][hc % 2] = ctx.val("vv")
                t.wait("vv", ctx.val("vv"))
                t.op("matmul", ps[6][:, :], CONST[0:1, cp["ones"]:cp["ones"] + 128],
                     cwrow[0:1, :], start=True, stop=True, incs=[ctx.inc("tt")])
                m0xcons[par] = ctx.val("tt")
                v.wait("tt", ctx.val("tt"))
                v.op("tensor_copy", cwb[:, :], ps[6][:, :], incs=[ctx.inc("vv")])
                cwv = ctx.val("vv")
                t.wait("vv", cwv)
                if e > 0 or half > 0:
                    g.wait(ACC[0], acc_done[tb][0])
                    g.wait(ACC[1], acc_done[tb][1])
                for dc in range(8):
                    ob = 4 + (dc % 2)
                    if dc >= 2:
                        t.wait("vv", st["_m0ov"][dc % 2])
                    for hc in range(NHC):
                        t.op("matmul", ps[ob][:, :],
                             w2T[:, hc * 1024 + dc * 128:hc * 1024 + dc * 128 + 128],
                             hact[:, hc * 512:(hc + 1) * 512],
                             start=(hc == 0), stop=(hc == NHC - 1),
                             incs=([ctx.inc("tt")] if hc == NHC - 1 else []))
                    v.wait("tt", ctx.val("tt"))
                    if dc >= 2:
                        v.wait(ACC[dc % 2], st["_m0adma"][dc % 2])
                    v.op("tensor_tensor", contrib[dc % 2], cwb[:, :], ps[ob][:, :],
                         ALU.mult, incs=[ctx.inc("vv")])
                    st["_m0ov"][dc % 2] = ctx.val("vv")
                    g.wait("vv", ctx.val("vv"))
                    g.op("dma_start", out=fm_ap(st["R1"], t0, TBLK, dc, 1),
                         in_=contrib[dc % 2].rearrange("p (c t) -> p c t", c=1),
                         accum_op=ALU.add, incs=[ctx.inc(ACC[dc % 2], 16)])
                    st["_m0adma"][dc % 2] = ctx.val(ACC[dc % 2])
                acc_done[tb] = (ctx.val(ACC[0]), ctx.val(ACC[1]))
            t.op("nop", incs=[ctx.inc("tt")])
            m0wfree = ctx.val("tt")
    st["moe0_done"] = (ctx.val(ACC[0]), ctx.val(ACC[1]))
    WB.pop(); AFA.pop()


def _blk8(tiles):
    t0 = tiles[0]
    n = t0.shape[1]
    par = t0.tensor
    off = t0.offset
    return bass.AP(par, off, [list(t0.ap[0]), [n, 8], [1, n]])


def _blk8x(tiles):
    return _blk8(tiles)


def stage_l1_attn(ctx, st):
    T, NB = st["T"], st["NB"]
    t, v, s, g, y = ctx.t, ctx.v, ctx.s, ctx.g, ctx.y
    WB, AFA, CONST, CONST16, ps = st["WB"], st["AF"], st["CONST"], st["CONST16"], st["ps"]
    cp = st["cpos"]
    io = st["io"]

    WB.push()
    winn = WB.alloc(8192, "winn")
    wgn = WB.alloc(8192, "wgn1")
    wdtn = WB.alloc(8192, "wdtn")
    woutn = WB.alloc(8192, "woutn")
    wbn = WB.alloc(128, "wbn")
    wcn = WB.alloc(128, "wcn")
    m0a, m0b = st["moe0_done"]
    g.wait("sl20", m0a)
    g.wait("sl21", m0b)
    for nme, dst in [("win", winn), ("wg", wgn), ("wdt", wdtn), ("wout", woutn)]:
        g.op("dma_start", out=dst.rearrange("p (c o) -> p c o", c=8),
             in_=io[nme].rearrange("(c p) o -> p c o", p=128), incs=[ctx.inc("gld", 16)])
    for nme, dst in [("wb", wbn), ("wc", wcn)]:
        g.op("dma_start", out=dst.rearrange("p (c o) -> p c o", c=8),
             in_=io[nme].rearrange("(c p) o -> p c o", p=128), incs=[ctx.inc("gld", 16)])
    wready = ctx.val("gld")
    t.wait("gld", wready)

    AFA.push(); WB.push()
    xf = [AFA.alloc(512, f"a1x{c}") for c in range(8)]
    sq = [AFA.alloc(512, "a1sq0"), AFA.alloc(512, "a1sq1")]
    r3 = AFA.alloc(1024, "a1r3")
    st["_ln_rows"] = [r3[0:1, 0:512], r3[0:1, 512:1024]]
    xinf = [AFA.alloc(512, "a1xi0"), AFA.alloc(512, "a1xi1")]
    xln16 = [WB.alloc(512, f"a1xln{c}") for c in range(8)]
    z16 = [WB.alloc(512, f"a1z{c}") for c in range(8)]
    st["_a1"] = {0: 0, 1: 0}
    st["_a1z"] = {0: 0, 1: 0}
    lnfree = 0
    zstore = 0
    for b in range(NB):
        t0 = b * TBLK
        lds = f"sl{14 + (b % 2)}"
        if b >= 1:
            g.wait("ss", lnfree)
        g.op("dma_start", out=_blk8(xf), in_=fm_ap(st["R1"], t0, TBLK),
             incs=[ctx.inc(lds, 16)])
        for q_ in (t, v, s):
            q_.wait(lds, ctx.val(lds))
        s.wait("sst", zstore)
        lnv = emit_ln(ctx, st, xf, xln16, 0, 1, 2, 3, sq)
        t.wait("vv", lnv)
        for oo in range(8):
            bk = 4 + (oo % 2)
            if oo >= 2:
                t.wait("ss", st["_a1"][oo % 2])
            for c in range(8):
                t.op("matmul", ps[bk][:, :], winn[:, c * 1024 + oo * 128:c * 1024 + oo * 128 + 128],
                     xln16[c], start=(c == 0), stop=(c == 7),
                     incs=([ctx.inc("tt")] if c == 7 else []))
            s.wait("tt", ctx.val("tt"))
            if oo >= 2:
                s.wait(f"sl{18 + (oo % 2)}", ctx.val(f"sl{18 + (oo % 2)}"))
            s.op("activation", xinf[oo % 2], ps[bk][:, :], AF.Identity,
                 bias=CONST[:, cp["b_in"] + oo:cp["b_in"] + oo + 1], incs=[ctx.inc("ss")])
            st["_a1"][oo % 2] = ctx.val("ss")
            y.wait("ss", ctx.val("ss"))
            y.op("dma_start", out=fm_ap(st["XIN"], t0, TBLK, oo, 1),
                 in_=xinf[oo % 2].rearrange("p (c t) -> p c t", c=1),
                 incs=[ctx.inc(f"sl{18 + (oo % 2)}", 16)])
        for oo in range(8):
            bk = 6 + (oo % 2)
            if oo >= 2:
                t.wait("ss", st["_a1z"][oo % 2])
            for c in range(8):
                t.op("matmul", ps[bk][:, :], wgn[:, c * 1024 + oo * 128:c * 1024 + oo * 128 + 128],
                     xln16[c], start=(c == 0), stop=(c == 7),
                     incs=([ctx.inc("tt")] if c == 7 else []))
            s.wait("tt", ctx.val("tt"))
            s.op("activation", z16[oo], ps[bk][:, :], AF.Silu,
                 bias=CONST[:, cp["bg"] + oo:cp["bg"] + oo + 1], incs=[ctx.inc("ss")])
            st["_a1z"][oo % 2] = ctx.val("ss")
        zv = ctx.val("ss")
        y.wait("ss", zv)
        y.op("dma_start", out=fm_ap(st["ZSC"], t0, TBLK), in_=_blk8(z16),
             incs=[ctx.inc("sst", 16)])
        zstore = ctx.val("sst")
        lnfree = ctx.val("ss")
    passa_st = (ctx.val("sl18"), ctx.val("sl19"), ctx.val("sst"))
    AFA.pop(); WB.pop()

    AFA.push(); WB.push()
    xinb = AFA.alloc(4112, "b1xin")
    xsf = [AFA.alloc(512, f"b1xs{c}") for c in range(8)]
    dtf = [xinb[:, c * 514:c * 514 + 512] for c in range(8)]
    halo = AFA.alloc(16, "b1halo")
    yc = AFA.alloc(512, "b1y")
    dA = AFA.alloc(1024, "b1dA")
    ub = AFA.alloc(1024, "b1u")
    hb = [AFA.alloc(512, "b1h0"), AFA.alloc(512, "b1h1")]
    ytmp = AFA.alloc(1024, "b1yt")
    hlast = AFA.alloc(128, "b1hl")
    bmS = ytmp[:, 0:512]
    sfp = yc
    xs16 = [WB.alloc(512, f"b1xs16{c}") for c in range(8)]
    dtx16 = [WB.alloc(512, f"b1dtx{c}") for c in range(8)]
    bmb16 = WB.alloc(8192, "b1bmb")
    cmb16 = WB.alloc(8192, "b1cmb")

    g.op("memset", hlast[:, :], 0.0)
    g.wait("sl18", passa_st[0])
    g.wait("sl19", passa_st[1])
    g.wait("sst", passa_st[2])
    g.op("nop" if False else "nop", incs=[ctx.inc("pp")])
    for q_ in (t, v, s):
        q_.wait("pp", ctx.val("pp"))
    st["_b1"] = {0: 0, 1: 0}
    st["_b1x"] = 0
    st["_b1bk"] = {0: 0, 1: 0}
    st["_b1row"] = {0: 0, 1: 0}
    for b in range(NB):
        t0 = b * TBLK
        lds = f"sl{14 + (b % 2)}"
        if b >= 1:
            g.wait("vv", st["_b1x"])
        g.op("dma_start", out=xinb.rearrange("p (c t) -> p c t", c=8)[:, :, 1:513],
             in_=fm_ap(st["XIN"], t0, TBLK), incs=[ctx.inc(lds, 16)])
        if b > 0:
            g.op("dma_start", out=halo[:, 0:8].rearrange("p (c t) -> p c t", c=8),
                 in_=fm_ap(st["XIN"], t0 - 1, 1), allow_slow_non_contiguous=True,
                 incs=[ctx.inc(lds, 16)])
        else:
            g.op("memset", halo[:, 0:8], 0.0, incs=[ctx.inc("pp")])
        if b < NB - 1:
            g.op("dma_start", out=halo[:, 8:16].rearrange("p (c t) -> p c t", c=8),
                 in_=fm_ap(st["XIN"], t0 + TBLK, 1), allow_slow_non_contiguous=True,
                 incs=[ctx.inc(lds, 16)])
        else:
            g.op("memset", halo[:, 8:16], 0.0, incs=[ctx.inc("pp")])
        for q_ in (t, v, s):
            q_.wait(lds, ctx.val(lds))
            q_.wait("pp", ctx.val("pp"))
        for c in range(8):
            v.op("tensor_copy", xinb[:, c * 514:c * 514 + 1], halo[:, c:c + 1])
            v.op("tensor_copy", xinb[:, c * 514 + 513:c * 514 + 514], halo[:, 8 + c:9 + c])
        v.op("drain")
        for c in range(8):
            o = c * 514
            v.op("tensor_scalar", xsf[c], xinb[:, o + 1:o + 513],
                 CONST[:, cp["convw"] + c * 3 + 1:cp["convw"] + c * 3 + 2], None, ALU.mult)
            v.op("drain")
            v.op("scalar_tensor_tensor", xsf[c], xinb[:, o:o + 512],
                 CONST[:, cp["convw"] + c * 3 + 0:cp["convw"] + c * 3 + 1], xsf[c],
                 ALU.mult, ALU.add)
            v.op("drain")
            v.op("scalar_tensor_tensor", xsf[c], xinb[:, o + 2:o + 514],
                 CONST[:, cp["convw"] + c * 3 + 2:cp["convw"] + c * 3 + 3], xsf[c],
                 ALU.mult, ALU.add, incs=[ctx.inc("vv")])
            s.wait("vv", ctx.val("vv"))
            s.op("activation", xs16[c], xsf[c], AF.Silu,
                 bias=CONST[:, cp["convb"] + c:cp["convb"] + c + 1])
            s.op("drain")
            s.op("activation", xsf[c], xsf[c], AF.Silu,
                 bias=CONST[:, cp["convb"] + c:cp["convb"] + c + 1], incs=[ctx.inc("ss")])
        convv = ctx.val("ss")
        t.wait("ss", convv)
        for oo in range(8):
            bk = 4 + (oo % 2)
            for c in range(8):
                t.op("matmul", ps[bk][:, :], wdtn[:, c * 1024 + oo * 128:c * 1024 + oo * 128 + 128],
                     xs16[c], start=(c == 0), stop=(c == 7),
                     incs=([ctx.inc("tt")] if c == 7 else []))
            s.wait("tt", ctx.val("tt"))
            s.op("activation", sfp[:, :], ps[bk][:, :], AF.Identity,
                 bias=CONST[:, cp["bdt"] + oo:cp["bdt"] + oo + 1])
            s.op("drain")
            s.op("activation", dtf[oo], sfp[:, :], AF.Abs)
            s.op("drain")
            s.op("activation", dtf[oo], dtf[oo], AF.Exp, scale=-1.0)
            s.op("drain")
            s.op("activation", dtf[oo], dtf[oo], AF.Ln, bias=1.0, incs=[ctx.inc("ss")])
            v.wait("ss", ctx.val("ss"))
            v.op("tensor_scalar", sfp[:, :], sfp[:, :], 0.0, None, ALU.max)
            v.op("drain")
            v.op("tensor_tensor", dtf[oo], dtf[oo], sfp[:, :], ALU.add)
            v.op("drain")
            v.op("tensor_tensor", dtx16[oo], dtf[oo], xsf[oo], ALU.mult,
                 incs=[ctx.inc("vv")])
            t.wait("vv", ctx.val("vv"))
            s.wait("vv", ctx.val("vv"))
        for im, (wn, bcol) in enumerate([(wbn, "bb"), (wcn, "bc")]):
            for c in range(8):
                t.op("matmul", ps[6][0:16, :], wn[:, c * 16:(c + 1) * 16], xs16[c],
                     start=(c == 0), stop=(c == 7),
                     incs=([ctx.inc("tt")] if c == 7 else []))
            s.wait("tt", ctx.val("tt"))
            s.op("activation", bmS[0:16, :], ps[6][0:16, :], AF.Identity,
                 bias=CONST[0:16, cp[bcol]:cp[bcol] + 1], incs=[ctx.inc("ss")])
            y.wait("ss", ctx.val("ss"))
            y.op("dma_start", out=st["BMR"][im, :, t0:t0 + 512], in_=bmS[0:16, :],
                 incs=[ctx.inc("cst", 16)])
            t.wait("cst", ctx.val("cst"))
        bmv = ctx.val("cst")
        g.wait("cst", bmv)
        ones_row = CONST[0:1, cp["ones"]:cp["ones"] + 128]
        for im in range(2):
            dstb = bmb16 if im == 0 else cmb16
            for n in range(SSM_N):
                rs = f"sl{22 + (n % 2)}"
                rowslot = (yc if n % 2 == 0 else dA)[0:1, 0:512]
                g.wait("vv", st["_b1row"][n % 2])
                g.op("dma_start", out=rowslot,
                     in_=st["BMR"][im, n:n + 1, t0:t0 + 512], incs=[ctx.inc(rs, 16)])
                t.wait(rs, ctx.val(rs))
                bk = 6 + (n % 2)
                if n >= 2 or im > 0:
                    t.wait("vv", st["_b1bk"][n % 2])
                t.op("matmul", ps[bk][:, :], ones_row, rowslot, start=True, stop=True,
                     incs=[ctx.inc("tt")])
                v.wait("tt", ctx.val("tt"))
                v.op("tensor_copy", dstb[:, n * 512:(n + 1) * 512], ps[bk][:, :],
                     incs=[ctx.inc("vv")])
                st["_b1bk"][n % 2] = ctx.val("vv")
                st["_b1row"][n % 2] = ctx.val("vv")
        v.op("nop", incs=[ctx.inc("vv")])
        bcv = ctx.val("vv")
        for q_ in (s, g):
            q_.wait("vv", bcv)
        scn = {0: 0, 1: 0}
        ysl = {0: 0, 1: 0}
        for c in range(8):
            ystore = ctx.val("sst")
            for n in range(SSM_N):
                hcol = c * 16 + n
                hbuf = hb[n % 2]
                sl2 = n % 2
                dAb = dA[:, sl2 * 512:sl2 * 512 + 512]
                ubb = ub[:, sl2 * 512:sl2 * 512 + 512]
                ytb = ytmp[:, sl2 * 512:sl2 * 512 + 512]
                if c + n > 0:
                    s.wait("vv", scn[sl2])
                    g.wait("vv", scn[sl2])
                s.op("activation", dAb, dtf[c], AF.Exp,
                     scale=CONST[:, cp["A"] + c * 16 + n:cp["A"] + c * 16 + n + 1],
                     incs=[ctx.inc("ss")])
                g.op("tensor_tensor", ubb, dtx16[c], bmb16[:, n * 512:(n + 1) * 512],
                     ALU.mult, incs=[ctx.inc("pp")])
                v.wait("ss", ctx.val("ss"))
                v.wait("pp", ctx.val("pp"))
                v.op("tensor_tensor_scan", hbuf[:, :], dAb, ubb,
                     hlast[:, hcol:hcol + 1], ALU.mult, ALU.add)
                v.op("drain")
                v.op("tensor_copy", hlast[:, hcol:hcol + 1], hbuf[:, 511:512],
                     incs=[ctx.inc("vv")])
                scn[sl2] = ctx.val("vv")
                g.wait("vv", scn[sl2])
                if n > 0:
                    g.wait("vv", ysl[sl2])
                g.op("tensor_tensor", ytb, hbuf[:, :],
                     cmb16[:, n * 512:(n + 1) * 512], ALU.mult, incs=[ctx.inc("pp")])
                v.wait("pp", ctx.val("pp"))
                if n == 0:
                    v.wait("sst", ystore)
                    v.op("tensor_copy", yc[:, :], ytb, incs=[ctx.inc("vv")])
                else:
                    v.op("tensor_tensor", yc[:, :], yc[:, :], ytb, ALU.add,
                         incs=[ctx.inc("vv")])
                ysl[sl2] = ctx.val("vv")
            y.wait("vv", ctx.val("vv"))
            y.op("dma_start", out=fm_ap(st["XSS"], t0, TBLK, c, 1),
                 in_=yc.rearrange("p (c t) -> p c t", c=1), incs=[ctx.inc("sst", 16)])
        v.op("nop", incs=[ctx.inc("vv")])
        st["_b1x"] = ctx.val("vv")
    passb_st = ctx.val("sst")
    AFA.pop(); WB.pop()

    AFA.push(); WB.push()
    xssf = [AFA.alloc(512, f"c1s{c}") for c in range(8)]
    r1f = [AFA.alloc(512, f"c1r{c}") for c in range(8)]
    of = [AFA.alloc(512, "c1o0"), AFA.alloc(512, "c1o1")]
    zm16 = [WB.alloc(512, f"c1z{c}") for c in range(8)]
    zl16 = [WB.alloc(512, f"c1zl{c}") for c in range(8)]
    c1f = 0
    g.wait("sst", passb_st)
    for b in range(NB):
        t0 = b * TBLK
        lds = f"sl{14 + (b % 2)}"
        if b >= 1:
            g.wait("vv", c1f)
        g.op("dma_start", out=_blk8(xssf), in_=fm_ap(st["XSS"], t0, TBLK),
             incs=[ctx.inc(lds, 16)])
        g.op("dma_start", out=_blk8(r1f), in_=fm_ap(st["R1"], t0, TBLK),
             incs=[ctx.inc(lds, 16)])
        g.op("dma_start", out=_blk8(zl16), in_=fm_ap(st["ZSC"], t0, TBLK),
             incs=[ctx.inc(lds, 16)])
        for q_ in (t, v):
            q_.wait(lds, ctx.val(lds))
        for c in range(8):
            v.op("tensor_tensor", zm16[c], xssf[c], zl16[c], ALU.mult,
                 incs=([ctx.inc("vv")] if c == 7 else []))
        zmv = ctx.val("vv")
        t.wait("vv", zmv)
        for oo in range(8):
            bk = 4 + (oo % 2)
            for c in range(8):
                t.op("matmul", ps[bk][:, :], woutn[:, c * 1024 + oo * 128:c * 1024 + oo * 128 + 128],
                     zm16[c], start=(c == 0), stop=(c == 7),
                     incs=([ctx.inc("tt")] if c == 7 else []))
            s.wait("tt", ctx.val("tt"))
            if oo >= 2:
                s.wait(f"sl{18 + (oo % 2)}", ctx.val(f"sl{18 + (oo % 2)}"))
            s.op("activation", of[oo % 2], ps[bk][:, :], AF.Identity,
                 bias=CONST[:, cp["bout"] + oo:cp["bout"] + oo + 1], incs=[ctx.inc("ss")])
            v.wait("ss", ctx.val("ss"))
            v.op("tensor_tensor", of[oo % 2], of[oo % 2], r1f[oo], ALU.add,
                 incs=[ctx.inc("vv")])
            t.wait("vv", ctx.val("vv"))
            y.wait("vv", ctx.val("vv"))
            y.op("dma_start", out=fm_ap(st["R0"], t0, TBLK, oo, 1),
                 in_=of[oo % 2].rearrange("p (c t) -> p c t", c=1),
                 incs=[ctx.inc(f"sl{18 + (oo % 2)}", 16)])
        v.op("nop", incs=[ctx.inc("vv")])
        c1f = ctx.val("vv")
    st["l1a_done"] = (ctx.val("sl18"), ctx.val("sl19"))
    AFA.pop(); WB.pop()
    WB.pop()


def stage_l1_ffn(ctx, st):
    T, NB = st["T"], st["NB"]
    t, v, s, g, y = ctx.t, ctx.v, ctx.s, ctx.g, ctx.y
    WB, AFA, CONST, CONST16, ps = st["WB"], st["AF"], st["CONST"], st["CONST16"], st["ps"]
    cp = st["cpos"]
    wscr = st["wscr"]
    id16 = CONST16[:, 0:128]

    WB.push(); AFA.push()
    xf = [AFA.alloc(512, f"f1x{c}") for c in range(8)]
    sq = [AFA.alloc(512, "f1sq0"), AFA.alloc(512, "f1sq1")]
    r3 = AFA.alloc(1024, "f1r3")
    st["_ln_rows"] = [r3[0:1, 0:512], r3[0:1, 512:1024]]
    gt = AFA.alloc(64, "f1gt")
    grow = AFA.alloc(16, "f1grow")
    st["_gt"], st["_grow"] = gt, grow
    cwb = AFA.alloc(512, "f1cwb")
    st["_gs512"] = cwb
    contrib = [AFA.alloc(512, "f1ct0"), AFA.alloc(512, "f1ct1")]
    cwrow2 = AFA.alloc(1024, "f1cwrow")

    xln16 = [WB.alloc(512, f"f1xln{c}") for c in range(8)]
    xln16b = [WB.alloc(512, f"f1xlnb{c}") for c in range(8)]
    xln2 = [xln16, xln16b]
    wgT = WB.alloc(64, "f1wgT")
    wgn = WB.alloc(1024, "f1wgn")
    w1T = WB.alloc(8192, "f1w1T")
    w3T = WB.alloc(8192, "f1w3T")
    w2T = WB.alloc(8192, "f1w2T")
    hact = WB.alloc(4096, "f1hact")
    hs16 = [WB.alloc(512, "f1hs0"), WB.alloc(512, "f1hs1")]

    st["_f1free"] = {0: 0, 1: 0}
    st["_f1wfree"] = {0: 0, 1: 0}
    st["_f1hv"] = {0: 0, 1: 0}
    st["_f1xcons"] = {0: 0, 1: 0}
    st["_f1ov"] = {0: 0, 1: 0}
    st["_f1adma"] = {0: 0, 1: 0}

    a1, a2 = st["l1a_done"]
    g.wait("sl18", a1)
    g.wait("sl19", a2)
    g.wait("vv", ctx.val("vv"))
    g.op("dma_start", out=wgn[0:8, :], in_=wscr["moe1_wg"], incs=[ctx.inc("gld", 16)])
    t.wait("gld", ctx.val("gld"))
    for c in range(8):
        t.op("matmul", ps[7][:, c * 8:(c + 1) * 8],
             wgn[0:8, c * 128:(c + 1) * 128], id16[0:8, 0:8], start=True, stop=True)
    t.op("nop", incs=[ctx.inc("tt")])
    v.wait("tt", ctx.val("tt"))
    v.op("tensor_copy", wgT[:, :], ps[7][:, 0:64], incs=[ctx.inc("vv")])
    t.wait("vv", ctx.val("vv"))

    f1free = 0
    for b in range(NB):
        t0 = b * TBLK
        lds = f"sl{14 + (b % 2)}"
        if b >= 1:
            g.wait("vv", f1free)
        g.op("dma_start", out=_blk8(xf), in_=fm_ap(st["R0"], t0, TBLK),
             incs=[ctx.inc(lds, 16)])
        for q_ in (t, v, s):
            q_.wait(lds, ctx.val(lds))
        lnv = emit_ln(ctx, st, xf, xln16, 0, 1, 2, 3, sq)
        y.wait("vv", lnv)
        y.op("dma_start", out=fm_ap(st["XLN"], t0, TBLK), in_=_blk8(xln16),
             incs=[ctx.inc("sst", 16)])
        t.wait("vv", lnv)
        emit_gates(ctx, st, xln16, st["GSC1"], wgT, t0, renorm=False, use_bias=True)
        v.op("nop", incs=[ctx.inc("vv")])
        f1free = ctx.val("vv")
    xlnv = ctx.val("sst")
    gatev = ctx.val("cst")

    g.wait("sst", xlnv)
    g.wait("cst", gatev)
    t.wait("sst", xlnv)
    acc_done = {tb: (0, 0) for tb in range(NB)}
    ACC = ("sl20", "sl21")
    vexp = [("sh", 0), ("sh", 1)] + [("e", e) for e in range(MOE1_E)]
    f1wfree = 0
    f1xcons = {0: 0, 1: 0}
    nld1 = [0]
    ldtgt1 = {}
    NIT1 = len(vexp) * NB

    def _f1_prefetch():
        it = nld1[0]
        par = it % 2
        t0p = (it % NB) * TBLK
        vip = it // NB
        kindp, ep = vexp[vip]
        lds = f"sl{14 + par}"
        g.wait("tt", f1xcons[par])
        g.op("dma_start", out=_blk8(xln2[par]), in_=fm_ap(st["XLN"], t0p, TBLK),
             incs=[ctx.inc(lds, 16)])
        if kindp == "e":
            g.op("dma_start", out=cwrow2[0:1, par * 512:par * 512 + 512],
                 in_=st["GSC1"][ep:ep + 1, t0p:t0p + 512], incs=[ctx.inc(lds, 16)])
        ldtgt1[it] = ctx.val(lds)
        nld1[0] += 1
    for vi, (kind, e) in enumerate(vexp):
        wld = f"sl{16 + (vi % 2)}"
        y.wait("tt", f1wfree)
        if kind == "sh":
            h0 = e * 1024
            for c in range(8):
                y.op("dma_start", out=w1T[:, c * 1024:(c + 1) * 1024],
                     in_=wscr["sw1"][h0:h0 + 1024, c * 128:(c + 1) * 128],
                     transpose=True, incs=[ctx.inc(wld, 16)])
                y.op("dma_start", out=w3T[:, c * 1024:(c + 1) * 1024],
                     in_=wscr["sw3"][h0:h0 + 1024, c * 128:(c + 1) * 128],
                     transpose=True, incs=[ctx.inc(wld, 16)])
                y.op("dma_start", out=w2T[:, c * 1024:(c + 1) * 1024],
                     in_=wscr["sw2"][:, h0 + c * 128:h0 + c * 128 + 128],
                     transpose=True, incs=[ctx.inc(wld, 16)])
        else:
            for c in range(8):
                y.op("dma_start", out=w1T[:, c * 1024:(c + 1) * 1024],
                     in_=wscr["moe1_w1"][e][:, c * 128:(c + 1) * 128],
                     transpose=True, incs=[ctx.inc(wld, 16)])
                y.op("dma_start", out=w3T[:, c * 1024:(c + 1) * 1024],
                     in_=wscr["moe1_w3"][e][:, c * 128:(c + 1) * 128],
                     transpose=True, incs=[ctx.inc(wld, 16)])
                y.op("dma_start", out=w2T[:, c * 1024:(c + 1) * 1024],
                     in_=wscr["moe1_w2"][e][:, c * 128:(c + 1) * 128],
                     transpose=True, incs=[ctx.inc(wld, 16)])
        t.wait(wld, ctx.val(wld))
        for tb in range(NB):
            t0 = tb * TBLK
            it = vi * NB + tb
            par = it % 2
            xln16c = xln2[par]
            cwrow = cwrow2[0:1, par * 512:par * 512 + 512]
            while nld1[0] <= it + 1 and nld1[0] < NIT1:
                _f1_prefetch()
            lds = f"sl{14 + par}"
            t.wait(lds, ldtgt1[it])
            v.wait(lds, ldtgt1[it])
            for hc in range(8):
                pa, pb_ = ps[2 * (hc % 2)], ps[2 * (hc % 2) + 1]
                if hc >= 2:
                    t.wait("vv", st["_f1hv"][hc % 2])
                for c in range(8):
                    t.op("matmul", pa[:, :], w1T[:, c * 1024 + hc * 128:c * 1024 + hc * 128 + 128],
                         xln16c[c], start=(c == 0), stop=(c == 7),
                         incs=([ctx.inc("tt")] if c == 7 else []))
                s.wait("tt", ctx.val("tt"))
                for c in range(8):
                    t.op("matmul", pb_[:, :], w3T[:, c * 1024 + hc * 128:c * 1024 + hc * 128 + 128],
                         xln16c[c], start=(c == 0), stop=(c == 7),
                         incs=([ctx.inc("tt")] if c == 7 else []))
                if hc >= 2:
                    s.wait("vv", st["_f1hv"][hc % 2])
                s.op("activation", hs16[hc % 2][:, :], pa[:, :], AF.Silu,
                     incs=[ctx.inc("ss")])
                v.wait("tt", ctx.val("tt"))
                v.wait("ss", ctx.val("ss"))
                v.op("tensor_tensor", hact[:, hc * 512:(hc + 1) * 512],
                     hs16[hc % 2][:, :], pb_[:, :], ALU.mult, incs=[ctx.inc("vv")])
                st["_f1hv"][hc % 2] = ctx.val("vv")
            t.wait("vv", ctx.val("vv"))
            if kind == "e":
                t.op("matmul", ps[6][:, :], CONST[0:1, cp["ones"]:cp["ones"] + 128],
                     cwrow[0:1, :], start=True, stop=True, incs=[ctx.inc("tt")])
                v.wait("tt", ctx.val("tt"))
                v.op("tensor_copy", cwb[:, :], ps[6][:, :], incs=[ctx.inc("vv")])
                cwv = ctx.val("vv")
                t.wait("vv", cwv)
            f1xcons[par] = ctx.val("tt")
            if vi > 0:
                g.wait(ACC[0], acc_done[tb][0])
                g.wait(ACC[1], acc_done[tb][1])
            for dc in range(8):
                ob = 4 + (dc % 2)
                if dc >= 2:
                    t.wait("vv", st["_f1ov"][dc % 2])
                for hc in range(8):
                    t.op("matmul", ps[ob][:, :],
                         w2T[:, hc * 1024 + dc * 128:hc * 1024 + dc * 128 + 128],
                         hact[:, hc * 512:(hc + 1) * 512],
                         start=(hc == 0), stop=(hc == 7),
                         incs=([ctx.inc("tt")] if hc == 7 else []))
                v.wait("tt", ctx.val("tt"))
                if dc >= 2:
                    v.wait(ACC[dc % 2], st["_f1adma"][dc % 2])
                if kind == "e":
                    v.op("tensor_tensor", contrib[dc % 2], cwb[:, :], ps[ob][:, :],
                         ALU.mult, incs=[ctx.inc("vv")])
                else:
                    v.op("tensor_copy", contrib[dc % 2], ps[ob][:, :],
                         incs=[ctx.inc("vv")])
                st["_f1ov"][dc % 2] = ctx.val("vv")
                g.wait("vv", ctx.val("vv"))
                g.op("dma_start", out=fm_ap(st["R0"], t0, TBLK, dc, 1),
                     in_=contrib[dc % 2].rearrange("p (c t) -> p c t", c=1),
                     accum_op=ALU.add, incs=[ctx.inc(ACC[dc % 2], 16)])
                st["_f1adma"][dc % 2] = ctx.val(ACC[dc % 2])
            acc_done[tb] = (ctx.val(ACC[0]), ctx.val(ACC[1]))
        t.op("nop", incs=[ctx.inc("tt")])
        f1wfree = ctx.val("tt")
    st["moe1_done"] = (ctx.val(ACC[0]), ctx.val(ACC[1]))
    WB.pop(); AFA.pop()


def stage_final_out(ctx, st):
    T, NT = st["T"], st["NT"]
    t, v, g, y = ctx.t, ctx.v, ctx.g, ctx.y
    AFA, CONST, ps = st["AF"], st["CONST"], st["ps"]
    id32 = CONST[:, 0:128]
    AFA.push()
    rin = [AFA.alloc(1024, "fo_in0"), AFA.alloc(1024, "fo_in1")]
    rtm = [AFA.alloc(1024, "fo_tm0"), AFA.alloc(1024, "fo_tm1")]
    ev_t = {0: 0, 1: 0}
    stg_t = {0: 0, 1: 0}
    m1a, m1b = st["moe1_done"]
    g.wait("sl20", m1a)
    g.wait("sl21", m1b)
    for i in range(NT):
        b = i % 2
        lds, sts = f"sl{8 + b}", f"sl{10 + b}"
        if i >= 2:
            g.wait("tt", stg_t[b])
        g.op("dma_start", out=rin[b].rearrange("p (c t) -> p c t", c=8),
             in_=fm_ap(st["R0"], i * 128, 128), incs=[ctx.inc(lds, 16)])
        t.wait(lds, ctx.val(lds))
        if i >= 2:
            t.wait("vv", ev_t[b])
        for c in range(8):
            bank = ps[c // 4 + 2 * b + 4]
            t.op("matmul", bank[:, (c % 4) * 128:(c % 4) * 128 + 128],
                 rin[b].rearrange("p (c n) -> p c n", c=8)[:, c, :], id32,
                 start=True, stop=True)
        t.op("nop", incs=[ctx.inc("tt")])
        stg_t[b] = ctx.val("tt")
        v.wait("tt", stg_t[b])
        if i >= 2:
            v.wait(sts, ctx.val(sts))
        v.op("tensor_copy", rtm[b][:, 0:512], ps[4 + 2 * b][:, :])
        v.op("tensor_copy", rtm[b][:, 512:1024], ps[5 + 2 * b][:, :], incs=[ctx.inc("vv")])
        ev_t[b] = ctx.val("vv")
        y.wait("vv", ev_t[b])
        y.op("dma_start", out=st["y"][i * 128:(i + 1) * 128, :], in_=rtm[b],
             incs=[ctx.inc(sts, 16)])
    AFA.pop()


def _mk_nc_full(T):
    return build_program(T, [stage_consts, stage_repack, stage_xt, stage_l0_attn,
                             stage_l0_moe, stage_l1_attn, stage_l1_ffn,
                             stage_final_out])


def stage_out_copy(ctx, st):
    nc, T, NT = ctx.nc, st["T"], st["NT"]
    g, t, v, y = ctx.g, ctx.t, ctx.v, ctx.y
    AFA, CONST, ps = st["AF"], st["CONST"], st["ps"]
    id32 = CONST[:, 0:128]
    AFA.push()
    rin = [AFA.alloc(1024, "oc_in0"), AFA.alloc(1024, "oc_in1")]
    rtm = [AFA.alloc(1024, "oc_tm0"), AFA.alloc(1024, "oc_tm1")]
    ev_t = {0: 0, 1: 0}
    stg_t = {0: 0, 1: 0}
    g.wait(f"sl6", st["xt_done"][0])
    g.wait(f"sl7", st["xt_done"][1])
    for i in range(NT):
        b = i % 2
        lds, sts = f"sl{8 + b}", f"sl{10 + b}"
        if i >= 2:
            g.wait("tt", stg_t[b])
        g.op("dma_start", out=rin[b].rearrange("p (c t) -> p c t", c=8),
             in_=fm_ap(st["R0"], i * 128, 128), incs=[ctx.inc(lds, 16)])
        t.wait(lds, ctx.val(lds))
        if i >= 2:
            t.wait("vv", ev_t[b])
        for c in range(8):
            bank = ps[c // 4 + 2 * b + 4]
            t.op("matmul", bank[:, (c % 4) * 128:(c % 4) * 128 + 128],
                 rin[b].rearrange("p (c n) -> p c n", c=8)[:, c, :], id32,
                 start=True, stop=True)
        t.op("nop", incs=[ctx.inc("tt")])
        stg_t[b] = ctx.val("tt")
        v.wait("tt", stg_t[b])
        if i >= 2:
            v.wait(sts, ctx.val(sts))
        v.op("tensor_copy", rtm[b][:, 0:512], ps[4 + 2 * b][:, :])
        v.op("tensor_copy", rtm[b][:, 512:1024], ps[5 + 2 * b][:, :], incs=[ctx.inc("vv")])
        ev_t[b] = ctx.val("vv")
        y.wait("vv", ev_t[b])
        y.op("dma_start", out=st["y"][i * 128:(i + 1) * 128, :], in_=rtm[b],
             incs=[ctx.inc(sts, 16)])
    AFA.pop()


def kernel(**inputs):
    T = 8192
    cst = host_constants()
    key = {
        "x": None,
        "q_w": "L0_attn_q_w", "k_w": "L0_attn_k_w", "v_w": "L0_attn_v_w",
        "o_w": "L0_attn_o_w", "cs_w1": "L0_attn_cs_w1", "cs_w2": "L0_attn_cs_w2",
        "moe0_w1": "L0_ffn_W1", "moe0_w2": "L0_ffn_W2", "moe0_w3": "L0_ffn_W3",
        "moe0_wg": "L0_ffn_Wg",
        "win": "L1_attn_Win", "wg": "L1_attn_Wg", "wdt": "L1_attn_Wdt",
        "wout": "L1_attn_Wout", "wb": "L1_attn_Wb", "wc": "L1_attn_Wc",
        "moe1_w1": "L1_ffn_w1", "moe1_w2": "L1_ffn_w2", "moe1_w3": "L1_ffn_w3",
        "moe1_wg": "L1_ffn_gate_w",
        "sw1": "L1_ffn_sw1", "sw3": "L1_ffn_sw3", "sw2": "L1_ffn_sw2",
        "cs_b1": "L0_attn_cs_b1", "cs_b2": "L0_attn_cs_b2",
        "decay": "L0_attn_decay_params",
        "b_in": "L1_attn_b_in", "bg": "L1_attn_bg", "bout": "L1_attn_bout",
        "bdt": "L1_attn_bdt", "convw": "L1_attn_convw", "convb": "L1_attn_convb",
        "A": "L1_attn_A", "bb": "L1_attn_bb", "bc": "L1_attn_bc",
        "gate_bias": "L1_ffn_gate_bias",
    }
    x = np.ascontiguousarray(inputs["L0_attn_x"])
    base = {}
    for k, src in key.items():
        if src is not None:
            base[k] = np.ascontiguousarray(inputs[src], dtype=np.float32)
    base.update(cst)
    nc = _mk_nc_full(T)
    in_maps = []
    for b in range(8):
        m = dict(base)
        m["x"] = np.ascontiguousarray(x[b])
        in_maps.append(m)
    res = run_bass_kernel_spmd(nc, in_maps, core_ids=list(range(8)))
    out = np.stack([r["y"] for r in res.results], axis=0)
    return out


# version 06 (final)
# speedup vs baseline: 1.1575x; 1.0598x over previous
import math
import numpy as np
import contextlib

import concourse.bass as bass
import concourse.mybir as mybir
from concourse.bass_utils import run_bass_kernel_spmd

F32 = mybir.dt.float32
BF16 = mybir.dt.bfloat16
AF = mybir.ActivationFunctionType
ALU = mybir.AluOpType
AX = mybir.AxisListType

D = 1024
NCHUNK = 8
H = 16
E = 64
CS_DH = 256
GLA_C = 64
MOE0_H = 2816
MOE0_E = 8
SSM_N = 16
MOE1_H = 1024
MOE1_E = 8
SHARED_H = 2048
TBLK = 512


class Q:

    def __init__(self, name):
        self.name = name
        self.ops = []

    def wait(self, sem, val):
        if val > 0:
            self.ops.append(("wait", sem, val))

    def op(self, meth, *args, incs=(), **kw):
        self.ops.append(("op", meth, args, kw, tuple(incs)))

    def fold_nops(self):
        out = []
        for o in self.ops:
            if o[0] == "op" and o[1] == "nop" and o[4]:
                for k in range(len(out) - 1, -1, -1):
                    if out[k][0] == "op" and out[k][1] != "dma_start":
                        merged = {}
                        for s_, n_ in out[k][4] + o[4]:
                            merged[s_] = merged.get(s_, 0) + n_
                        assert len(merged) <= 1, f"multi-sem inc on PE op: {merged}"
                        out[k] = (out[k][0], out[k][1], out[k][2], out[k][3],
                                  tuple(merged.items()))
                        break
                else:
                    out.append(o)
                continue
            out.append(o)
        self.ops = out

    def replay(self, eng, sems):
        for o in self.ops:
            if o[0] == "wait":
                eng.wait_ge(sems[o[1]], o[2])
            else:
                _, meth, args, kw, incs = o
                r = getattr(eng, meth)(*args, **kw)
                for s, n in incs:
                    r = r.then_inc(sems[s], n)


class Ctx:

    def __init__(self, nc, T):
        self.nc = nc
        self.T = T
        self.t = Q("tensor")
        self.v = Q("vector")
        self.s = Q("scalar")
        self.g = Q("gpsimd")
        self.y = Q("sync")
        self.sem = {}
        self.semset = []
        self.dma_sems = set()

    def new_sem(self, name):
        assert name not in self.sem
        self.sem[name] = 0
        self.semset.append(name)
        return name

    def inc(self, name, n=1):
        self.sem[name] += n
        if n >= 16:
            self.dma_sems.add(name)
        return (name, n)

    def val(self, name):
        return self.sem[name]

    def queues(self):
        return [("tensor", self.t), ("vector", self.v), ("scalar", self.s),
                ("gpsimd", self.g), ("sync", self.y)]


class Arena:

    def __init__(self, tensor, n_elems, esize):
        self.t = tensor
        self.n = n_elems
        self.esize = esize
        self.off = 0
        self.marks = []

    def alloc(self, n_elems, tag=""):
        assert self.off + n_elems <= self.n, (
            f"arena overflow: {tag} wants {n_elems} at {self.off}/{self.n}")
        o = self.off
        self.off += n_elems
        return self.t[:, o:o + n_elems]

    def push(self):
        self.marks.append(self.off)

    def pop(self):
        self.off = self.marks.pop()


def fm_ap(dram3d, t0, tl, c0=0, cn=NCHUNK):
    return dram3d[c0:c0 + cn, :, t0:t0 + tl].rearrange("c p t -> p c t")


def silu_np(x):
    return x / (1 + np.exp(-x))


def host_constants():
    cst = {}
    cst["c_id32"] = np.eye(128, dtype=np.float32)
    cst["c_id16"] = np.eye(128, dtype=np.float32)
    cst["c_tri"] = np.triu(np.ones((GLA_C, GLA_C), np.float32))
    cst["c_triu16"] = np.triu(np.ones((GLA_C, GLA_C), np.float32))
    sel = np.zeros((H, NCHUNK * 128), np.float32)
    for j in range(NCHUNK):
        sel[2 * j, j * 128:j * 128 + 64] = 1.0
        sel[2 * j + 1, j * 128 + 64:j * 128 + 128] = 1.0
    cst["c_sel"] = sel
    cst["c_ones"] = np.ones((1, 128), np.float32)
    return cst


def build_program(T, stages, core_dbg=False):
    nc = bass.Bass("TRN2", target_bir_lowering=False)
    NT = T // 128
    NB = T // TBLK

    io = {}

    def inp(name, shape, dtype=F32):
        io[name] = nc.dram_tensor(name, list(shape), dtype, kind="ExternalInput").ap()
        return io[name]

    x_in = inp("x", (T, D))
    w = {}
    for n, shape in [
            ("q_w", (D, D)), ("k_w", (D, D)), ("v_w", (D, D)), ("o_w", (D, D)),
            ("cs_w1", (CS_DH, D)), ("cs_w2", (H, CS_DH)),
            ("moe0_w1", (MOE0_E, MOE0_H, D)), ("moe0_w2", (MOE0_E, D, MOE0_H)),
            ("moe0_w3", (MOE0_E, MOE0_H, D)), ("moe0_wg", (MOE0_E, D)),
            ("win", (D, D)), ("wg", (D, D)), ("wdt", (D, D)), ("wout", (D, D)),
            ("wb", (D, SSM_N)), ("wc", (D, SSM_N)),
            ("moe1_w1", (MOE1_E, MOE1_H, D)), ("moe1_w2", (MOE1_E, D, MOE1_H)),
            ("moe1_w3", (MOE1_E, MOE1_H, D)), ("moe1_wg", (MOE1_E, D)),
            ("sw1", (SHARED_H, D)), ("sw3", (SHARED_H, D)), ("sw2", (D, SHARED_H)),
    ]:
        w[n] = inp(n, shape)
    vecs = {}
    for n, shape in [
            ("cs_b1", (CS_DH,)), ("cs_b2", (H,)), ("decay", (H, E)),
            ("b_in", (D,)), ("bg", (D,)), ("bout", (D,)), ("bdt", (D,)),
            ("convw", (D, 3)), ("convb", (D,)), ("A", (D, SSM_N)),
            ("bb", (SSM_N,)), ("bc", (SSM_N,)), ("gate_bias", (MOE1_E,)),
    ]:
        vecs[n] = inp(n, shape)
    for n, shape in [("c_id32", (128, 128)), ("c_id16", (128, 128)),
                     ("c_tri", (GLA_C, GLA_C)), ("c_triu16", (GLA_C, GLA_C)),
                     ("c_sel", (H, NCHUNK * 128)), ("c_ones", (1, 128))]:
        inp(n, shape)

    y_out = nc.dram_tensor("y", [T, D], F32, kind="ExternalOutput").ap()

    def scr(name, shape, dtype):
        return nc.dram_tensor(name, list(shape), dtype, kind="Internal").ap()

    R0 = scr("R0", (NCHUNK, 128, T), F32)
    R1 = scr("R1", (NCHUNK, 128, T), F32)
    XLN = scr("XLN", (NCHUNK, 128, T), BF16)
    ZSC = scr("ZSC", (NCHUNK, 128, T), BF16)
    XSS = scr("XSS", (NCHUNK, 128, T), F32)
    XIN = scr("XIN", (NCHUNK, 128, T), F32)
    GSC = scr("GSC", (MOE0_E, T), F32)
    GSC1 = scr("GSC1", (MOE1_E, T), F32)
    BMR = scr("BMR", (2, SSM_N, T), F32)
    wscr = {}
    for n in ["q_w", "k_w", "v_w", "o_w", "cs_w1", "cs_w2", "moe0_w1",
              "moe0_w2", "moe0_w3", "moe1_w1", "moe1_w2", "moe1_w3",
              "sw1", "sw3", "sw2", "moe0_wg", "moe1_wg"]:
        wscr[n] = scr("S_" + n, io[n].shape, BF16)

    es = contextlib.ExitStack()
    prog = {}
    with es:
        WB_N = 70656
        AF_N = 13056
        WB = Arena(es.enter_context(nc.sbuf_tensor("WB", [128, WB_N], BF16)), WB_N, 2)
        AFA = Arena(es.enter_context(nc.sbuf_tensor("AFA", [128, AF_N], F32)), AF_N, 4)
        CONST = es.enter_context(nc.sbuf_tensor("CONST", [128, 2048], F32))
        CONST16 = es.enter_context(nc.sbuf_tensor("CONST16", [128, 512], BF16))
        ps = [es.enter_context(nc.psum_tensor(f"ps{i}", [128, 512], F32))
              for i in range(8)]

        ctx = Ctx(nc, T)
        for s in (["gld", "hld", "sst", "cst", "tt", "vv", "ss", "pp", "fin"]
                  + [f"sl{i}" for i in range(24)]):
            ctx.new_sem(s)
        sems = {}
        for s in ctx.semset:
            sems[s] = es.enter_context(nc.semaphore(s))

        st = {"io": io, "w": w, "vecs": vecs, "wscr": wscr, "R0": R0, "R1": R1,
              "XLN": XLN, "ZSC": ZSC, "XSS": XSS, "XIN": XIN, "BMR": BMR,
              "GSC": GSC, "GSC1": GSC1,
              "WB": WB, "AF": AFA, "CONST": CONST, "CONST16": CONST16,
              "ps": ps, "y": y_out, "x": x_in,
              "T": T, "NT": NT, "NB": NB}

        for stage_fn in stages:
            stage_fn(ctx, st)

        for name in ctx.semset:
            if name in ctx.dma_sems and ctx.val(name) > 0:
                ctx.y.wait(name, ctx.val(name))

        ctx.t.fold_nops()
        ctx.v.fold_nops()
        ctx.s.fold_nops()
        ctx.g.fold_nops()
        with nc.Block() as block:
            @block.tensor
            def _(eng):
                ctx.t.replay(eng, sems)

            @block.vector
            def _(eng):
                ctx.v.replay(eng, sems)

            @block.scalar
            def _(eng):
                ctx.s.replay(eng, sems)

            @block.gpsimd
            def _(eng):
                ctx.g.replay(eng, sems)

            @block.sync
            def _(eng):
                ctx.y.replay(eng, sems)

    return nc


def stage_consts(ctx, st):
    io, vecs = st["io"], st["vecs"]
    nc = ctx.nc
    g, v, s, t, y = ctx.g, ctx.v, ctx.s, ctx.t, ctx.y
    CONST, CONST16 = st["CONST"], st["CONST16"]

    st["cpos"] = dict(id32=0, logbase=128, decay_stage=136, cs_b1=144, b_in=152,
                      bg=160, bout=168, bdt=176, convb=184, convw=192, A=216,
                      cs_b2=344, bb=345, bc=346, gate_bias=347, tri=348,
                      sel=412, ones=1440, ones_col=1568, gbrow=1570, gbias_b=1580)
    cp = st["cpos"]

    g.op("dma_start", out=CONST[:, 0:128], in_=io["c_id32"], incs=[ctx.inc("gld", 16)])
    g.op("dma_start", out=CONST16[:, 0:128], in_=io["c_id16"], incs=[ctx.inc("gld", 16)])
    g.op("dma_start", out=CONST16[0:64, 128:192], in_=io["c_triu16"], incs=[ctx.inc("gld", 16)])
    g.op("memset", CONST16[:, 192:320], 0.0)
    g.op("dma_start", out=CONST[0:64, cp["tri"]:cp["tri"] + 64], in_=io["c_tri"],
         incs=[ctx.inc("gld", 16)])
    g.op("dma_start", out=CONST[0:16, cp["sel"]:cp["sel"] + NCHUNK * 128].rearrange("p (c n) -> p c n", c=NCHUNK),
         in_=io["c_sel"].rearrange("p (c n) -> p c n", c=NCHUNK), incs=[ctx.inc("gld", 16)])
    g.op("dma_start", out=CONST[0:1, cp["ones"]:cp["ones"] + 128], in_=io["c_ones"],
         incs=[ctx.inc("gld", 16)])
    g.op("memset", CONST[:, cp["ones_col"]:cp["ones_col"] + 1], 1.0)
    for name, pos in [("b_in", "b_in"), ("bg", "bg"), ("bout", "bout"),
                      ("bdt", "bdt"), ("convb", "convb")]:
        g.op("dma_start", out=CONST[:, cp[pos]:cp[pos] + 8],
             in_=vecs[name].rearrange("(c p) -> p c", p=128),
             allow_slow_non_contiguous=True, incs=[ctx.inc("gld", 16)])
    g.op("dma_start", out=CONST[:, cp["cs_b1"]:cp["cs_b1"] + 2],
         in_=vecs["cs_b1"].rearrange("(c p) -> p c", p=128),
         allow_slow_non_contiguous=True, incs=[ctx.inc("gld", 16)])
    g.op("dma_start", out=CONST[:, cp["convw"]:cp["convw"] + 24].rearrange("p (c k) -> p c k", c=NCHUNK),
         in_=vecs["convw"].rearrange("(c p) k -> p c k", p=128), incs=[ctx.inc("gld", 16)])
    g.op("dma_start", out=CONST[:, cp["A"]:cp["A"] + 128].rearrange("p (c k) -> p c k", c=NCHUNK),
         in_=vecs["A"].rearrange("(c p) k -> p c k", p=128), incs=[ctx.inc("gld", 16)])
    g.op("dma_start", out=CONST[0:16, cp["cs_b2"]:cp["cs_b2"] + 1], in_=vecs["cs_b2"][:, None],
         incs=[ctx.inc("gld", 16)])
    g.op("dma_start", out=CONST[0:16, cp["bb"]:cp["bb"] + 1], in_=vecs["bb"][:, None],
         incs=[ctx.inc("gld", 16)])
    g.op("dma_start", out=CONST[0:16, cp["bc"]:cp["bc"] + 1], in_=vecs["bc"][:, None],
         incs=[ctx.inc("gld", 16)])
    g.op("dma_start", out=CONST[0:8, cp["gate_bias"]:cp["gate_bias"] + 1],
         in_=vecs["gate_bias"][:, None], incs=[ctx.inc("gld", 16)])
    g.op("dma_start", out=CONST[0:1, cp["gbrow"]:cp["gbrow"] + 8],
         in_=vecs["gate_bias"][None, :], incs=[ctx.inc("gld", 16)])
    g.op("dma_start", out=CONST[:, cp["decay_stage"]:cp["decay_stage"] + 8],
         in_=vecs["decay"].rearrange("h e -> (h e)").rearrange("(c p) -> p c", p=128),
         allow_slow_non_contiguous=True, incs=[ctx.inc("gld", 16)])
    gld_all = ctx.val("gld")

    v.wait("gld", gld_all)
    dstage = CONST[:, cp["decay_stage"]:cp["decay_stage"] + 8]
    lb = CONST[:, cp["logbase"]:cp["logbase"] + 8]
    s.wait("gld", gld_all)
    s.op("activation", lb, dstage, AF.Abs)
    s.op("activation", lb, lb, AF.Exp, scale=-1.0)
    s.op("activation", lb, lb, AF.Ln, bias=1.0, incs=[ctx.inc("ss")])
    v.wait("ss", ctx.val("ss"))
    v.op("tensor_scalar", dstage, dstage, -1.0, 0.0, ALU.mult, ALU.max)
    v.op("drain")
    v.op("tensor_tensor", lb, lb, dstage, ALU.add)
    v.op("drain")
    v.op("tensor_scalar", lb, lb, -1.0, None, ALU.mult, incs=[ctx.inc("vv")])
    t.wait("gld", gld_all)
    t.op("matmul", st["ps"][0][:, 0:8], CONST[0:1, cp["ones"]:cp["ones"] + 128],
         CONST[0:1, cp["gbrow"]:cp["gbrow"] + 8], start=True, stop=True,
         incs=[ctx.inc("tt")])
    v.wait("tt", ctx.val("tt"))
    v.op("tensor_copy", CONST[:, cp["gbias_b"]:cp["gbias_b"] + 8], st["ps"][0][:, 0:8],
         incs=[ctx.inc("vv")])
    st["_gbias_slice"] = CONST[:, cp["gbias_b"]:cp["gbias_b"] + 8]
    st["consts_done"] = (ctx.val("vv"), gld_all)


def _repack_one(ctx, st, src_ap, dst_ap, R, C, stage_tiles):
    g, y = ctx.g, ctx.y
    a = max(1, 2048 // C)
    while R % (128 * a) != 0:
        a //= 2
    assert a >= 1 and R % (128 * a) == 0
    nslab = R // (128 * a)
    for i in range(nslab):
        k = st["_rp_i"] % 2
        st["_rp_i"] += 1
        lds, sts = f"sl{2 * k}", f"sl{2 * k + 1}"
        g.wait(sts, ctx.val(sts))
        g.op("dma_start",
             out=stage_tiles[k][:, 0:a * C],
             in_=src_ap[i * 128 * a:(i + 1) * 128 * a, :].rearrange("(p a) c -> p (a c)", a=a),
             max_dma_last_dim=4096, incs=[ctx.inc(lds, 16)])
        y.wait(lds, ctx.val(lds))
        y.op("dma_start",
             out=dst_ap[i * 128 * a:(i + 1) * 128 * a, :].rearrange("(p a) c -> p (a c)", a=a),
             in_=stage_tiles[k][:, 0:a * C], incs=[ctx.inc(sts, 16)])


def stage_repack(ctx, st):
    WB = st["WB"]
    WB.push()
    stage_tiles = [WB.alloc(4096, "rp0"), WB.alloc(4096, "rp1")]
    st["_rp_i"] = 0
    io, wscr = st["io"], st["wscr"]
    jobs = []
    for n in ["q_w", "k_w", "v_w", "o_w", "cs_w1", "sw1", "sw3", "sw2"]:
        shp = io[n].shape
        jobs.append((io[n], wscr[n], shp[0], shp[1]))
    for n in ["moe0_w1", "moe0_w2", "moe0_w3", "moe1_w1", "moe1_w2", "moe1_w3"]:
        shp = io[n].shape
        for e in range(shp[0]):
            jobs.append((io[n][e], wscr[n][e], shp[1], shp[2]))
    for s_, d_, R, C in jobs:
        _repack_one(ctx, st, s_, d_, R, C, stage_tiles)
    g, y = ctx.g, ctx.y
    for n in ["cs_w2", "moe0_wg", "moe1_wg"]:
        shp = io[n].shape
        k = st["_rp_i"] % 2
        st["_rp_i"] += 1
        lds, sts = f"sl{2 * k}", f"sl{2 * k + 1}"
        g.wait(sts, ctx.val(sts))
        g.op("dma_start", out=stage_tiles[k][0:shp[0], 0:shp[1]], in_=io[n],
             max_dma_last_dim=4096, incs=[ctx.inc(lds, 16)])
        y.wait(lds, ctx.val(lds))
        y.op("dma_start", out=wscr[n], in_=stage_tiles[k][0:shp[0], 0:shp[1]],
             incs=[ctx.inc(sts, 16)])
    st["repack_done"] = (ctx.val("sl1"), ctx.val("sl3"))
    WB.pop()


def stage_xt(ctx, st):
    nc, T, NT = ctx.nc, st["T"], st["NT"]
    g, t, v, y = ctx.g, ctx.t, ctx.v, ctx.y
    AFA, CONST, ps = st["AF"], st["CONST"], st["ps"]
    cp = st["cpos"]
    id32 = CONST[:, 0:128]
    AFA.push()
    xin = [AFA.alloc(1024, "xt_in0"), AFA.alloc(1024, "xt_in1")]
    xfm = [AFA.alloc(1024, "xt_fm0"), AFA.alloc(1024, "xt_fm1")]
    ev_t = {0: 0, 1: 0}
    stg_t = {0: 0, 1: 0}
    cw = st["consts_done"]
    t.wait("gld", cw[1])
    for i in range(NT):
        b = i % 2
        lds, sts = f"sl{4 + b}", f"sl{6 + b}"
        if i >= 2:
            g.wait("tt", stg_t[b])
        g.op("dma_start", out=xin[b], in_=st["x"][i * 128:(i + 1) * 128, :],
             incs=[ctx.inc(lds, 16)])
        t.wait(lds, ctx.val(lds))
        if i >= 2:
            t.wait("vv", ev_t[b])
        for c in range(8):
            bank = ps[c // 4 + 2 * b]
            t.op("matmul", bank[:, (c % 4) * 128:(c % 4) * 128 + 128],
                 xin[b].rearrange("p (c n) -> p c n", c=8)[:, c, :], id32,
                 start=True, stop=True)
        t.op("nop", incs=[ctx.inc("tt")])
        stg_t[b] = ctx.val("tt")
        v.wait("tt", stg_t[b])
        if i >= 2:
            v.wait(sts, ctx.val(sts))
        v.op("tensor_copy", xfm[b][:, 0:512], ps[2 * b][:, :])
        v.op("tensor_copy", xfm[b][:, 512:1024], ps[2 * b + 1][:, :], incs=[ctx.inc("vv")])
        ev_t[b] = ctx.val("vv")
        y.wait("vv", ev_t[b])
        y.op("dma_start", out=fm_ap(st["R0"], i * 128, 128),
             in_=xfm[b].rearrange("p (c t) -> p c t", c=8), incs=[ctx.inc(sts, 16)])
    AFA.pop()
    st["xt_done"] = (ctx.val("sl6"), ctx.val("sl7"))


def emit_ln(ctx, st, xf, xln16, bS, bQ, bR, bM, sq):
    t, v, s = ctx.t, ctx.v, ctx.s
    CONST, cp = st["CONST"], st["cpos"]
    ps = st["ps"]
    ones_col = CONST[:, cp["ones_col"]:cp["ones_col"] + 1]
    ones_row = CONST[0:1, cp["ones"]:cp["ones"] + 128]
    m_, q_ = st["_ln_rows"][0], st["_ln_rows"][1]
    r_ = sq[0][0:1, 0:512]
    for c in range(8):
        t.op("matmul", ps[bS][0:1, :], ones_col, xf[c],
             start=(c == 0), stop=(c == 7), incs=([ctx.inc("tt")] if c == 7 else []))
    sqmm = {}
    for c in range(8):
        if c >= 2:
            s.wait("tt", sqmm[c - 2])
        s.op("activation", sq[c % 2], xf[c], AF.Square, incs=[ctx.inc("ss")])
        t.wait("ss", ctx.val("ss"))
        t.op("matmul", ps[bQ][0:1, :], ones_col, sq[c % 2],
             start=(c == 0), stop=(c == 7), incs=[ctx.inc("tt")])
        sqmm[c] = ctx.val("tt")
    tstat = ctx.val("tt")
    v.wait("tt", tstat)
    v.op("tensor_scalar", m_, ps[bS][0:1, :], 1.0 / D, None, ALU.mult)
    v.op("tensor_scalar", q_, ps[bQ][0:1, :], 1.0 / D, None, ALU.mult)
    v.op("drain")
    v.op("tensor_tensor", r_, m_, m_, ALU.mult)
    v.op("drain")
    v.op("tensor_tensor", q_, q_, r_, ALU.subtract)
    v.op("drain")
    v.op("tensor_scalar", q_, q_, 1e-5, None, ALU.add, incs=[ctx.inc("vv")])
    s.wait("vv", ctx.val("vv"))
    s.op("activation", q_, q_, AF.Ln)
    s.op("drain")
    s.op("activation", q_, q_, AF.Exp, scale=-0.5, incs=[ctx.inc("ss")])
    v.wait("ss", ctx.val("ss"))
    v.op("tensor_tensor", m_, m_, q_, ALU.mult, incs=[ctx.inc("vv")])
    t.wait("vv", ctx.val("vv"))
    t.op("matmul", ps[bR][:, :], ones_row, q_, start=True, stop=True)
    t.op("matmul", ps[bM][:, :], ones_row, m_, start=True, stop=True,
         incs=[ctx.inc("tt")])
    v.wait("tt", ctx.val("tt"))
    for c in range(8):
        v.op("tensor_tensor", sq[c % 2], xf[c], ps[bR][:, :], ALU.mult)
        v.op("drain")
        v.op("tensor_tensor", xln16[c], sq[c % 2], ps[bM][:, :], ALU.subtract,
             incs=([ctx.inc("vv")] if c == 7 else []))
    return ctx.val("vv")


L0_PHASES = 99
L0_SUB = 99
L0_KHT = 2


def stage_l0_attn(ctx, st):
    T, NB = st["T"], st["NB"]
    t, v, s, g, y = ctx.t, ctx.v, ctx.s, ctx.g, ctx.y
    WB, AFA, CONST, CONST16, ps = st["WB"], st["AF"], st["CONST"], st["CONST16"], st["ps"]
    cp = st["cpos"]
    wscr = st["wscr"]
    id32 = CONST[:, 0:128]
    id16 = CONST16[:, 0:128]
    triu16 = CONST16[0:64, 128:192]

    WB.push(); AFA.push()
    qkvoT = {n: WB.alloc(8192, n + "T") for n in ["q_w", "k_w", "v_w", "o_w"]}
    csw1T = WB.alloc(2048, "cs_w1T")
    csw2T = WB.alloc(32, "cs_w2T")
    y.wait("sl1", st["repack_done"][0])
    y.wait("sl3", st["repack_done"][1])
    for n, dst in qkvoT.items():
        for c in range(8):
            y.op("dma_start", out=dst[:, c * 1024:(c + 1) * 1024],
                 in_=wscr[n][:, c * 128:(c + 1) * 128], transpose=True,
                 incs=[ctx.inc("hld", 16)])
    for c in range(8):
        y.op("dma_start", out=csw1T[:, c * 256:(c + 1) * 256],
             in_=wscr["cs_w1"][:, c * 128:(c + 1) * 128], transpose=True,
             incs=[ctx.inc("hld", 16)])
    for c in range(2):
        y.op("dma_start", out=csw2T[:, c * 16:(c + 1) * 16],
             in_=wscr["cs_w2"][:, c * 128:(c + 1) * 128], transpose=True,
             incs=[ctx.inc("hld", 16)])
    wready = ctx.val("hld")

    xfb = AFA.alloc(4096, "l0x")
    xf = [xfb[:, c * 512:(c + 1) * 512] for c in range(8)]
    sq = [AFA.alloc(512, "l0sq0"), AFA.alloc(512, "l0sq1")]
    r3 = AFA.alloc(1024, "l0r3")
    st["_ln_rows"] = [r3[0:1, 0:512], r3[0:1, 512:1024]]
    lam = AFA.alloc(4096, "l0lam")
    ltm = AFA.alloc(1024, "l0ltm")
    negd = AFA.alloc(8, "l0negd")
    posd = AFA.alloc(8, "l0posd")
    sd = AFA.alloc(64, "l0sd")
    lend = AFA.alloc(64, "l0lend")
    Sst = AFA.alloc(512, "l0S")
    kvt = AFA.alloc(1024, "l0kvt")
    lcb = sq[1]
    xln16 = [WB.alloc(512, f"l0xln{c}") for c in range(8)]
    qs = [WB.alloc(512, f"l0q{c}") for c in range(8)]
    ks = [WB.alloc(512, f"l0k{c}") for c in range(8)]
    kh = [WB.alloc(512, f"l0kh{c}") for c in range(8)]
    lam16 = [WB.alloc(512, f"l0g{c}") for c in range(8)]
    vtb = [WB.alloc(1024, "l0vt0"), WB.alloc(1024, "l0vt1")]
    aofmb = WB.alloc(4096, "l0ao")
    aofm = [aofmb[:, c * 512:(c + 1) * 512] for c in range(8)]
    cs116 = aofmb[:, 0:1024]
    S16 = WB.alloc(512, "l0S16")
    aoc = WB.alloc(512, "l0aoc")
    aoc2 = WB.alloc(512, "l0aoc2")
    khT = WB.alloc(1024, "l0khT")
    pb = WB.alloc(1024, "l0pb")
    ao = WB.alloc(1024, "l0aoh")

    g.op("memset", Sst[:, :], 0.0, incs=[ctx.inc("pp")])
    v.wait("pp", ctx.val("pp"))
    t.wait("hld", wready)
    t.wait("vv", st["consts_done"][0])
    t.wait("gld", st["consts_done"][1])
    s.wait("gld", st["consts_done"][1])
    xt0, xt1 = st["xt_done"]
    g.wait("sl6", xt0)
    g.wait("sl7", xt1)

    st["_l0aev"] = {0: 0, 1: 0}
    st["_l0vfree"] = {0: 0, 1: 0}
    st["_l0khv"] = 0
    xfree = 0
    for b in range(NB):
        t0 = b * TBLK
        lds = f"sl{14 + (b % 2)}"
        if b >= 1:
            g.wait("vv", xfree)
        g.op("dma_start", out=xfb.rearrange("p (c t) -> p c t", c=8),
             in_=fm_ap(st["R0"], t0, TBLK), incs=[ctx.inc(lds, 16)])
        for q_ in (t, v, s):
            q_.wait(lds, ctx.val(lds))
        for q_ in (v, s):
            q_.wait("sl16", ctx.val("sl16"))
            q_.wait("sl17", ctx.val("sl17"))
        lnv = emit_ln(ctx, st, xf, xln16, 0, 1, 2, 3, sq)
        t.wait("vv", lnv)
        for j in range(2):
            for c in range(8):
                t.op("matmul", ps[4][:, :],
                     csw1T[:, c * 256 + j * 128:c * 256 + j * 128 + 128],
                     xln16[c], start=(c == 0), stop=(c == 7),
                     incs=([ctx.inc("tt")] if c == 7 else []))
            s.wait("tt", ctx.val("tt"))
            s.op("activation", cs116[:, j * 512:(j + 1) * 512], ps[4][:, :], AF.Tanh,
                 bias=CONST[:, cp["cs_b1"] + j:cp["cs_b1"] + j + 1], incs=[ctx.inc("ss")])
            t.wait("ss", ctx.val("ss"))
        for j in range(2):
            t.op("matmul", ps[5][0:16, :], csw2T[:, j * 16:(j + 1) * 16],
                 cs116[:, j * 512:(j + 1) * 512], start=(j == 0), stop=(j == 1),
                 incs=([ctx.inc("tt")] if j == 1 else []))
        s.wait("tt", ctx.val("tt"))
        s.op("activation", lcb[0:16, :], ps[5][0:16, :], AF.Sigmoid,
             bias=CONST[0:16, cp["cs_b2"]:cp["cs_b2"] + 1])
        s.op("drain")
        s.op("activation", lcb[0:16, :], lcb[0:16, :], AF.Ln, scale=0.2, bias=1.0,
             incs=[ctx.inc("ss")])
        t.wait("ss", ctx.val("ss"))
        lamv = {}
        for j in range(8):
            bk = 4 + (j % 2)
            if j >= 2:
                t.wait("vv", lamv[j % 2])
            t.op("matmul", ps[bk][:, :],
                 CONST[0:16, cp["sel"] + j * 128:cp["sel"] + (j + 1) * 128],
                 lcb[0:16, :], start=True, stop=True, incs=[ctx.inc("tt")])
            v.wait("tt", ctx.val("tt"))
            v.op("tensor_scalar", lam[:, j * 512:(j + 1) * 512], ps[bk][:, :],
                 CONST[:, cp["logbase"] + j:cp["logbase"] + j + 1],
                 math.log(0.9995), ALU.add, ALU.min, incs=[ctx.inc("vv")])
            lamv[j % 2] = ctx.val("vv")
        t.wait("vv", ctx.val("vv"))
        if L0_PHASES < 1:
            continue
        for gch in range(8):
            for j in range(8):
                t.op("matmul", ps[4 + j // 4][0:64, (j % 4) * 128:(j % 4) * 128 + 128],
                     lam[:, j * 512 + gch * 64:j * 512 + gch * 64 + 64], id32,
                     start=True, stop=True)
            t.op("nop", incs=[ctx.inc("tt")])
            v.wait("tt", ctx.val("tt"))
            v.op("tensor_copy", ltm[0:64, 0:512], ps[4][0:64, :])
            v.op("tensor_copy", ltm[0:64, 512:1024], ps[5][0:64, :], incs=[ctx.inc("vv")])
            t.wait("vv", ctx.val("vv"))
            for half in range(2):
                t.op("matmul", ps[6 + half][0:64, 0:512],
                     CONST[0:64, cp["tri"]:cp["tri"] + 64],
                     ltm[0:64, half * 512:(half + 1) * 512], start=True, stop=True,
                     incs=([ctx.inc("tt")] if half == 1 else []))
            v.wait("tt", ctx.val("tt"))
            v.op("tensor_copy", ltm[0:64, 0:512], ps[6][0:64, 0:512])
            v.op("tensor_copy", ltm[0:64, 512:1024], ps[7][0:64, 0:512],
                 incs=[ctx.inc("vv")])
            t.wait("vv", ctx.val("vv"))
            for j in range(8):
                t.op("matmul", ps[4 + j // 4][:, (j % 4) * 64:(j % 4) * 64 + 64],
                     ltm[0:64, j * 128:(j + 1) * 128], id32[0:64, 0:64],
                     start=True, stop=True)
            t.op("nop", incs=[ctx.inc("tt")])
            s.wait("tt", ctx.val("tt"))
            for j in range(8):
                Lp = ps[4 + j // 4][:, (j % 4) * 64:(j % 4) * 64 + 64]
                Lcol = ps[4 + j // 4][:, (j % 4) * 64 + 63:(j % 4) * 64 + 64]
                dcol = gch * 8 + j
                s.op("activation", negd[:, j:j + 1], Lcol, AF.Copy, scale=-0.5, bias=0.0)
                s.op("activation", posd[:, j:j + 1], Lcol, AF.Copy, scale=0.5, bias=0.0)
                s.op("activation", sd[:, dcol:dcol + 1], Lcol, AF.Exp, scale=0.5)
                s.op("activation", lend[:, dcol:dcol + 1], Lcol, AF.Exp)
                s.op("drain")
                s.op("activation", lam16[j][:, gch * 64:gch * 64 + 64], Lp, AF.Exp,
                     bias=negd[:, j:j + 1])
                s.op("activation", kh[j][:, gch * 64:gch * 64 + 64], Lp, AF.Exp,
                     scale=-1.0, bias=posd[:, j:j + 1])
            s.op("nop", incs=[ctx.inc("ss")])
            t.wait("ss", ctx.val("ss"))
        if L0_PHASES < 2:
            continue
        for j in range(8):
            for c in range(8):
                t.op("matmul", ps[4][:, :],
                     qkvoT["q_w"][:, c * 1024 + j * 128:c * 1024 + j * 128 + 128],
                     xln16[c], start=(c == 0), stop=(c == 7),
                     incs=([ctx.inc("tt")] if c == 7 else []))
            v.wait("tt", ctx.val("tt"))
            v.op("tensor_tensor", qs[j], ps[4][:, :], lam16[j], ALU.mult,
                 incs=[ctx.inc("vv")])
            qdone = ctx.val("vv")
            for c in range(8):
                t.op("matmul", ps[5][:, :],
                     qkvoT["k_w"][:, c * 1024 + j * 128:c * 1024 + j * 128 + 128],
                     xln16[c], start=(c == 0), stop=(c == 7),
                     incs=([ctx.inc("tt")] if c == 7 else []))
            t.wait("vv", qdone)
            v.wait("tt", ctx.val("tt"))
            v.op("tensor_tensor", ks[j], ps[5][:, :], kh[j], ALU.mult)
            v.op("drain")
            for gch in range(8):
                dcol = gch * 8 + j
                v.op("tensor_scalar", kh[j][:, gch * 64:gch * 64 + 64],
                     ks[j][:, gch * 64:gch * 64 + 64], sd[:, dcol:dcol + 1],
                     None, ALU.mult)
            v.op("nop", incs=[ctx.inc("vv")])
            t.wait("vv", ctx.val("vv"))
        projv = ctx.val("vv")
        if L0_PHASES < 3:
            continue
        t.wait("vv", projv)
        for gch in range(8):
            cc = slice(gch * 64, gch * 64 + 64)
            vt = vtb[gch % 2]
            t.wait("vv", st["_l0khv"])
            for oh in range(2):
                for c in range(8):
                    t.op("matmul", ps[6 + oh][0:64, :],
                         xln16[c][:, cc],
                         qkvoT["v_w"][:, c * 1024 + oh * 512:c * 1024 + oh * 512 + 512],
                         start=(c == 0), stop=(c == 7),
                         incs=([ctx.inc("tt")] if c == 7 else []))
            if gch >= 2:
                v.wait("tt", st["_l0vfree"][gch % 2])
            v.wait("tt", ctx.val("tt"))
            v.op("tensor_copy", vt[0:64, 0:512], ps[6][0:64, :])
            v.op("tensor_copy", vt[0:64, 512:1024], ps[7][0:64, :], incs=[ctx.inc("vv")])
            t.wait("vv", ctx.val("vv"))
            if L0_SUB < 1:
                continue
            for j in range(8):
                dcol = gch * 8 + j
                v.op("tensor_scalar", S16[:, j * 64:(j + 1) * 64],
                     Sst[:, j * 64:(j + 1) * 64], sd[:, dcol:dcol + 1], None, ALU.mult,
                     incs=([ctx.inc("vv")] if j == 7 else []))
            s16v = ctx.val("vv")
            t.wait("vv", s16v)
            if L0_KHT >= 1:
                for j in range(8 if L0_KHT != 3 else 1):
                    t.op("matmul", ps[6 + j // 4][0:64, (j % 4) * 128:(j % 4) * 128 + 128],
                         kh[j][:, cc], id16, start=True, stop=True)
                t.op("nop", incs=[ctx.inc("tt")])
            if L0_KHT >= 2:
                v.wait("tt", ctx.val("tt"))
                v.op("tensor_copy", khT[0:64, 0:512], ps[6][0:64, :])
                v.op("tensor_copy", khT[0:64, 512:1024], ps[7][0:64, :],
                     incs=[ctx.inc("vv")])
            khv = ctx.val("vv")
            st["_l0khv"] = khv
            if L0_SUB < 2:
                continue
            horder = [h for h in range(16) if h % 2 == 0] + \
                     [h for h in range(16) if h % 2 == 1]
            for i_, h in enumerate(horder):
                j, hh = h // 2, h % 2
                rr = slice(hh * 64, hh * 64 + 64)
                t.op("matmul", ps[h // 8][0:64, (h % 8) * 64:(h % 8) * 64 + 64],
                     ks[j][rr, cc], qs[j][rr, cc], start=True, stop=True,
                     incs=([ctx.inc("tt")] if i_ == 15 else []))
            v.wait("tt", ctx.val("tt"))
            for h in range(16):
                v.op("tensor_tensor", pb[0:64, h * 64:(h + 1) * 64],
                     ps[h // 8][0:64, (h % 8) * 64:(h % 8) * 64 + 64], triu16,
                     ALU.mult, incs=([ctx.inc("vv")] if h == 15 else []))
            pbv = ctx.val("vv")
            if L0_SUB < 3:
                continue
            t.wait("vv", pbv)
            for h in range(16):
                t.op("matmul", ps[2 + h // 8][0:64, (h % 8) * 64:(h % 8) * 64 + 64],
                     pb[0:64, h * 64:(h + 1) * 64],
                     vt[0:64, h * 64:h * 64 + 64], start=True, stop=True)
            for i_, h in enumerate(horder):
                j, hh = h // 2, h % 2
                rr = slice(hh * 64, hh * 64 + 64)
                t.op("matmul", ps[6 + h // 8][0:64, (h % 8) * 64:(h % 8) * 64 + 64],
                     qs[j][rr, cc], S16[rr, j * 64:(j + 1) * 64], start=True, stop=True,
                     incs=([ctx.inc("tt")] if i_ == 15 else []))
            s.wait("tt", ctx.val("tt"))
            s.op("activation", aoc[0:64, :], ps[2][0:64, :], AF.Copy, bias=0.0)
            s.op("activation", aoc[0:64, 512:1024] if False else aoc2[0:64, :],
                 ps[3][0:64, :], AF.Copy, bias=0.0, incs=[ctx.inc("ss")])
            v.wait("ss", ctx.val("ss"))
            v.op("tensor_tensor", ao[0:64, 0:512], aoc[0:64, :], ps[6][0:64, :], ALU.add)
            v.op("tensor_tensor", ao[0:64, 512:1024], aoc2[0:64, :], ps[7][0:64, :],
                 ALU.add, incs=[ctx.inc("vv")])
            st["_l0aev"][0] = ctx.val("vv")
            if L0_SUB < 4:
                continue
            t.wait("vv", khv)
            for h in range(16):
                j, hh = h // 2, h % 2
                t.op("matmul", ps[4 + h // 8][0:64, (h % 8) * 64:(h % 8) * 64 + 64],
                     vt[0:64, h * 64:h * 64 + 64],
                     khT[0:64, j * 128 + hh * 64:j * 128 + hh * 64 + 64],
                     start=True, stop=True,
                     incs=([ctx.inc("tt")] if h == 15 else []))
            v.wait("tt", ctx.val("tt"))
            v.op("tensor_copy", kvt[0:64, 0:512], ps[4][0:64, 0:512])
            v.op("tensor_copy", kvt[0:64, 512:1024], ps[5][0:64, 0:512],
                 incs=[ctx.inc("vv")])
            t.wait("vv", ctx.val("vv"))
            if L0_SUB < 5:
                continue
            for j in range(8):
                t.op("matmul", ps[6][:, j * 64:(j + 1) * 64],
                     kvt[0:64, j * 128:(j + 1) * 128], id32[0:64, 0:64],
                     start=True, stop=True)
                t.op("matmul", ps[7][:, j * 64:(j + 1) * 64],
                     ao[0:64, j * 128:(j + 1) * 128], id16[0:64, 0:64],
                     start=True, stop=True)
            t.op("nop", incs=[ctx.inc("tt")])
            v.wait("tt", ctx.val("tt"))
            for j in range(8):
                dcol = gch * 8 + j
                v.op("scalar_tensor_tensor", Sst[:, j * 64:(j + 1) * 64],
                     Sst[:, j * 64:(j + 1) * 64], lend[:, dcol:dcol + 1],
                     ps[6][:, j * 64:(j + 1) * 64], ALU.mult, ALU.add)
                v.op("tensor_copy", aofm[j][:, cc], ps[7][:, j * 64:(j + 1) * 64])
            v.op("nop", incs=[ctx.inc("vv")])
            st["_l0vfree"][gch % 2] = ctx.val("tt")
            t.wait("vv", ctx.val("vv"))
        if L0_PHASES < 4:
            continue
        for oo in range(8):
            bk = oo % 2
            for c in range(8):
                t.op("matmul", ps[bk][:, :],
                     qkvoT["o_w"][:, c * 1024 + oo * 128:c * 1024 + oo * 128 + 128],
                     aofm[c], start=(c == 0), stop=(c == 7),
                     incs=([ctx.inc("tt")] if c == 7 else []))
            v.wait("tt", ctx.val("tt"))
            if oo >= 2:
                v.wait(f"sl{16 + (oo % 2)}", ctx.val(f"sl{16 + (oo % 2)}"))
            v.op("tensor_tensor", sq[oo % 2], xf[oo], ps[bk][:, :], ALU.add,
                 incs=[ctx.inc("vv")])
            evv = ctx.val("vv")
            t.wait("vv", evv)
            y.wait("vv", evv)
            y.op("dma_start", out=fm_ap(st["R1"], t0, TBLK, oo, 1),
                 in_=sq[oo % 2].rearrange("p (c t) -> p c t", c=1),
                 incs=[ctx.inc(f"sl{16 + (oo % 2)}", 16)])
        v.op("nop", incs=[ctx.inc("vv")])
        xfree = ctx.val("vv")
    st["l0_done"] = (ctx.val("sl16"), ctx.val("sl17"))
    WB.pop(); AFA.pop()


def emit_gates(ctx, st, xln16, gscr, wgT, t0, renorm, use_bias=False):
    t, v, s, g, y = ctx.t, ctx.v, ctx.s, ctx.g, ctx.y
    ps = st["ps"]
    gt = st["_gt"]
    grow = st["_grow"]
    gs512 = st["_gs512"]
    CONST, cp = st["CONST"], st["cpos"]
    id32 = CONST[:, 0:128]
    for c in range(8):
        t.op("matmul", ps[4][0:8, :], wgT[:, c * 8:(c + 1) * 8], xln16[c],
             start=(c == 0), stop=(c == 7), incs=([ctx.inc("tt")] if c == 7 else []))
    logv = ctx.val("tt")
    v.wait("tt", logv)
    v.op("tensor_copy", gs512[0:8, :], ps[4][0:8, :], incs=[ctx.inc("vv")])
    t.wait("vv", ctx.val("vv"))
    for ti in range(4):
        t.op("matmul", ps[5][:, ti * 8:(ti + 1) * 8],
             gs512[0:8, ti * 128:(ti + 1) * 128], id32[0:8, 0:8],
             start=True, stop=True)
    t.op("nop", incs=[ctx.inc("tt")])
    v.wait("tt", ctx.val("tt"))
    lg = gt[:, 0:32]
    v.op("tensor_copy", lg, ps[5][:, 0:32], incs=[ctx.inc("vv")])
    s.wait("vv", ctx.val("vv"))
    m1 = grow[:, 0:1]; sm = grow[:, 1:2]; m2 = grow[:, 2:3]; ws = grow[:, 3:4]
    sc = gt[:, 32:64]
    for ti in range(4):
        sl = lg[:, ti * 8:(ti + 1) * 8]
        so = sc[:, ti * 8:(ti + 1) * 8]
        v.op("tensor_reduce", m1, sl, AX.X, ALU.max)
        v.op("drain")
        v.op("tensor_scalar", so, sl, m1, None, ALU.subtract, incs=[ctx.inc("vv")])
        s.wait("vv", ctx.val("vv"))
        s.op("activation", so, so, AF.Exp, incs=[ctx.inc("ss")])
        v.wait("ss", ctx.val("ss"))
        v.op("tensor_reduce", sm, so, AX.X, ALU.add)
        v.op("drain")
        v.op("reciprocal", sm, sm)
        v.op("drain")
        v.op("tensor_scalar", so, so, sm, None, ALU.mult)
        v.op("drain")
        if use_bias:
            sb = lg[:, ti * 8:(ti + 1) * 8]
            v.op("tensor_tensor", sb, so, st["_gbias_slice"], ALU.add)
            v.op("drain")
            base = sb
        else:
            base = so
        v.op("tensor_reduce", m1, base, AX.X, ALU.max)
        v.op("drain")
        v.op("tensor_scalar", grow[:, 4:12], base, m1, None, ALU.is_lt)
        v.op("drain")
        v.op("tensor_tensor", grow[:, 4:12], base, grow[:, 4:12], ALU.mult)
        v.op("drain")
        v.op("tensor_reduce", m2, grow[:, 4:12], AX.X, ALU.max)
        v.op("drain")
        v.op("tensor_scalar", grow[:, 4:12], base, m2, None, ALU.is_ge)
        v.op("drain")
        v.op("tensor_tensor", so, so, grow[:, 4:12], ALU.mult)
        v.op("drain")
        if renorm:
            v.op("tensor_reduce", ws, so, AX.X, ALU.add)
            v.op("drain")
            v.op("tensor_scalar", ws, ws, 1e-20, None, ALU.add)
            v.op("drain")
            v.op("reciprocal", ws, ws)
            v.op("drain")
            v.op("tensor_scalar", so, so, ws, None, ALU.mult)
            v.op("drain")
    v.op("nop", incs=[ctx.inc("vv")])
    t.wait("vv", ctx.val("vv"))
    for ti in range(4):
        t.op("matmul", ps[6][0:8, ti * 128:(ti + 1) * 128],
             sc[:, ti * 8:(ti + 1) * 8], id32, start=True, stop=True)
    t.op("nop", incs=[ctx.inc("tt")])
    v.wait("tt", ctx.val("tt"))
    v.op("tensor_copy", gs512[0:8, :], ps[6][0:8, :], incs=[ctx.inc("vv")])
    y.wait("vv", ctx.val("vv"))
    y.op("dma_start", out=gscr[:, t0:t0 + 512], in_=gs512[0:8, :],
         incs=[ctx.inc("cst", 16)])


def stage_l0_moe(ctx, st):
    T, NB = st["T"], st["NB"]
    t, v, s, g, y = ctx.t, ctx.v, ctx.s, ctx.g, ctx.y
    WB, AFA, CONST, CONST16, ps = st["WB"], st["AF"], st["CONST"], st["CONST16"], st["ps"]
    cp = st["cpos"]
    wscr = st["wscr"]
    id16 = CONST16[:, 0:128]

    WB.push(); AFA.push()
    xf = [AFA.alloc(512, f"m0x{c}") for c in range(8)]
    sq = [AFA.alloc(512, "m0sq0"), AFA.alloc(512, "m0sq1")]
    r3 = AFA.alloc(1024, "m0r3")
    st["_ln_rows"] = [r3[0:1, 0:512], r3[0:1, 512:1024]]
    gt = AFA.alloc(64, "m0gt")
    grow = AFA.alloc(16, "m0grow")
    st["_gt"], st["_grow"] = gt, grow
    cwb = AFA.alloc(512, "m0cwb")
    st["_gs512"] = cwb
    contrib = [AFA.alloc(512, "m0ct0"), AFA.alloc(512, "m0ct1")]
    cwrow2 = AFA.alloc(1024, "m0cwrow")

    xln16 = [WB.alloc(512, f"m0xln{c}") for c in range(8)]
    xln16b = [WB.alloc(512, f"m0xlnb{c}") for c in range(8)]
    xln2 = [xln16, xln16b]
    wgT = WB.alloc(64, "m0wgT")
    wgn = WB.alloc(1024, "m0wgn")
    _w1 = WB.alloc(11264, "m0w1T")
    _w3 = WB.alloc(11264, "m0w3T")
    _w2 = WB.alloc(11264, "m0w2T")
    w1T2 = [_w1, _w1]
    w3T2 = [_w3, _w3]
    w2T2 = [_w2, _w2]
    hact = WB.alloc(5632, "m0hact")
    hs16 = [WB.alloc(512, "m0hs0"), WB.alloc(512, "m0hs1")]

    st["_m0hv"] = {0: 0, 1: 0}
    st["_m0ov"] = {0: 0, 1: 0}
    st["_m0adma"] = {0: 0, 1: 0}

    l0a, l0b = st["l0_done"]
    g.wait("sl16", l0a)
    g.wait("sl17", l0b)
    g.op("dma_start", out=wgn[0:8, :], in_=wscr["moe0_wg"], incs=[ctx.inc("gld", 16)])
    t.wait("gld", ctx.val("gld"))
    for c in range(8):
        t.op("matmul", ps[7][:, c * 8:(c + 1) * 8],
             wgn[0:8, c * 128:(c + 1) * 128], id16[0:8, 0:8], start=True, stop=True)
    t.op("nop", incs=[ctx.inc("tt")])
    v.wait("tt", ctx.val("tt"))
    v.op("tensor_copy", wgT[:, :], ps[7][:, 0:64], incs=[ctx.inc("vv")])
    t.wait("vv", ctx.val("vv"))

    g.wait("vv", ctx.val("vv"))
    m0free = 0
    for b in range(NB):
        t0 = b * TBLK
        lds = f"sl{14 + (b % 2)}"
        if b >= 1:
            g.wait("vv", m0free)
        g.op("dma_start", out=_blk8(xf), in_=fm_ap(st["R1"], t0, TBLK),
             incs=[ctx.inc(lds, 16)])
        for q_ in (t, v, s):
            q_.wait(lds, ctx.val(lds))
        lnv = emit_ln(ctx, st, xf, xln16, 0, 1, 2, 3, sq)
        y.wait("vv", lnv)
        y.op("dma_start", out=fm_ap(st["XLN"], t0, TBLK), in_=_blk8x(xln16),
             incs=[ctx.inc("sst", 16)])
        t.wait("vv", lnv)
        emit_gates(ctx, st, xln16, st["GSC"], wgT, t0, renorm=True)
        v.op("nop", incs=[ctx.inc("vv")])
        m0free = ctx.val("vv")
    xlnv = ctx.val("sst")
    gatev = ctx.val("cst")

    g.wait("sst", xlnv)
    g.wait("cst", gatev)
    t.wait("sst", xlnv)
    acc_done = {tb: (0, 0) for tb in range(NB)}
    ACC = ("sl20", "sl21")
    QOFF = [0, 11]
    QSZ = [11, 11]
    vexp0 = [(e, q) for e in range(MOE0_E) for q in range(2)]
    NIT = len(vexp0) * NB
    m0wfree = [0]
    m0xcons = {0: 0, 1: 0}
    nld = [0]
    ldtgt = {}

    def _m0_prefetch():
        it = nld[0]
        par = it % 2
        t0p = (it % NB) * TBLK
        ep = vexp0[it // NB][0]
        lds = f"sl{14 + par}"
        g.wait("tt", m0xcons[par])
        g.op("dma_start", out=_blk8(xln2[par]), in_=fm_ap(st["XLN"], t0p, TBLK),
             incs=[ctx.inc(lds, 16)])
        g.op("dma_start", out=cwrow2[0:1, par * 512:par * 512 + 512],
             in_=st["GSC"][ep:ep + 1, t0p:t0p + 512], incs=[ctx.inc(lds, 16)])
        ldtgt[it] = ctx.val(lds)
        nld[0] += 1

    for vi, (e, qq) in enumerate(vexp0):
        h0 = QOFF[qq] * 128
        nhc = QSZ[qq]
        HQ = nhc * 128
        wld = f"sl{18 + (vi % 2)}"
        w1T, w3T, w2T = w1T2[vi % 2], w3T2[vi % 2], w2T2[vi % 2]
        y.wait("tt", m0wfree[0])
        for c in range(8):
            y.op("dma_start", out=w1T[:, c * HQ:(c + 1) * HQ],
                 in_=wscr["moe0_w1"][e][h0:h0 + HQ, c * 128:(c + 1) * 128],
                 transpose=True, incs=[ctx.inc(wld, 16)])
            y.op("dma_start", out=w3T[:, c * HQ:(c + 1) * HQ],
                 in_=wscr["moe0_w3"][e][h0:h0 + HQ, c * 128:(c + 1) * 128],
                 transpose=True, incs=[ctx.inc(wld, 16)])
        for hc in range(nhc):
            y.op("dma_start", out=w2T[:, hc * 1024:(hc + 1) * 1024],
                 in_=wscr["moe0_w2"][e][:, h0 + hc * 128:h0 + hc * 128 + 128],
                 transpose=True, incs=[ctx.inc(wld, 16)])
        t.wait(wld, ctx.val(wld))
        for tb in range(NB):
            t0 = tb * TBLK
            it = vi * NB + tb
            par = it % 2
            xln16c = xln2[par]
            cwrow = cwrow2[0:1, par * 512:par * 512 + 512]
            while nld[0] <= it + 1 and nld[0] < NIT:
                _m0_prefetch()
            lds = f"sl{14 + par}"
            t.wait(lds, ldtgt[it])
            v.wait(lds, ldtgt[it])
            for hc in range(nhc):
                pa, pb_ = ps[2 * (hc % 2)], ps[2 * (hc % 2) + 1]
                if hc >= 2:
                    t.wait("vv", st["_m0hv"][hc % 2])
                for c in range(8):
                    t.op("matmul", pa[:, :],
                         w1T[:, c * HQ + hc * 128:c * HQ + hc * 128 + 128],
                         xln16c[c], start=(c == 0), stop=(c == 7),
                         incs=([ctx.inc("tt")] if c == 7 else []))
                s.wait("tt", ctx.val("tt"))
                for c in range(8):
                    t.op("matmul", pb_[:, :],
                         w3T[:, c * HQ + hc * 128:c * HQ + hc * 128 + 128],
                         xln16c[c], start=(c == 0), stop=(c == 7),
                         incs=([ctx.inc("tt")] if c == 7 else []))
                if hc >= 2:
                    s.wait("vv", st["_m0hv"][hc % 2])
                s.op("activation", hs16[hc % 2][:, :], pa[:, :], AF.Silu,
                     incs=[ctx.inc("ss")])
                v.wait("tt", ctx.val("tt"))
                v.wait("ss", ctx.val("ss"))
                v.op("tensor_tensor", hact[:, hc * 512:(hc + 1) * 512],
                     hs16[hc % 2][:, :], pb_[:, :], ALU.mult, incs=[ctx.inc("vv")])
                st["_m0hv"][hc % 2] = ctx.val("vv")
            t.wait("vv", ctx.val("vv"))
            t.op("matmul", ps[6][:, :], CONST[0:1, cp["ones"]:cp["ones"] + 128],
                 cwrow[0:1, :], start=True, stop=True, incs=[ctx.inc("tt")])
            m0xcons[par] = ctx.val("tt")
            v.wait("tt", ctx.val("tt"))
            v.op("tensor_copy", cwb[:, :], ps[6][:, :], incs=[ctx.inc("vv")])
            cwv = ctx.val("vv")
            t.wait("vv", cwv)
            if vi > 0:
                g.wait(ACC[0], acc_done[tb][0])
                g.wait(ACC[1], acc_done[tb][1])
            for dc in range(8):
                ob = 4 + (dc % 2)
                if dc >= 2:
                    t.wait("vv", st["_m0ov"][dc % 2])
                for hc in range(nhc):
                    t.op("matmul", ps[ob][:, :],
                         w2T[:, hc * 1024 + dc * 128:hc * 1024 + dc * 128 + 128],
                         hact[:, hc * 512:(hc + 1) * 512],
                         start=(hc == 0), stop=(hc == nhc - 1),
                         incs=([ctx.inc("tt")] if hc == nhc - 1 else []))
                v.wait("tt", ctx.val("tt"))
                if dc >= 2:
                    v.wait(ACC[dc % 2], st["_m0adma"][dc % 2])
                v.op("tensor_tensor", contrib[dc % 2], cwb[:, :], ps[ob][:, :],
                     ALU.mult, incs=[ctx.inc("vv")])
                st["_m0ov"][dc % 2] = ctx.val("vv")
                g.wait("vv", ctx.val("vv"))
                g.op("dma_start", out=fm_ap(st["R1"], t0, TBLK, dc, 1),
                     in_=contrib[dc % 2].rearrange("p (c t) -> p c t", c=1),
                     accum_op=ALU.add, incs=[ctx.inc(ACC[dc % 2], 16)])
                st["_m0adma"][dc % 2] = ctx.val(ACC[dc % 2])
            acc_done[tb] = (ctx.val(ACC[0]), ctx.val(ACC[1]))
        t.op("nop", incs=[ctx.inc("tt")])
        m0wfree[0] = ctx.val("tt")
    st["moe0_done"] = (ctx.val(ACC[0]), ctx.val(ACC[1]))
    WB.pop(); AFA.pop()


def _blk8(tiles):
    t0 = tiles[0]
    n = t0.shape[1]
    par = t0.tensor
    off = t0.offset
    return bass.AP(par, off, [list(t0.ap[0]), [n, 8], [1, n]])


def _blk8x(tiles):
    return _blk8(tiles)


def stage_l1_attn(ctx, st):
    T, NB = st["T"], st["NB"]
    t, v, s, g, y = ctx.t, ctx.v, ctx.s, ctx.g, ctx.y
    WB, AFA, CONST, CONST16, ps = st["WB"], st["AF"], st["CONST"], st["CONST16"], st["ps"]
    cp = st["cpos"]
    io = st["io"]

    WB.push()
    winn = WB.alloc(8192, "winn")
    wgn = WB.alloc(8192, "wgn1")
    wdtn = WB.alloc(8192, "wdtn")
    woutn = WB.alloc(8192, "woutn")
    wbn = WB.alloc(128, "wbn")
    wcn = WB.alloc(128, "wcn")
    m0a, m0b = st["moe0_done"]
    g.wait("sl20", m0a)
    g.wait("sl21", m0b)
    for nme, dst in [("win", winn), ("wg", wgn), ("wdt", wdtn), ("wout", woutn)]:
        g.op("dma_start", out=dst.rearrange("p (c o) -> p c o", c=8),
             in_=io[nme].rearrange("(c p) o -> p c o", p=128), incs=[ctx.inc("gld", 16)])
    for nme, dst in [("wb", wbn), ("wc", wcn)]:
        g.op("dma_start", out=dst.rearrange("p (c o) -> p c o", c=8),
             in_=io[nme].rearrange("(c p) o -> p c o", p=128), incs=[ctx.inc("gld", 16)])
    wready = ctx.val("gld")
    t.wait("gld", wready)

    AFA.push(); WB.push()
    xf = [AFA.alloc(512, f"a1x{c}") for c in range(8)]
    sq = [AFA.alloc(512, "a1sq0"), AFA.alloc(512, "a1sq1")]
    r3 = AFA.alloc(1024, "a1r3")
    st["_ln_rows"] = [r3[0:1, 0:512], r3[0:1, 512:1024]]
    xinf = [AFA.alloc(512, "a1xi0"), AFA.alloc(512, "a1xi1")]
    xln16 = [WB.alloc(512, f"a1xln{c}") for c in range(8)]
    z16 = [WB.alloc(512, f"a1z{c}") for c in range(8)]
    st["_a1"] = {0: 0, 1: 0}
    st["_a1z"] = {0: 0, 1: 0}
    lnfree = 0
    zstore = 0
    for b in range(NB):
        t0 = b * TBLK
        lds = f"sl{14 + (b % 2)}"
        if b >= 1:
            g.wait("ss", lnfree)
        g.op("dma_start", out=_blk8(xf), in_=fm_ap(st["R1"], t0, TBLK),
             incs=[ctx.inc(lds, 16)])
        for q_ in (t, v, s):
            q_.wait(lds, ctx.val(lds))
        s.wait("sst", zstore)
        lnv = emit_ln(ctx, st, xf, xln16, 0, 1, 2, 3, sq)
        t.wait("vv", lnv)
        for oo in range(8):
            bk = 4 + (oo % 2)
            if oo >= 2:
                t.wait("ss", st["_a1"][oo % 2])
            for c in range(8):
                t.op("matmul", ps[bk][:, :], winn[:, c * 1024 + oo * 128:c * 1024 + oo * 128 + 128],
                     xln16[c], start=(c == 0), stop=(c == 7),
                     incs=([ctx.inc("tt")] if c == 7 else []))
            s.wait("tt", ctx.val("tt"))
            if oo >= 2:
                s.wait(f"sl{18 + (oo % 2)}", ctx.val(f"sl{18 + (oo % 2)}"))
            s.op("activation", xinf[oo % 2], ps[bk][:, :], AF.Identity,
                 bias=CONST[:, cp["b_in"] + oo:cp["b_in"] + oo + 1], incs=[ctx.inc("ss")])
            st["_a1"][oo % 2] = ctx.val("ss")
            y.wait("ss", ctx.val("ss"))
            y.op("dma_start", out=fm_ap(st["XIN"], t0, TBLK, oo, 1),
                 in_=xinf[oo % 2].rearrange("p (c t) -> p c t", c=1),
                 incs=[ctx.inc(f"sl{18 + (oo % 2)}", 16)])
        for oo in range(8):
            bk = 6 + (oo % 2)
            if oo >= 2:
                t.wait("ss", st["_a1z"][oo % 2])
            for c in range(8):
                t.op("matmul", ps[bk][:, :], wgn[:, c * 1024 + oo * 128:c * 1024 + oo * 128 + 128],
                     xln16[c], start=(c == 0), stop=(c == 7),
                     incs=([ctx.inc("tt")] if c == 7 else []))
            s.wait("tt", ctx.val("tt"))
            s.op("activation", z16[oo], ps[bk][:, :], AF.Silu,
                 bias=CONST[:, cp["bg"] + oo:cp["bg"] + oo + 1], incs=[ctx.inc("ss")])
            st["_a1z"][oo % 2] = ctx.val("ss")
        zv = ctx.val("ss")
        y.wait("ss", zv)
        y.op("dma_start", out=fm_ap(st["ZSC"], t0, TBLK), in_=_blk8(z16),
             incs=[ctx.inc("sst", 16)])
        zstore = ctx.val("sst")
        lnfree = ctx.val("ss")
    passa_st = (ctx.val("sl18"), ctx.val("sl19"), ctx.val("sst"))
    AFA.pop(); WB.pop()

    AFA.push(); WB.push()
    xinb = AFA.alloc(4112, "b1xin")
    xsf = [AFA.alloc(512, f"b1xs{c}") for c in range(8)]
    dtf = [xinb[:, c * 514:c * 514 + 512] for c in range(8)]
    halo = AFA.alloc(16, "b1halo")
    yc = AFA.alloc(512, "b1y")
    dA = AFA.alloc(1024, "b1dA")
    ub = AFA.alloc(1024, "b1u")
    hb = [AFA.alloc(512, "b1h0"), AFA.alloc(512, "b1h1")]
    ytmp = AFA.alloc(1024, "b1yt")
    hlast = AFA.alloc(128, "b1hl")
    bmS = ytmp[:, 0:512]
    sfp = yc
    xs16 = [WB.alloc(512, f"b1xs16{c}") for c in range(8)]
    dtx16 = [WB.alloc(512, f"b1dtx{c}") for c in range(8)]
    bmb16 = WB.alloc(8192, "b1bmb")
    cmb16 = WB.alloc(8192, "b1cmb")
    uall = WB.alloc(8192, "b1uall")

    g.op("memset", hlast[:, :], 0.0)
    g.wait("sl18", passa_st[0])
    g.wait("sl19", passa_st[1])
    g.wait("sst", passa_st[2])
    g.op("nop" if False else "nop", incs=[ctx.inc("pp")])
    for q_ in (t, v, s):
        q_.wait("pp", ctx.val("pp"))
    st["_b1"] = {0: 0, 1: 0}
    st["_b1x"] = 0
    st["_b1ucons"] = 0
    st["_b1bk"] = {0: 0, 1: 0}
    st["_b1row"] = {0: 0, 1: 0}
    for b in range(NB):
        t0 = b * TBLK
        lds = f"sl{14 + (b % 2)}"
        if b >= 1:
            g.wait("vv", st["_b1x"])
        g.op("dma_start", out=xinb.rearrange("p (c t) -> p c t", c=8)[:, :, 1:513],
             in_=fm_ap(st["XIN"], t0, TBLK), incs=[ctx.inc(lds, 16)])
        if b > 0:
            g.op("dma_start", out=halo[:, 0:8].rearrange("p (c t) -> p c t", c=8),
                 in_=fm_ap(st["XIN"], t0 - 1, 1), allow_slow_non_contiguous=True,
                 incs=[ctx.inc(lds, 16)])
        else:
            g.op("memset", halo[:, 0:8], 0.0, incs=[ctx.inc("pp")])
        if b < NB - 1:
            g.op("dma_start", out=halo[:, 8:16].rearrange("p (c t) -> p c t", c=8),
                 in_=fm_ap(st["XIN"], t0 + TBLK, 1), allow_slow_non_contiguous=True,
                 incs=[ctx.inc(lds, 16)])
        else:
            g.op("memset", halo[:, 8:16], 0.0, incs=[ctx.inc("pp")])
        for q_ in (t, v, s):
            q_.wait(lds, ctx.val(lds))
            q_.wait("pp", ctx.val("pp"))
        for c in range(8):
            v.op("tensor_copy", xinb[:, c * 514:c * 514 + 1], halo[:, c:c + 1])
            v.op("tensor_copy", xinb[:, c * 514 + 513:c * 514 + 514], halo[:, 8 + c:9 + c])
        v.op("drain")
        for c in range(8):
            o = c * 514
            v.op("tensor_scalar", xsf[c], xinb[:, o + 1:o + 513],
                 CONST[:, cp["convw"] + c * 3 + 1:cp["convw"] + c * 3 + 2], None, ALU.mult)
            v.op("drain")
            v.op("scalar_tensor_tensor", xsf[c], xinb[:, o:o + 512],
                 CONST[:, cp["convw"] + c * 3 + 0:cp["convw"] + c * 3 + 1], xsf[c],
                 ALU.mult, ALU.add)
            v.op("drain")
            v.op("scalar_tensor_tensor", xsf[c], xinb[:, o + 2:o + 514],
                 CONST[:, cp["convw"] + c * 3 + 2:cp["convw"] + c * 3 + 3], xsf[c],
                 ALU.mult, ALU.add, incs=[ctx.inc("vv")])
            s.wait("vv", ctx.val("vv"))
            s.op("activation", xs16[c], xsf[c], AF.Silu,
                 bias=CONST[:, cp["convb"] + c:cp["convb"] + c + 1])
            s.op("drain")
            s.op("activation", xsf[c], xsf[c], AF.Silu,
                 bias=CONST[:, cp["convb"] + c:cp["convb"] + c + 1], incs=[ctx.inc("ss")])
        convv = ctx.val("ss")
        t.wait("ss", convv)
        for oo in range(8):
            bk = 4 + (oo % 2)
            for c in range(8):
                t.op("matmul", ps[bk][:, :], wdtn[:, c * 1024 + oo * 128:c * 1024 + oo * 128 + 128],
                     xs16[c], start=(c == 0), stop=(c == 7),
                     incs=([ctx.inc("tt")] if c == 7 else []))
            s.wait("tt", ctx.val("tt"))
            s.op("activation", sfp[:, :], ps[bk][:, :], AF.Identity,
                 bias=CONST[:, cp["bdt"] + oo:cp["bdt"] + oo + 1])
            s.op("drain")
            s.op("activation", dtf[oo], sfp[:, :], AF.Abs)
            s.op("drain")
            s.op("activation", dtf[oo], dtf[oo], AF.Exp, scale=-1.0)
            s.op("drain")
            s.op("activation", dtf[oo], dtf[oo], AF.Ln, bias=1.0, incs=[ctx.inc("ss")])
            v.wait("ss", ctx.val("ss"))
            v.op("tensor_scalar", sfp[:, :], sfp[:, :], 0.0, None, ALU.max)
            v.op("drain")
            v.op("tensor_tensor", dtf[oo], dtf[oo], sfp[:, :], ALU.add)
            v.op("drain")
            v.op("tensor_tensor", dtx16[oo], dtf[oo], xsf[oo], ALU.mult,
                 incs=[ctx.inc("vv")])
            t.wait("vv", ctx.val("vv"))
            s.wait("vv", ctx.val("vv"))
        for im, (wn, bcol) in enumerate([(wbn, "bb"), (wcn, "bc")]):
            for c in range(8):
                t.op("matmul", ps[6][0:16, :], wn[:, c * 16:(c + 1) * 16], xs16[c],
                     start=(c == 0), stop=(c == 7),
                     incs=([ctx.inc("tt")] if c == 7 else []))
            s.wait("tt", ctx.val("tt"))
            s.op("activation", bmS[0:16, :], ps[6][0:16, :], AF.Identity,
                 bias=CONST[0:16, cp[bcol]:cp[bcol] + 1], incs=[ctx.inc("ss")])
            y.wait("ss", ctx.val("ss"))
            y.op("dma_start", out=st["BMR"][im, :, t0:t0 + 512], in_=bmS[0:16, :],
                 incs=[ctx.inc("cst", 16)])
            t.wait("cst", ctx.val("cst"))
        bmv = ctx.val("cst")
        g.wait("cst", bmv)
        ones_row = CONST[0:1, cp["ones"]:cp["ones"] + 128]
        for im in range(2):
            dstb = bmb16 if im == 0 else cmb16
            for n in range(SSM_N):
                rs = f"sl{22 + (n % 2)}"
                rowslot = (yc if n % 2 == 0 else dA)[0:1, 0:512]
                g.wait("vv", st["_b1row"][n % 2])
                g.op("dma_start", out=rowslot,
                     in_=st["BMR"][im, n:n + 1, t0:t0 + 512], incs=[ctx.inc(rs, 16)])
                t.wait(rs, ctx.val(rs))
                bk = 6 + (n % 2)
                if n >= 2 or im > 0:
                    t.wait("vv", st["_b1bk"][n % 2])
                t.op("matmul", ps[bk][:, :], ones_row, rowslot, start=True, stop=True,
                     incs=[ctx.inc("tt")])
                v.wait("tt", ctx.val("tt"))
                v.op("tensor_copy", dstb[:, n * 512:(n + 1) * 512], ps[bk][:, :],
                     incs=[ctx.inc("vv")])
                st["_b1bk"][n % 2] = ctx.val("vv")
                st["_b1row"][n % 2] = ctx.val("vv")
        v.op("nop", incs=[ctx.inc("vv")])
        bcv = ctx.val("vv")
        for q_ in (s, g):
            q_.wait("vv", bcv)
        scn = {0: 0, 1: 0}
        ymv = {0: 0, 1: 0}
        ysl = {0: 0, 1: 0}
        for c in range(8):
            ystore = ctx.val("sst")
            if c == 0:
                g.wait("vv", ctx.val("vv"))
            else:
                g.wait("vv", st["_b1ucons"])
            dtxb = bass.AP(dtx16[c].tensor, dtx16[c].offset,
                           [list(dtx16[c].ap[0]), [0, SSM_N], [1, 512]])
            g.op("tensor_tensor", uall.rearrange("p (n t) -> p n t", n=SSM_N),
                 dtxb, bmb16.rearrange("p (n t) -> p n t", n=SSM_N),
                 ALU.mult, incs=[ctx.inc("pp")])
            uv = ctx.val("pp")
            v.wait("pp", uv)
            for n in range(SSM_N):
                hcol = c * 16 + n
                hbuf = hb[n % 2]
                sl2 = n % 2
                dAb = dA[:, sl2 * 512:sl2 * 512 + 512]
                ytb = ytmp[:, sl2 * 512:sl2 * 512 + 512]
                if c + n > 0:
                    s.wait("vv", scn[sl2])
                s.op("activation", dAb, dtf[c], AF.Exp,
                     scale=CONST[:, cp["A"] + c * 16 + n:cp["A"] + c * 16 + n + 1],
                     incs=[ctx.inc("ss")])
                v.wait("ss", ctx.val("ss"))
                v.op("tensor_tensor_scan", hbuf[:, :], dAb,
                     uall[:, n * 512:(n + 1) * 512],
                     hlast[:, hcol:hcol + 1], ALU.mult, ALU.add)
                v.op("drain")
                v.op("tensor_copy", hlast[:, hcol:hcol + 1], hbuf[:, 511:512],
                     incs=[ctx.inc("vv")])
                scn[sl2] = ctx.val("vv")
                if n > 0:
                    v.wait("pp", ymv[1 - sl2])
                    if n == 1:
                        v.wait("sst", ystore)
                        v.op("tensor_copy", yc[:, :], ytmp[:, (1 - sl2) * 512:(1 - sl2) * 512 + 512],
                             incs=[ctx.inc("vv")])
                    else:
                        v.op("tensor_tensor", yc[:, :], yc[:, :],
                             ytmp[:, (1 - sl2) * 512:(1 - sl2) * 512 + 512], ALU.add,
                             incs=[ctx.inc("vv")])
                    ysl[1 - sl2] = ctx.val("vv")
                g.wait("vv", scn[sl2])
                if n > 1:
                    g.wait("vv", ysl[sl2])
                g.op("tensor_tensor", ytb, hbuf[:, :],
                     cmb16[:, n * 512:(n + 1) * 512], ALU.mult, incs=[ctx.inc("pp")])
                ymv[sl2] = ctx.val("pp")
            st["_b1ucons"] = ctx.val("vv")
            v.wait("pp", ymv[(SSM_N - 1) % 2])
            v.op("tensor_tensor", yc[:, :], yc[:, :],
                 ytmp[:, ((SSM_N - 1) % 2) * 512:((SSM_N - 1) % 2) * 512 + 512],
                 ALU.add, incs=[ctx.inc("vv")])
            ysl[(SSM_N - 1) % 2] = ctx.val("vv")
            y.wait("vv", ctx.val("vv"))
            y.op("dma_start", out=fm_ap(st["XSS"], t0, TBLK, c, 1),
                 in_=yc.rearrange("p (c t) -> p c t", c=1), incs=[ctx.inc("sst", 16)])
        v.op("nop", incs=[ctx.inc("vv")])
        st["_b1x"] = ctx.val("vv")
    passb_st = ctx.val("sst")
    AFA.pop(); WB.pop()

    AFA.push(); WB.push()
    xssf = [AFA.alloc(512, f"c1s{c}") for c in range(8)]
    r1f = [AFA.alloc(512, f"c1r{c}") for c in range(8)]
    of = [AFA.alloc(512, "c1o0"), AFA.alloc(512, "c1o1")]
    zm16 = [WB.alloc(512, f"c1z{c}") for c in range(8)]
    zl16 = [WB.alloc(512, f"c1zl{c}") for c in range(8)]
    c1f = 0
    g.wait("sst", passb_st)
    for b in range(NB):
        t0 = b * TBLK
        lds = f"sl{14 + (b % 2)}"
        if b >= 1:
            g.wait("vv", c1f)
        g.op("dma_start", out=_blk8(xssf), in_=fm_ap(st["XSS"], t0, TBLK),
             incs=[ctx.inc(lds, 16)])
        g.op("dma_start", out=_blk8(r1f), in_=fm_ap(st["R1"], t0, TBLK),
             incs=[ctx.inc(lds, 16)])
        g.op("dma_start", out=_blk8(zl16), in_=fm_ap(st["ZSC"], t0, TBLK),
             incs=[ctx.inc(lds, 16)])
        for q_ in (t, v):
            q_.wait(lds, ctx.val(lds))
        for c in range(8):
            v.op("tensor_tensor", zm16[c], xssf[c], zl16[c], ALU.mult,
                 incs=([ctx.inc("vv")] if c == 7 else []))
        zmv = ctx.val("vv")
        t.wait("vv", zmv)
        for oo in range(8):
            bk = 4 + (oo % 2)
            for c in range(8):
                t.op("matmul", ps[bk][:, :], woutn[:, c * 1024 + oo * 128:c * 1024 + oo * 128 + 128],
                     zm16[c], start=(c == 0), stop=(c == 7),
                     incs=([ctx.inc("tt")] if c == 7 else []))
            s.wait("tt", ctx.val("tt"))
            if oo >= 2:
                s.wait(f"sl{18 + (oo % 2)}", ctx.val(f"sl{18 + (oo % 2)}"))
            s.op("activation", of[oo % 2], ps[bk][:, :], AF.Identity,
                 bias=CONST[:, cp["bout"] + oo:cp["bout"] + oo + 1], incs=[ctx.inc("ss")])
            v.wait("ss", ctx.val("ss"))
            v.op("tensor_tensor", of[oo % 2], of[oo % 2], r1f[oo], ALU.add,
                 incs=[ctx.inc("vv")])
            t.wait("vv", ctx.val("vv"))
            y.wait("vv", ctx.val("vv"))
            y.op("dma_start", out=fm_ap(st["R0"], t0, TBLK, oo, 1),
                 in_=of[oo % 2].rearrange("p (c t) -> p c t", c=1),
                 incs=[ctx.inc(f"sl{18 + (oo % 2)}", 16)])
        v.op("nop", incs=[ctx.inc("vv")])
        c1f = ctx.val("vv")
    st["l1a_done"] = (ctx.val("sl18"), ctx.val("sl19"))
    AFA.pop(); WB.pop()
    WB.pop()


def stage_l1_ffn(ctx, st):
    T, NB = st["T"], st["NB"]
    t, v, s, g, y = ctx.t, ctx.v, ctx.s, ctx.g, ctx.y
    WB, AFA, CONST, CONST16, ps = st["WB"], st["AF"], st["CONST"], st["CONST16"], st["ps"]
    cp = st["cpos"]
    wscr = st["wscr"]
    id16 = CONST16[:, 0:128]

    WB.push(); AFA.push()
    xf = [AFA.alloc(512, f"f1x{c}") for c in range(8)]
    sq = [AFA.alloc(512, "f1sq0"), AFA.alloc(512, "f1sq1")]
    r3 = AFA.alloc(1024, "f1r3")
    st["_ln_rows"] = [r3[0:1, 0:512], r3[0:1, 512:1024]]
    gt = AFA.alloc(64, "f1gt")
    grow = AFA.alloc(16, "f1grow")
    st["_gt"], st["_grow"] = gt, grow
    cwb = AFA.alloc(512, "f1cwb")
    st["_gs512"] = cwb
    contrib = [AFA.alloc(512, "f1ct0"), AFA.alloc(512, "f1ct1")]
    cwrow2 = AFA.alloc(1024, "f1cwrow")

    xln16 = [WB.alloc(512, f"f1xln{c}") for c in range(8)]
    xln16b = [WB.alloc(512, f"f1xlnb{c}") for c in range(8)]
    xln2 = [xln16, xln16b]
    wgT = WB.alloc(64, "f1wgT")
    wgn = WB.alloc(1024, "f1wgn")
    w1T2 = [WB.alloc(8192, "f1w1Ta"), WB.alloc(8192, "f1w1Tb")]
    w3T2 = [WB.alloc(8192, "f1w3Ta"), WB.alloc(8192, "f1w3Tb")]
    w2T2 = [WB.alloc(8192, "f1w2Ta"), WB.alloc(8192, "f1w2Tb")]
    hact = WB.alloc(4096, "f1hact")
    hs16 = [WB.alloc(512, "f1hs0"), WB.alloc(512, "f1hs1")]

    st["_f1free"] = {0: 0, 1: 0}
    st["_f1wfree"] = {0: 0, 1: 0}
    st["_f1hv"] = {0: 0, 1: 0}
    st["_f1xcons"] = {0: 0, 1: 0}
    st["_f1ov"] = {0: 0, 1: 0}
    st["_f1adma"] = {0: 0, 1: 0}

    a1, a2 = st["l1a_done"]
    g.wait("sl18", a1)
    g.wait("sl19", a2)
    g.wait("vv", ctx.val("vv"))
    g.op("dma_start", out=wgn[0:8, :], in_=wscr["moe1_wg"], incs=[ctx.inc("gld", 16)])
    t.wait("gld", ctx.val("gld"))
    for c in range(8):
        t.op("matmul", ps[7][:, c * 8:(c + 1) * 8],
             wgn[0:8, c * 128:(c + 1) * 128], id16[0:8, 0:8], start=True, stop=True)
    t.op("nop", incs=[ctx.inc("tt")])
    v.wait("tt", ctx.val("tt"))
    v.op("tensor_copy", wgT[:, :], ps[7][:, 0:64], incs=[ctx.inc("vv")])
    t.wait("vv", ctx.val("vv"))

    f1free = 0
    for b in range(NB):
        t0 = b * TBLK
        lds = f"sl{14 + (b % 2)}"
        if b >= 1:
            g.wait("vv", f1free)
        g.op("dma_start", out=_blk8(xf), in_=fm_ap(st["R0"], t0, TBLK),
             incs=[ctx.inc(lds, 16)])
        for q_ in (t, v, s):
            q_.wait(lds, ctx.val(lds))
        lnv = emit_ln(ctx, st, xf, xln16, 0, 1, 2, 3, sq)
        y.wait("vv", lnv)
        y.op("dma_start", out=fm_ap(st["XLN"], t0, TBLK), in_=_blk8(xln16),
             incs=[ctx.inc("sst", 16)])
        t.wait("vv", lnv)
        emit_gates(ctx, st, xln16, st["GSC1"], wgT, t0, renorm=False, use_bias=True)
        v.op("nop", incs=[ctx.inc("vv")])
        f1free = ctx.val("vv")
    xlnv = ctx.val("sst")
    gatev = ctx.val("cst")

    g.wait("sst", xlnv)
    g.wait("cst", gatev)
    t.wait("sst", xlnv)
    acc_done = {tb: (0, 0) for tb in range(NB)}
    ACC = ("sl20", "sl21")
    vexp = [("sh", 0), ("sh", 1)] + [("e", e) for e in range(MOE1_E)]
    f1wfree = {0: 0, 1: 0}
    f1xcons = {0: 0, 1: 0}
    nld1 = [0]
    ldtgt1 = {}
    NIT1 = len(vexp) * NB

    def _f1_prefetch():
        it = nld1[0]
        par = it % 2
        t0p = (it % NB) * TBLK
        vip = it // NB
        kindp, ep = vexp[vip]
        lds = f"sl{14 + par}"
        g.wait("tt", f1xcons[par])
        g.op("dma_start", out=_blk8(xln2[par]), in_=fm_ap(st["XLN"], t0p, TBLK),
             incs=[ctx.inc(lds, 16)])
        if kindp == "e":
            g.op("dma_start", out=cwrow2[0:1, par * 512:par * 512 + 512],
                 in_=st["GSC1"][ep:ep + 1, t0p:t0p + 512], incs=[ctx.inc(lds, 16)])
        ldtgt1[it] = ctx.val(lds)
        nld1[0] += 1
    for vi, (kind, e) in enumerate(vexp):
        wld = f"sl{16 + (vi % 2)}"
        w1T, w3T, w2T = w1T2[vi % 2], w3T2[vi % 2], w2T2[vi % 2]
        y.wait("tt", f1wfree[vi % 2])
        if kind == "sh":
            h0 = e * 1024
            for c in range(8):
                y.op("dma_start", out=w1T[:, c * 1024:(c + 1) * 1024],
                     in_=wscr["sw1"][h0:h0 + 1024, c * 128:(c + 1) * 128],
                     transpose=True, incs=[ctx.inc(wld, 16)])
                y.op("dma_start", out=w3T[:, c * 1024:(c + 1) * 1024],
                     in_=wscr["sw3"][h0:h0 + 1024, c * 128:(c + 1) * 128],
                     transpose=True, incs=[ctx.inc(wld, 16)])
                y.op("dma_start", out=w2T[:, c * 1024:(c + 1) * 1024],
                     in_=wscr["sw2"][:, h0 + c * 128:h0 + c * 128 + 128],
                     transpose=True, incs=[ctx.inc(wld, 16)])
        else:
            for c in range(8):
                y.op("dma_start", out=w1T[:, c * 1024:(c + 1) * 1024],
                     in_=wscr["moe1_w1"][e][:, c * 128:(c + 1) * 128],
                     transpose=True, incs=[ctx.inc(wld, 16)])
                y.op("dma_start", out=w3T[:, c * 1024:(c + 1) * 1024],
                     in_=wscr["moe1_w3"][e][:, c * 128:(c + 1) * 128],
                     transpose=True, incs=[ctx.inc(wld, 16)])
                y.op("dma_start", out=w2T[:, c * 1024:(c + 1) * 1024],
                     in_=wscr["moe1_w2"][e][:, c * 128:(c + 1) * 128],
                     transpose=True, incs=[ctx.inc(wld, 16)])
        t.wait(wld, ctx.val(wld))
        for tb in range(NB):
            t0 = tb * TBLK
            it = vi * NB + tb
            par = it % 2
            xln16c = xln2[par]
            cwrow = cwrow2[0:1, par * 512:par * 512 + 512]
            while nld1[0] <= it + 1 and nld1[0] < NIT1:
                _f1_prefetch()
            lds = f"sl{14 + par}"
            t.wait(lds, ldtgt1[it])
            v.wait(lds, ldtgt1[it])
            for hc in range(8):
                pa, pb_ = ps[2 * (hc % 2)], ps[2 * (hc % 2) + 1]
                if hc >= 2:
                    t.wait("vv", st["_f1hv"][hc % 2])
                for c in range(8):
                    t.op("matmul", pa[:, :], w1T[:, c * 1024 + hc * 128:c * 1024 + hc * 128 + 128],
                         xln16c[c], start=(c == 0), stop=(c == 7),
                         incs=([ctx.inc("tt")] if c == 7 else []))
                s.wait("tt", ctx.val("tt"))
                for c in range(8):
                    t.op("matmul", pb_[:, :], w3T[:, c * 1024 + hc * 128:c * 1024 + hc * 128 + 128],
                         xln16c[c], start=(c == 0), stop=(c == 7),
                         incs=([ctx.inc("tt")] if c == 7 else []))
                if hc >= 2:
                    s.wait("vv", st["_f1hv"][hc % 2])
                s.op("activation", hs16[hc % 2][:, :], pa[:, :], AF.Silu,
                     incs=[ctx.inc("ss")])
                v.wait("tt", ctx.val("tt"))
                v.wait("ss", ctx.val("ss"))
                v.op("tensor_tensor", hact[:, hc * 512:(hc + 1) * 512],
                     hs16[hc % 2][:, :], pb_[:, :], ALU.mult, incs=[ctx.inc("vv")])
                st["_f1hv"][hc % 2] = ctx.val("vv")
            t.wait("vv", ctx.val("vv"))
            if kind == "e":
                t.op("matmul", ps[6][:, :], CONST[0:1, cp["ones"]:cp["ones"] + 128],
                     cwrow[0:1, :], start=True, stop=True, incs=[ctx.inc("tt")])
                v.wait("tt", ctx.val("tt"))
                v.op("tensor_copy", cwb[:, :], ps[6][:, :], incs=[ctx.inc("vv")])
                cwv = ctx.val("vv")
                t.wait("vv", cwv)
            f1xcons[par] = ctx.val("tt")
            if vi > 0:
                g.wait(ACC[0], acc_done[tb][0])
                g.wait(ACC[1], acc_done[tb][1])
            for dc in range(8):
                ob = 4 + (dc % 2)
                if dc >= 2:
                    t.wait("vv", st["_f1ov"][dc % 2])
                for hc in range(8):
                    t.op("matmul", ps[ob][:, :],
                         w2T[:, hc * 1024 + dc * 128:hc * 1024 + dc * 128 + 128],
                         hact[:, hc * 512:(hc + 1) * 512],
                         start=(hc == 0), stop=(hc == 7),
                         incs=([ctx.inc("tt")] if hc == 7 else []))
                v.wait("tt", ctx.val("tt"))
                if dc >= 2:
                    v.wait(ACC[dc % 2], st["_f1adma"][dc % 2])
                if kind == "e":
                    v.op("tensor_tensor", contrib[dc % 2], cwb[:, :], ps[ob][:, :],
                         ALU.mult, incs=[ctx.inc("vv")])
                else:
                    v.op("tensor_copy", contrib[dc % 2], ps[ob][:, :],
                         incs=[ctx.inc("vv")])
                st["_f1ov"][dc % 2] = ctx.val("vv")
                g.wait("vv", ctx.val("vv"))
                g.op("dma_start", out=fm_ap(st["R0"], t0, TBLK, dc, 1),
                     in_=contrib[dc % 2].rearrange("p (c t) -> p c t", c=1),
                     accum_op=ALU.add, incs=[ctx.inc(ACC[dc % 2], 16)])
                st["_f1adma"][dc % 2] = ctx.val(ACC[dc % 2])
            acc_done[tb] = (ctx.val(ACC[0]), ctx.val(ACC[1]))
        t.op("nop", incs=[ctx.inc("tt")])
        f1wfree[vi % 2] = ctx.val("tt")
    st["moe1_done"] = (ctx.val(ACC[0]), ctx.val(ACC[1]))
    WB.pop(); AFA.pop()


def stage_final_out(ctx, st):
    T, NT = st["T"], st["NT"]
    t, v, g, y = ctx.t, ctx.v, ctx.g, ctx.y
    AFA, CONST, ps = st["AF"], st["CONST"], st["ps"]
    id32 = CONST[:, 0:128]
    AFA.push()
    rin = [AFA.alloc(1024, "fo_in0"), AFA.alloc(1024, "fo_in1")]
    rtm = [AFA.alloc(1024, "fo_tm0"), AFA.alloc(1024, "fo_tm1")]
    ev_t = {0: 0, 1: 0}
    stg_t = {0: 0, 1: 0}
    m1a, m1b = st["moe1_done"]
    g.wait("sl20", m1a)
    g.wait("sl21", m1b)
    for i in range(NT):
        b = i % 2
        lds, sts = f"sl{8 + b}", f"sl{10 + b}"
        if i >= 2:
            g.wait("tt", stg_t[b])
        g.op("dma_start", out=rin[b].rearrange("p (c t) -> p c t", c=8),
             in_=fm_ap(st["R0"], i * 128, 128), incs=[ctx.inc(lds, 16)])
        t.wait(lds, ctx.val(lds))
        if i >= 2:
            t.wait("vv", ev_t[b])
        for c in range(8):
            bank = ps[c // 4 + 2 * b + 4]
            t.op("matmul", bank[:, (c % 4) * 128:(c % 4) * 128 + 128],
                 rin[b].rearrange("p (c n) -> p c n", c=8)[:, c, :], id32,
                 start=True, stop=True)
        t.op("nop", incs=[ctx.inc("tt")])
        stg_t[b] = ctx.val("tt")
        v.wait("tt", stg_t[b])
        if i >= 2:
            v.wait(sts, ctx.val(sts))
        v.op("tensor_copy", rtm[b][:, 0:512], ps[4 + 2 * b][:, :])
        v.op("tensor_copy", rtm[b][:, 512:1024], ps[5 + 2 * b][:, :], incs=[ctx.inc("vv")])
        ev_t[b] = ctx.val("vv")
        y.wait("vv", ev_t[b])
        y.op("dma_start", out=st["y"][i * 128:(i + 1) * 128, :], in_=rtm[b],
             incs=[ctx.inc(sts, 16)])
    AFA.pop()


def _mk_nc_full(T):
    return build_program(T, [stage_consts, stage_repack, stage_xt, stage_l0_attn,
                             stage_l0_moe, stage_l1_attn, stage_l1_ffn,
                             stage_final_out])


def stage_out_copy(ctx, st):
    nc, T, NT = ctx.nc, st["T"], st["NT"]
    g, t, v, y = ctx.g, ctx.t, ctx.v, ctx.y
    AFA, CONST, ps = st["AF"], st["CONST"], st["ps"]
    id32 = CONST[:, 0:128]
    AFA.push()
    rin = [AFA.alloc(1024, "oc_in0"), AFA.alloc(1024, "oc_in1")]
    rtm = [AFA.alloc(1024, "oc_tm0"), AFA.alloc(1024, "oc_tm1")]
    ev_t = {0: 0, 1: 0}
    stg_t = {0: 0, 1: 0}
    g.wait(f"sl6", st["xt_done"][0])
    g.wait(f"sl7", st["xt_done"][1])
    for i in range(NT):
        b = i % 2
        lds, sts = f"sl{8 + b}", f"sl{10 + b}"
        if i >= 2:
            g.wait("tt", stg_t[b])
        g.op("dma_start", out=rin[b].rearrange("p (c t) -> p c t", c=8),
             in_=fm_ap(st["R0"], i * 128, 128), incs=[ctx.inc(lds, 16)])
        t.wait(lds, ctx.val(lds))
        if i >= 2:
            t.wait("vv", ev_t[b])
        for c in range(8):
            bank = ps[c // 4 + 2 * b + 4]
            t.op("matmul", bank[:, (c % 4) * 128:(c % 4) * 128 + 128],
                 rin[b].rearrange("p (c n) -> p c n", c=8)[:, c, :], id32,
                 start=True, stop=True)
        t.op("nop", incs=[ctx.inc("tt")])
        stg_t[b] = ctx.val("tt")
        v.wait("tt", stg_t[b])
        if i >= 2:
            v.wait(sts, ctx.val(sts))
        v.op("tensor_copy", rtm[b][:, 0:512], ps[4 + 2 * b][:, :])
        v.op("tensor_copy", rtm[b][:, 512:1024], ps[5 + 2 * b][:, :], incs=[ctx.inc("vv")])
        ev_t[b] = ctx.val("vv")
        y.wait("vv", ev_t[b])
        y.op("dma_start", out=st["y"][i * 128:(i + 1) * 128, :], in_=rtm[b],
             incs=[ctx.inc(sts, 16)])
    AFA.pop()


def kernel(**inputs):
    T = 8192
    cst = host_constants()
    key = {
        "x": None,
        "q_w": "L0_attn_q_w", "k_w": "L0_attn_k_w", "v_w": "L0_attn_v_w",
        "o_w": "L0_attn_o_w", "cs_w1": "L0_attn_cs_w1", "cs_w2": "L0_attn_cs_w2",
        "moe0_w1": "L0_ffn_W1", "moe0_w2": "L0_ffn_W2", "moe0_w3": "L0_ffn_W3",
        "moe0_wg": "L0_ffn_Wg",
        "win": "L1_attn_Win", "wg": "L1_attn_Wg", "wdt": "L1_attn_Wdt",
        "wout": "L1_attn_Wout", "wb": "L1_attn_Wb", "wc": "L1_attn_Wc",
        "moe1_w1": "L1_ffn_w1", "moe1_w2": "L1_ffn_w2", "moe1_w3": "L1_ffn_w3",
        "moe1_wg": "L1_ffn_gate_w",
        "sw1": "L1_ffn_sw1", "sw3": "L1_ffn_sw3", "sw2": "L1_ffn_sw2",
        "cs_b1": "L0_attn_cs_b1", "cs_b2": "L0_attn_cs_b2",
        "decay": "L0_attn_decay_params",
        "b_in": "L1_attn_b_in", "bg": "L1_attn_bg", "bout": "L1_attn_bout",
        "bdt": "L1_attn_bdt", "convw": "L1_attn_convw", "convb": "L1_attn_convb",
        "A": "L1_attn_A", "bb": "L1_attn_bb", "bc": "L1_attn_bc",
        "gate_bias": "L1_ffn_gate_bias",
    }
    x = np.ascontiguousarray(inputs["L0_attn_x"])
    base = {}
    for k, src in key.items():
        if src is not None:
            base[k] = np.ascontiguousarray(inputs[src], dtype=np.float32)
    base.update(cst)
    nc = _mk_nc_full(T)
    in_maps = []
    for b in range(8):
        m = dict(base)
        m["x"] = np.ascontiguousarray(x[b])
        in_maps.append(m)
    res = run_bass_kernel_spmd(nc, in_maps, core_ids=list(range(8)))
    out = np.stack([r["y"] for r in res.results], axis=0)
    return out
